# version 01
# baseline (speedup 1.0000x reference)
import os
import dataclasses
import numpy as np
import concourse.bass as bass
import concourse.mybir as mybir
from concourse.bass_utils import run_bass_kernel_spmd
from contextlib import ExitStack

F32 = mybir.dt.float32
BF16 = mybir.dt.bfloat16
I32 = mybir.dt.int32
AF = mybir.ActivationFunctionType
ALU = mybir.AluOpType
AXX = mybir.AxisListType.X

P = 128
H = 2048
NHC = 16
TC = 1024
E = 1024
NEC = 8
NEXP = 6
HEADS = 32

PKEY = ("p", "pe")


class Em:

    def __init__(self, nc, ctx):
        self.nc = nc
        self.eng = {"pe": nc.tensor, "act": nc.scalar, "dve": nc.vector,
                    "sp": nc.sync, "pool": nc.gpsimd}
        self.sem = {}
        for e in ("pe", "act", "dve", "pool"):
            self.sem[("p", e)] = ctx.enter_context(nc.semaphore("prog_" + e))
        for q in ("sp", "act", "pool"):
            self.sem[("d", q)] = ctx.enter_context(nc.semaphore("dma_" + q))
        self.cnt = {k: 0 for k in self.sem}
        self.hw = {}
        self.writer = {}
        self.readers = {}

    def _wait(self, eng, tok):
        if tok is None:
            return
        sk, v = tok
        if sk == PKEY and eng == "pe":
            return
        if v <= self.hw.get((eng, sk), 0):
            return
        self.eng[eng].wait_ge(self.sem[sk], v)
        self.hw[(eng, sk)] = v

    def _deps(self, eng, reads, writes):
        for k in reads:
            self._wait(eng, self.writer.get(k))
        for k in writes:
            self._wait(eng, self.writer.get(k))
            for sk, v in self.readers.get(k, {}).items():
                self._wait(eng, (sk, v))

    def _record(self, tok, reads, writes):
        for k in writes:
            self.writer[k] = tok
            self.readers[k] = {}
        if tok is not None:
            sk, v = tok
            for k in reads:
                r = self.readers.setdefault(k, {})
                r[sk] = max(r.get(sk, 0), v)

    def op(self, eng, fn, reads=(), writes=()):
        self._deps(eng, reads, writes)
        inst = fn(self.eng[eng])
        sk = ("p", eng)
        self.cnt[sk] += 1
        inst.then_inc(self.sem[sk], 1)
        tok = (sk, self.cnt[sk])
        self._record(tok, reads, writes)
        return tok

    def dma(self, q, out, in_, reads=(), writes=(), slow=False):
        self._deps(q, reads, writes)
        inst = self.eng[q].dma_start(out=out, in_=in_,
                                     allow_slow_non_contiguous=slow)
        sk = ("d", q)
        self.cnt[sk] += 16
        inst.then_inc(self.sem[sk], 16)
        tok = (sk, self.cnt[sk])
        self._record(tok, reads, writes)
        return tok

    def rankdma(self, g, r_prank, mk0, mk1, reads=(), writes=()):
        self._deps("pool", reads, writes)
        sk = ("d", "pool")
        with g.If_eq(r_prank, 0):
            o0, i0 = mk0()
            g.dma_start(out=o0, in_=i0).then_inc(self.sem[sk], 16)
        with g.Else():
            o1, i1 = mk1()
            g.dma_start(out=o1, in_=i1).then_inc(self.sem[sk], 16)
        self.cnt[sk] += 16
        tok = (sk, self.cnt[sk])
        self._record(tok, reads, writes)
        return tok

    def mm(self, pap, pkeys, kitems, start=True, stop=True):
        W = pap.shape[-1]
        halves = [(0, W)] if W <= 512 else [(a, a + 512) for a in range(0, W, 512)]
        all_reads = set()
        n = len(kitems)
        for j, (la, ra, reads) in enumerate(kitems):
            first, last = j == 0, j == n - 1
            self._deps("pe", reads, pkeys if (first and start) else ())
            all_reads |= set(reads)
            for hi, (a, b) in enumerate(halves):
                rv = ra if len(halves) == 1 else ra[:, a:b]
                pv = pap if len(halves) == 1 else pap[:, a:b]
                inst = self.eng["pe"].matmul(
                    pv, la, rv, start=(first and start),
                    stop=(last and stop))
                if last and hi == len(halves) - 1:
                    self.cnt[PKEY] += 1
                    inst.then_inc(self.sem[PKEY], 1)
                    tok = (PKEY, self.cnt[PKEY])
        if start:
            self._record(tok, sorted(all_reads), pkeys)
        else:
            for k in pkeys:
                self.writer[k] = tok
            self._record(tok, sorted(all_reads), ())
        return tok

    def fence(self):
        for eng in ("pe", "act", "dve", "sp", "pool"):
            for sk, c in self.cnt.items():
                if c == 0:
                    continue
                if sk == ("p", eng):
                    continue
                self._wait(eng, (sk, c))


class Bump:
    def __init__(self, pool, base, limit):
        self.pool, self.base, self.limit, self.off = pool, base, limit, base

    def take(self, n, f32=False):
        elems = 2 * n if f32 else n
        pad = (elems + 63) & ~63
        a = self.off
        assert a + pad <= self.limit, f"scratch overflow {a}+{pad}>{self.limit}"
        self.off = a + pad
        v = self.pool[:, a:a + elems]
        return v.bitcast(F32) if f32 else v


def build(stage="full"):
    nc = bass.Bass("TRN2", target_bir_lowering=False)

    din = {}
    for name, shape, dt in [
        ("xT", [H, TC], F32),
        ("wqT", [H, H], F32), ("wkT", [H, H], F32),
        ("wvT", [H, H], F32), ("woT", [H, H], F32),
        ("wg", [P, 256], F32),
        ("wup", [NEXP, H, E], F32), ("wsw", [NEXP, E, E], F32),
        ("wdn", [NEXP, E, H], F32),
        ("wsu", [H, 2 * E], F32), ("wsa", [E, E], F32), ("wsd", [2 * E, H], F32),
        ("bupr", [P, 48], F32), ("bswr", [P, 48], F32), ("bsur", [P, 16], F32),
        ("bsar", [P, 8], F32), ("bsdr", [P, 16], F32), ("bgr", [8, 2], F32),
        ("bdnr", [NEXP, H], F32),
        ("relrev", [1024, 64], F32),
        ("consts", [P, 257], F32),
        ("ind6h", [8, NEXP * P], F32),
        ("onesf", [P, P], F32),
        ("noncein", [1, 16], I32),
    ]:
        din[name] = nc.dram_tensor(name, shape, dt, kind="ExternalInput")

    outd = nc.dram_tensor("out", [H, TC], F32, kind="ExternalOutput")

    x1d = nc.dram_tensor("x1d", [H, TC], F32, kind="Internal")
    x2d = nc.dram_tensor("x2d", [H, TC], F32, kind="Internal")
    hd = nc.dram_tensor("hd", [NEXP, E, TC], BF16, kind="Internal")
    Pd = nc.dram_tensor("Pd", [4096], F32, kind="Internal")
    mbrd = nc.dram_tensor("mbrd", [1024], F32, kind="Internal")
    BTd = nc.dram_tensor("BTd", [2 * 2 * TC, TC], BF16, kind="Internal")
    ktsh = nc.dram_tensor("ktsh", [H, 2 * TC], BF16, kind="Internal",
                          addr_space="Shared")
    vsh = nc.dram_tensor("vsh", [2 * TC, HEADS * 65], BF16, kind="Internal",
                         addr_space="Shared")
    flg = nc.dram_tensor("flg", [4, 16], I32, kind="Internal",
                         addr_space="Shared")

    with ExitStack() as ctx:
        em = Em(nc, ctx)
        spin_sem = ctx.enter_context(nc.semaphore("spin"))

        POOL_E = 92 * 1024
        pool = ctx.enter_context(nc.sbuf_tensor("pool", [P, POOL_E], BF16))
        A1, A2, A3, A4, SC = 0, 16384, 32768, 49152, 65536
        SCE = POOL_E

        bcb = ctx.enter_context(nc.sbuf_tensor("bcb", [P, 2048], F32))
        consb = ctx.enter_context(nc.sbuf_tensor("consb", [P, 257], BF16))
        onesf = ctx.enter_context(nc.sbuf_tensor("onesfs", [P, P], F32))
        biasf = ctx.enter_context(nc.sbuf_tensor("biasf", [P, 136], F32))
        bg_sb = ctx.enter_context(nc.sbuf_tensor("bg_sb", [8, 2], F32))
        bdn_sb = ctx.enter_context(nc.sbuf_tensor("bdn_sb", [NEXP, H], BF16))
        wg_sb = ctx.enter_context(nc.sbuf_tensor("wg_sb", [P, 256], BF16))
        ind6 = ctx.enter_context(nc.sbuf_tensor("ind6", [8, NEXP * P], BF16))
        gsT = bcb[:, 0:512].bitcast(BF16)[0:8, :]
        wT_sb = bcb[:, 512:1024].bitcast(BF16)[0:8, :]
        nonce_sb = ctx.enter_context(nc.sbuf_tensor("nonce_sb", [1, 16], I32))
        epsc = ctx.enter_context(nc.sbuf_tensor("epsc", [1, 8], F32))
        pf_sb = ctx.enter_context(nc.sbuf_tensor("pf_sb", [1, 32], I32))
        pf2_sb = ctx.enter_context(nc.sbuf_tensor("pf2_sb", [1, 32], I32))

        ps = [ctx.enter_context(nc.psum_tensor(f"ps{i}", [P, 1024], F32))
              for i in range(4)]

        block = ctx.enter_context(nc.Block())

        def pk(i):
            return (f"ps{i}a", f"ps{i}b")

        ID = consb[:, 0:128]
        JM = consb[:, 128:256]
        ONE1 = consb[:, 256:257]

        def bf3(base, nch, w):
            return pool[:, base:base + nch * w].rearrange(
                "p (c w) -> p c w", w=w)

        xln = bf3(A1, NHC, 1024)
        KLN = "xlnA1"

        em.dma("pool", consb[:], din["consts"].ap(), writes=["consb"])
        em.dma("sp", onesf[:], din["onesf"].ap(), writes=["onesf"])
        em.dma("sp", biasf[:, 0:48], din["bupr"].ap(), writes=["biasf"])
        em.dma("sp", biasf[:, 48:96], din["bswr"].ap(), writes=["biasf"])
        em.dma("sp", biasf[:, 96:112], din["bsur"].ap(), writes=["biasf"])
        em.dma("sp", biasf[:, 112:120], din["bsar"].ap(), writes=["biasf"])
        em.dma("sp", biasf[:, 120:136], din["bsdr"].ap(), writes=["biasf"])
        em.dma("sp", bg_sb[:], din["bgr"].ap(), writes=["bg"])
        em.dma("pool", bdn_sb[:], din["bdnr"].ap(), writes=["bdn"])
        em.dma("pool", wg_sb[:], din["wg"].ap(), writes=["wg"])
        em.dma("pool", ind6[:], din["ind6h"].ap(), writes=["ind6"])
        em.dma("sp", nonce_sb[:], din["noncein"].ap(), writes=["nonce"])
        em.op("dve", lambda v: v.memset(epsc[:], 1e-5), writes=["epsc"])

        g = em.eng["pool"]
        pid = g.partition_id()
        r_prank = ctx.enter_context(g.register("r_prank"))
        g.reg_alu(r_prank, pid, 1, ALU.bitwise_and)


        scp = Bump(pool, SC, SCE)
        wwin = [scp.take(1024), scp.take(1024)]
        btst = [scp.take(1024), scp.take(1024)]
        relsb = scp.take(512, f32=True)
        mbrf = scp.take(8, f32=True)
        em.dma("sp", relsb.rearrange("p (c d) -> p c d", d=64),
               din["relrev"].ap().rearrange("(c p) d -> p c d", p=P),
               writes=["relsb"])
        for c in range(8):
            em.op("dve", lambda v, c=c: v.reduce_sum(
                out=mbrf[:, c:c + 1], in_=relsb[:, c * 64:(c + 1) * 64],
                axis=AXX), reads=["relsb"], writes=["mbr"])
        em.op("dve", lambda v: v.tensor_scalar(
            out=mbrf[:], in0=mbrf[:], scalar1=1.0 / 64, scalar2=None,
            op0=ALU.mult), reads=["mbr"], writes=["mbr"])
        mbr_dst = dataclasses.replace(mbrd.ap(), ap=[[1, P], [P, 8]], offset=0)
        em.dma("sp", mbr_dst, mbrf[:], reads=["mbr"], writes=["mbrd"], slow=True)
        em.dma("sp", dataclasses.replace(Pd.ap(), ap=[[1, 1023]], offset=1536),
               dataclasses.replace(mbrd.ap(), ap=[[1, 1023]], offset=0),
               reads=["mbrd"], writes=["Pd"])
        em.dma("sp", dataclasses.replace(Pd.ap(), ap=[[1, 1536], [1, 1]], offset=0),
               dataclasses.replace(mbrd.ap(), ap=[[0, 1536], [1, 1]], offset=0),
               reads=["mbrd"], writes=["Pd"])
        em.dma("sp",
               dataclasses.replace(Pd.ap(), ap=[[1, 4096 - 2559], [1, 1]], offset=2559),
               dataclasses.replace(mbrd.ap(), ap=[[0, 4096 - 2559], [1, 1]], offset=1022),
               reads=["mbrd"], writes=["Pd"])

        for sel in range(2):
            for kc in range(16):
                par = kc % 2
                src = dataclasses.replace(
                    Pd.ap(), ap=[[1, P], [1, 1024]],
                    offset=1920 + sel * TC - 128 * kc)
                em.dma("pool", wwin[par][:], src, reads=["Pd"],
                       writes=[f"wwin{par}"])
                em.mm(ps[3], pk(3), [(JM, wwin[par][:], [f"wwin{par}", "consb"])])
                em.op("act", lambda s, par=par: s.copy(btst[par][:], ps[3][:]),
                      reads=pk(3), writes=[f"btst{par}"])
                em.dma("pool",
                       BTd.ap()[(sel * 16 + kc) * P:(sel * 16 + kc + 1) * P, :],
                       btst[par][:], reads=[f"btst{par}"], writes=["BTd"])

        def emit_ln(src_dram, src_key):
            em.fence()
            xh = pool[:, A2:A2 + 2 * NHC * 1024].bitcast(F32).rearrange(
                "p (c w) -> p c w", w=1024)
            em.dma("sp", xh[:, :, :],
                   src_dram.ap().rearrange("(c p) t -> p c t", p=P),
                   reads=[src_key], writes=["xh"])
            s2 = Bump(pool, SC, SCE)
            xb = [s2.take(1024), s2.take(1024)]
            xq = [s2.take(1024), s2.take(1024)]
            lnr_mean = s2.take(1024, f32=True)
            lnr_var = s2.take(1024, f32=True)
            lnr_tmp = s2.take(1024, f32=True)
            lnr_rstd = s2.take(1024, f32=True)
            lnr_mrs = s2.take(1024, f32=True)
            for hc in range(NHC):
                par = hc % 2
                em.op("act", lambda s, hc=hc, par=par: s.copy(
                    xb[par][:], xh[:, hc, :]), reads=["xh"], writes=[f"xb{par}"])
                em.op("act", lambda s, hc=hc, par=par: s.activation(
                    xq[par][:], xh[:, hc, :], AF.Square),
                    reads=["xh"], writes=[f"xq{par}"])
                em.mm(ps[2][0:1, :], pk(2),
                      [(ONE1, xb[par][:], [f"xb{par}", "consb"])],
                      start=(hc == 0), stop=(hc == NHC - 1))
                em.mm(ps[3][0:1, :], pk(3),
                      [(ONE1, xq[par][:], [f"xq{par}", "consb"])],
                      start=(hc == 0), stop=(hc == NHC - 1))
            em.op("act", lambda s: s.mul(lnr_mean[0:1, :], ps[2][0:1, :],
                                          1.0 / H),
                  reads=pk(2), writes=["lnmean"])
            em.op("act", lambda s: s.mul(lnr_var[0:1, :], ps[3][0:1, :],
                                         1.0 / H),
                  reads=pk(3), writes=["lnvar"])
            em.op("act", lambda s: s.square(lnr_tmp[0:1, :], lnr_mean[0:1, :]),
                  reads=["lnmean"], writes=["lntmp"])
            em.op("dve", lambda v: v.tensor_tensor(
                out=lnr_var[0:1, :], in0=lnr_var[0:1, :],
                in1=lnr_tmp[0:1, :], op=ALU.subtract),
                reads=["lnvar", "lntmp"], writes=["lnvar"])
            em.op("act", lambda s: s.activation(
                lnr_tmp[0:1, :], lnr_var[0:1, :], AF.Ln,
                bias=epsc[0:1, 0:1]),
                reads=["lnvar", "epsc"], writes=["lntmp"])
            em.op("act", lambda s: s.activation(
                lnr_rstd[0:1, :], lnr_tmp[0:1, :], AF.Exp, scale=-0.5),
                reads=["lntmp"], writes=["rstd"])
            em.op("dve", lambda v: v.tensor_tensor(
                out=lnr_mrs[0:1, :], in0=lnr_mean[0:1, :],
                in1=lnr_rstd[0:1, :], op=ALU.mult),
                reads=["lnmean", "rstd"], writes=["mrs"])
            em.mm(ps[2], pk(2), [(onesf[0:1, :], lnr_rstd[0:1, :], ["rstd", "onesf"])])
            em.op("act", lambda s: s.copy(bcb[:, 0:1024], ps[2][:]),
                  reads=pk(2), writes=["bcb0"])
            em.mm(ps[3], pk(3), [(onesf[0:1, :], lnr_mrs[0:1, :], ["mrs", "onesf"])])
            em.op("act", lambda s: s.copy(bcb[:, 1024:2048], ps[3][:]),
                  reads=pk(3), writes=["bcb1"])
            t32 = [s2.take(1024, f32=True), s2.take(1024, f32=True)]
            for hc in range(NHC):
                par = hc % 2
                em.op("dve", lambda v, hc=hc, par=par: v.tensor_tensor(
                    out=t32[par][:], in0=xh[:, hc, :], in1=bcb[:, 0:1024],
                    op=ALU.mult), reads=["xh", "bcb0"], writes=[f"t32{par}"])
                em.op("dve", lambda v, hc=hc, par=par: v.tensor_tensor(
                    out=xln[:, hc, :], in0=t32[par][:], in1=bcb[:, 1024:2048],
                    op=ALU.subtract), reads=[f"t32{par}", "bcb1"], writes=[KLN])

        class WS:
            def __init__(self, name, scb, nch, w, nbuf=4):
                self.name, self.nbuf = name, nbuf
                self.bufs = [scb.take(nch * w).rearrange("p (c w) -> p c w", w=w)
                             for _ in range(nbuf)]

            def key(self, i):
                return f"{self.name}{i % self.nbuf}"

            def load(self, i, src_ap):
                em.dma("pool", self.bufs[i % self.nbuf][:, :, :], src_ap,
                       writes=[self.key(i)])

            def get(self, i):
                return self.bufs[i % self.nbuf], self.key(i)

        def colslice(t_ap, c0, c1, p=P):
            return t_ap[:, c0:c1].rearrange("(c p) o -> p c o", p=p)

        def emit_ffn(j, xt3, xt_key, out_mode, base_dram, base_key,
                     out_dram, out_key, also_out=None):
            em.fence()
            sc = Bump(pool, SC, SCE)
            gtm = sc.take(64, f32=True)
            gsc = sc.take(40, f32=True)
            wtmb = sc.take(64)
            em.mm(ps[2][0:8, :], pk(2),
                  [(wg_sb[:, j * 128 + kc * 8: j * 128 + kc * 8 + 8],
                    xt3[:, kc, :], ["wg", xt_key]) for kc in range(NHC)])
            em.op("act", lambda s: s.activation(
                gsT[:], ps[2][0:8, :], AF.Identity, bias=bg_sb[:, j:j + 1]),
                reads=pk(2) + ("bg",), writes=["gsT"])
            for tb in range(8):
                bank = 0 if tb < 4 else 1
                em.mm(ps[3][:, tb * 128: tb * 128 + 8], (pk(3)[bank],),
                      [(gsT[:, tb * 128:(tb + 1) * 128], ID[0:8, 0:8],
                        ["gsT", "consb"])],
                      start=(tb % 4 == 0), stop=(tb % 4 == 3))
            for tb in range(8):
                em.op("act", lambda s, tb=tb: s.copy(
                    gtm[:, tb * 8:(tb + 1) * 8],
                    ps[3][:, tb * 128:tb * 128 + 8]),
                    reads=pk(3), writes=["gtm"])
            srt, msk, e8, nmx, den = (gsc[:, 0:8], gsc[:, 8:16], gsc[:, 16:24],
                                      gsc[:, 24:25], gsc[:, 25:26])
            for tb in range(8):
                gt = gtm[:, tb * 8:(tb + 1) * 8]
                em.op("dve", lambda v, gt=gt: v.max(out=srt, in_=gt),
                      reads=["gtm"], writes=["srt"])
                em.op("dve", lambda v, gt=gt: v.tensor_scalar(
                    out=msk, in0=gt, scalar1=srt[:, 5:6], scalar2=None,
                    op0=ALU.is_ge), reads=["gtm", "srt"], writes=["msk"])
                em.op("dve", lambda v: v.tensor_scalar(
                    out=nmx, in0=srt[:, 0:1], scalar1=-1.0, scalar2=None,
                    op0=ALU.mult), reads=["srt"], writes=["nmx"])
                em.op("act", lambda s, gt=gt: s.activation(
                    e8, gt, AF.Exp, bias=nmx[:, 0:1]),
                    reads=["gtm", "nmx"], writes=["e8"])
                em.op("dve", lambda v: v.tensor_tensor(
                    out=e8, in0=e8, in1=msk, op=ALU.mult),
                    reads=["e8", "msk"], writes=["e8"])
                em.op("dve", lambda v: v.reduce_sum(out=den, in_=e8, axis=AXX),
                      reads=["e8"], writes=["den"])
                em.op("dve", lambda v: v.reciprocal(out=den, in_=den),
                      reads=["den"], writes=["den"])
                em.op("dve", lambda v, tb=tb: v.tensor_scalar(
                    out=wtmb[:, tb * 8:(tb + 1) * 8], in0=e8,
                    scalar1=den[:, 0:1], scalar2=None, op0=ALU.mult),
                    reads=["e8", "den"], writes=["wtmb"])
            for tb in range(8):
                bank = 0 if tb < 4 else 1
                em.mm(ps[2][0:8, tb * 128:(tb + 1) * 128], (pk(2)[bank],),
                      [(wtmb[:, tb * 8:(tb + 1) * 8], ID, ["wtmb", "consb"])],
                      start=(tb % 4 == 0), stop=(tb % 4 == 3))
            em.op("act", lambda s: s.copy(wT_sb[:], ps[2][0:8, :]),
                  reads=pk(2), writes=["wT"])
            wbc = sc.take(NEXP * 1024).rearrange("p (n w) -> p n w", w=1024)
            for n in range(NEXP):
                i = n % 2
                em.mm(ps[i], pk(i),
                      [(ind6[:, n * 128:(n + 1) * 128], wT_sb[:],
                        ["ind6", "wT"])])
                em.op("act", lambda s, n=n, i=i: s.copy(wbc[:, n, :], ps[i][:]),
                      reads=pk(i), writes=["wbc"])

            tq = [sc.take(1024), sc.take(1024)]
            scbase = sc.off
            scS = Bump(pool, scbase, SCE)
            sh3 = bf3(A4, NHC, 1024)
            su = WS("su", scS, NHC, 128)
            for uc in range(3):
                su.load(uc, colslice(din["wsu"].ap(), uc * 128, (uc + 1) * 128))
            for uc in range(NHC):
                if uc + 3 < NHC:
                    su.load(uc + 3, colslice(din["wsu"].ap(), (uc + 3) * 128,
                                             (uc + 4) * 128))
                buf, bk = su.get(uc)
                i = uc % 2
                em.mm(ps[i], pk(i),
                      [(buf[:, kc, :], xt3[:, kc, :], [bk, xt_key])
                       for kc in range(NHC)])
                em.op("act", lambda s, uc=uc, i=i: s.activation(
                    sh3[:, uc, :], ps[i][:], AF.Identity,
                    bias=biasf[:, 96 + uc:97 + uc]),
                    reads=pk(i) + ("biasf",), writes=["sh"])
            outc3 = bf3(A3, NHC, 1024)
            sa = WS("sa", scS, NEC, 128)
            for q_ in range(2):
                sa.load(q_, colslice(din["wsa"].ap(), q_ * 128, (q_ + 1) * 128))
            for s_ in range(2):
                for fc in range(NEC):
                    gidx = s_ * NEC + fc
                    if gidx + 2 < 16:
                        f2 = (gidx + 2) % NEC
                        sa.load(gidx + 2, colslice(din["wsa"].ap(), f2 * 128,
                                                   (f2 + 1) * 128))
                    buf, bk = sa.get(gidx)
                    i = gidx % 2
                    em.mm(ps[i], pk(i),
                          [(buf[:, kc, :], sh3[:, s_ * NEC + kc, :], [bk, "sh"])
                           for kc in range(NEC)])
                    em.op("act", lambda s, fc=fc, i=i: s.activation(
                        tq[i][:], ps[i][:], AF.Silu,
                        bias=biasf[:, 112 + fc:113 + fc]),
                        reads=pk(i) + ("biasf",), writes=[f"tq{i}"])
                    em.op("dve", lambda v, s_=s_, fc=fc, i=i: v.tensor_tensor(
                        out=outc3[:, s_ * NEC + fc, :], in0=tq[i][:],
                        in1=sh3[:, s_ * NEC + fc, :], op=ALU.mult),
                        reads=[f"tq{i}", "sh"], writes=["outc"])

            em.fence()
            h13 = bf3(A4, NEC, 1024)
            h1w3 = bf3(A4 + NEC * 1024, NEC, 1024)
            scE = Bump(pool, scbase, SCE)
            up = WS("up", scE, NHC, 128, nbuf=3)
            sw = WS("sw", scE, NEC, 128)
            hst = scE.take(NEC * 1024).rearrange("p (c w) -> p c w", w=1024)

            def up_src(gi):
                n, ec = divmod(gi, NEC)
                return colslice(din["wup"].ap()[n], ec * 128, (ec + 1) * 128)

            def sw_src(gi):
                n, fc = divmod(gi, NEC)
                return colslice(din["wsw"].ap()[n], fc * 128, (fc + 1) * 128)

            for i0 in range(2):
                up.load(i0, up_src(i0))
                sw.load(i0, sw_src(i0))
            sw.load(2, sw_src(2))
            for n in range(NEXP):
                for ec in range(NEC):
                    gi = n * NEC + ec
                    if gi + 2 < NEXP * NEC:
                        up.load(gi + 2, up_src(gi + 2))
                    buf, bk = up.get(gi)
                    i = ec % 2
                    em.mm(ps[i], pk(i),
                          [(buf[:, kc, :], xt3[:, kc, :], [bk, xt_key])
                           for kc in range(NHC)])
                    em.op("act", lambda s, n=n, ec=ec, i=i: s.activation(
                        h13[:, ec, :], ps[i][:], AF.Identity,
                        bias=biasf[:, n * 8 + ec:n * 8 + ec + 1]),
                        reads=pk(i) + ("biasf",), writes=["h1"])
                    em.op("dve", lambda v, n=n, ec=ec: v.tensor_tensor(
                        out=h1w3[:, ec, :], in0=h13[:, ec, :], in1=wbc[:, n, :],
                        op=ALU.mult), reads=["h1", "wbc"], writes=["h1w"])
                for fc in range(NEC):
                    gi = n * NEC + fc
                    if gi + 3 < NEXP * NEC:
                        sw.load(gi + 3, sw_src(gi + 3))

                    buf, bk = sw.get(gi)
                    i = fc % 2
                    em.mm(ps[i], pk(i),
                          [(buf[:, kc, :], h13[:, kc, :], [bk, "h1"])
                           for kc in range(NEC)])
                    em.op("act", lambda s, n=n, fc=fc, i=i: s.activation(
                        tq[i][:], ps[i][:], AF.Silu,
                        bias=biasf[:, 48 + n * 8 + fc:49 + n * 8 + fc]),
                        reads=pk(i) + ("biasf",), writes=[f"tq{i}"])
                    em.op("dve", lambda v, fc=fc, i=i: v.tensor_tensor(
                        out=hst[:, fc, :], in0=tq[i][:], in1=h1w3[:, fc, :],
                        op=ALU.mult), reads=[f"tq{i}", "h1w"], writes=["hst"])
                em.dma("sp", hd.ap()[n].rearrange("(c p) t -> p c t", p=P),
                       hst[:, :, :], reads=["hst"], writes=[f"hd{n}"])

            em.fence()
            sc2 = Bump(pool, SC, SCE)
            dnb = [sc2.take(NEC * 512).rearrange("p (c w) -> p c w", w=512)
                   for _ in range(2)]
            sdb = sc2.take(NHC * 512).rearrange("p (c w) -> p c w", w=512)
            hrv = [bf3(A4, NEC, 1024), bf3(A4 + NEC * 1024, NEC, 1024)]
            if out_mode == "resid":
                scA2 = Bump(pool, A2, A3)
                xin = scA2.take(4 * 1024, f32=True).rearrange(
                    "p (c w) -> p c w", w=1024)
                xo = [scA2.take(1024, f32=True), scA2.take(1024, f32=True)]
                u32 = scA2.take(1024, f32=True)
            xt1_3 = bf3(A2, NHC, 1024)
            for gidx in range(4):
                hc0 = gidx * 4
                em.dma("pool", dnb[0][:, :, :],
                       colslice(din["wdn"].ap()[0], gidx * 512,
                                (gidx + 1) * 512), writes=["dnb0"])
                em.dma("pool", sdb[:, :, :],
                       colslice(din["wsd"].ap(), gidx * 512, (gidx + 1) * 512),
                       writes=["sdb"])
                em.dma("sp", hrv[0][:, :, :],
                       hd.ap()[0].rearrange("(c p) t -> p c t", p=P),
                       reads=["hd0"], writes=["hr0"])
                if out_mode == "resid":
                    em.dma("sp", xin[:, :, :],
                           base_dram.ap()[hc0 * P:(hc0 + 4) * P, :].rearrange(
                               "(c p) t -> p c t", p=P),
                           reads=[base_key], writes=["xin"])
                for i in range(4):
                    hc = hc0 + i
                    em.mm(ps[i], pk(i),
                          [(bdn_sb[:, hc * 128:(hc + 1) * 128], wT_sb[0:6, :],
                            ["bdn", "wT"])], start=True, stop=False)
                for n in range(NEXP):
                    if n + 1 < NEXP:
                        em.dma("pool", dnb[(n + 1) % 2][:, :, :],
                               colslice(din["wdn"].ap()[n + 1], gidx * 512,
                                        (gidx + 1) * 512),
                               writes=[f"dnb{(n + 1) % 2}"])
                        em.dma("sp", hrv[(n + 1) % 2][:, :, :],
                               hd.ap()[n + 1].rearrange("(c p) t -> p c t", p=P),
                               reads=[f"hd{n + 1}"],
                               writes=[f"hr{(n + 1) % 2}"])
                    for i in range(4):
                        em.mm(ps[i], pk(i),
                              [(dnb[n % 2][:, ec, i * 128:(i + 1) * 128],
                                hrv[n % 2][:, ec, :],
                                [f"dnb{n % 2}", f"hr{n % 2}"])
                               for ec in range(NEC)],
                              start=False, stop=False)
                for i in range(4):
                    em.mm(ps[i], pk(i),
                          [(sdb[:, kc, i * 128:(i + 1) * 128],
                            outc3[:, kc, :], ["sdb", "outc"])
                           for kc in range(NHC)],
                          start=False, stop=True)
                for i in range(4):
                    hc = hc0 + i
                    bsd_c = biasf[:, 120 + hc:121 + hc]
                    if out_mode == "xt1":
                        em.op("dve", lambda v, i=i, hc=hc, bsd_c=bsd_c:
                              v.scalar_tensor_tensor(
                                  out=xt1_3[:, hc, :], in0=ps[i][:],
                                  scalar=bsd_c, in1=xln[:, hc, :],
                                  op0=ALU.add, op1=ALU.add),
                              reads=pk(i) + ("biasf", KLN), writes=["xt1"])
                    else:
                        em.op("dve", lambda v, i=i, hc=hc, bsd_c=bsd_c:
                              v.scalar_tensor_tensor(
                                  out=u32[:], in0=ps[i][:], scalar=bsd_c,
                                  in1=xln[:, hc, :], op0=ALU.add, op1=ALU.add),
                              reads=pk(i) + ("biasf", KLN), writes=["u32"])
                        par = i % 2
                        em.op("dve", lambda v, i=i, par=par:
                              v.scalar_tensor_tensor(
                                  out=xo[par][:], in0=u32[:], scalar=0.5,
                                  in1=xin[:, i, :], op0=ALU.mult, op1=ALU.add),
                              reads=["u32", "xin"], writes=[f"xo{par}"])
                        em.dma("act", out_dram.ap()[hc * P:(hc + 1) * P, :],
                               xo[par][:], reads=[f"xo{par}"], writes=[out_key])
                        if also_out is not None:
                            em.dma("act",
                                   also_out.ap()[hc * P:(hc + 1) * P, :],
                                   xo[par][:], reads=[f"xo{par}"],
                                   writes=["alsoout"])

        def emit_ffn_call(xsrc_dram, xsrc_key, out_dram, out_key, also_out=None):
            emit_ln(xsrc_dram, xsrc_key)
            xt1_3 = bf3(A2, NHC, 1024)
            emit_ffn(0, xln, KLN, "xt1", None, None, None, None)
            emit_ffn(1, xt1_3, "xt1", "resid", xsrc_dram, xsrc_key,
                     out_dram, out_key, also_out=also_out)

        if stage == "xln1":
            emit_ln(din["xT"], "xT")
            s3 = Bump(pool, A3, SC)
            st = [s3.take(1024, f32=True), s3.take(1024, f32=True)]
            for hc in range(NHC):
                par = hc % 2
                em.op("act", lambda s, hc=hc, par=par: s.copy(
                    st[par][:], xln[:, hc, :]), reads=[KLN],
                    writes=[f"st{par}"])
                em.dma("act", outd.ap()[hc * P:(hc + 1) * P, :], st[par][:],
                       reads=[f"st{par}"], writes=["outd"])
        else:
            emit_ffn_call(din["xT"], "xT", x1d, "x1d",
                          also_out=(outd if stage == "x1" else None))

        if stage in ("x2", "full"):
            emit_ln(x1d, "x1d")
            em.fence()
            sc = Bump(pool, SC, SCE)
            qT3 = bf3(A2, NHC, 1024)
            wsl = WS("wsl", sc, NHC, 128)
            kst = [sc.take(1024), sc.take(1024)]
            vst = [sc.take(16 * 65), sc.take(16 * 65)]

            def proj_src(wname, mc):
                return colslice(din[wname].ap(), mc * 128, (mc + 1) * 128)

            def proj_loop(idx0, wname, consume):
                for mc in range(3):
                    wsl.load(idx0 + mc, proj_src(wname, mc))
                for mc in range(NHC):
                    if mc + 3 < NHC:
                        wsl.load(idx0 + mc + 3, proj_src(wname, mc + 3))
                    buf, bk = wsl.get(idx0 + mc)
                    consume(mc, buf, bk)

            def q_consume(mc, buf, bk):
                i = mc % 2
                em.mm(ps[i], pk(i),
                      [(buf[:, kc, :], xln[:, kc, :], [bk, KLN])
                       for kc in range(NHC)])
                em.op("act", lambda s, mc=mc, i=i: s.activation(
                    qT3[:, mc, :], ps[i][:], AF.Identity, scale=0.125),
                    reads=pk(i), writes=["qT"])

            def k_consume(mc, buf, bk):
                i = mc % 2
                em.mm(ps[i], pk(i),
                      [(buf[:, kc, :], xln[:, kc, :], [bk, KLN])
                       for kc in range(NHC)])
                em.op("act", lambda s, i=i: s.copy(kst[i][:], ps[i][:]),
                      reads=pk(i), writes=[f"kst{i}"])
                em.rankdma(
                    g, r_prank,
                    lambda mc=mc, i=i: (
                        ktsh.ap()[mc * P:(mc + 1) * P, 0:TC], kst[i][:]),
                    lambda mc=mc, i=i: (
                        ktsh.ap()[mc * P:(mc + 1) * P, TC:2 * TC], kst[i][:]),
                    reads=[f"kst{i}"], writes=["ktsh"])

            proj_loop(0, "wqT", q_consume)
            proj_loop(16, "wkT", k_consume)

            vhalf = bf3(A4, NHC, 1024)
            for half in range(2):
                em.fence() if half == 0 else None
                em.dma("pool", vhalf[:, :, :],
                       din["wvT"].ap()[:, half * 1024:(half + 1) * 1024]
                       .rearrange("(c p) o -> p c o", p=P),
                       writes=["vhalf"])
                for tc in range(8):
                    i = tc % 2
                    em.mm(ps[i], pk(i),
                          [(xln[:, kc, tc * 128:(tc + 1) * 128],
                            vhalf[:, kc, :], [KLN, "vhalf"])
                           for kc in range(NHC)])
                    em.op("dve", lambda v, i=i: v.memset(vst[i][:], 1.0),
                          writes=[f"vst{i}"])
                    em.op("act", lambda s, i=i: s.copy(
                        vst[i].rearrange("p (b c) -> p b c", c=65)[:, :, 0:64],
                        ps[i][:].rearrange("p (b c) -> p b c", c=64)),
                        reads=pk(i) + (f"vst{i}",), writes=[f"vst{i}"])
                    def _vdst(q0v, tc=tc, half=half):
                        return dataclasses.replace(
                            vsh.ap(), ap=[[HEADS * 65, P], [65, 16], [1, 65]],
                            offset=(q0v + tc * 128) * HEADS * 65 + half * 16 * 65)
                    em.rankdma(
                        g, r_prank,
                        lambda i=i: (_vdst(0),
                                     vst[i].rearrange("p (b c) -> p b c", c=65)),
                        lambda i=i: (_vdst(TC),
                                     vst[i].rearrange("p (b c) -> p b c", c=65)),
                        reads=[f"vst{i}"], writes=["vsh"])

            em.rankdma(g, r_prank,
                       lambda: (flg.ap()[0:1, :], nonce_sb[:]),
                       lambda: (flg.ap()[1:2, :], nonce_sb[:]),
                       reads=["ktsh", "vsh", "nonce"], writes=["flg"])
            g.wait_ge(em.sem[("d", "pool")], em.cnt[("d", "pool")])
            em.hw[("pool", ("d", "pool"))] = em.cnt[("d", "pool")]
            with (g.register("r_sm") as r_sm, g.register("r_ne") as r_ne,
                  g.register("r_2n") as r_2n, g.register("r_nn") as r_nn,
                  g.register("r_tg") as r_tg):
                g.reg_load(r_nn, nonce_sb[0:1, 0:1])
                g.reg_add(r_2n, r_nn, r_nn)
                g.reg_mov(r_ne, 1)
                g.reg_mov(r_tg, 0)
                with g.While(r_ne):
                    g.dma_start(out=pf_sb[:, 0:32],
                                in_=flg.ap()[0:2, :].rearrange("a b -> (a b)")
                                .rearrange("(o ab) -> o ab", o=1)
                                ).then_inc(spin_sem, 16)
                    g.reg_add(r_tg, r_tg, 16)
                    g.wait_ge(spin_sem, r_tg)
                    g.reg_load(r_sm, pf_sb[0:1, 0:1])
                    g.reg_load(r_ne, pf_sb[0:1, 16:17])
                    g.reg_add(r_sm, r_sm, r_ne)
                    g.reg_alu(r_ne, r_sm, r_2n, ALU.not_equal)
            barrier_tok = em.op("pool", lambda gg: gg.memset(pf2_sb[:], 0))
            em.writer["ktsh"] = barrier_tok
            em.writer["vsh"] = barrier_tok

            em.fence()
            BT3 = bf3(A3, NHC, 1024)
            em.rankdma(g, r_prank,
                       lambda: (BT3[:, :, :],
                                BTd.ap()[0:16 * P, :].rearrange(
                                    "(c p) t -> p c t", p=P)),
                       lambda: (BT3[:, :, :],
                                BTd.ap()[16 * P:32 * P, :].rearrange(
                                    "(c p) t -> p c t", p=P)),
                       reads=["BTd"], writes=["BT"])
            ctxn3 = bf3(A4, NHC, 1024)
            scH = Bump(pool, sc.off, SCE)
            ktt = scH.take(2048)
            vh = [scH.take(16 * 65).rearrange("p (c w) -> p c w", w=65)
                  for _ in range(2)]
            sbe = [scH.take(1024), scH.take(1024)]
            Eb = [scH.take(1024), scH.take(1024)]
            rb = scH.take(1024, f32=True)
            rden = scH.take(1024, f32=True)
            ctmp = scH.take(1024)

            def kth_load(h):
                p0 = (h % 2) * 64
                em.dma("sp", ktt[p0:p0 + 64, :],
                       ktsh.ap()[h * 64:(h + 1) * 64, :],
                       reads=["ktsh"], writes=[f"ktt{h % 2}"])

            def vh_load(h):
                em.dma("sp", vh[h % 2][:, :, :],
                       vsh.ap().rearrange("(c p) (b e) -> p c b e", p=P, e=65)
                       [:, :, h, :], reads=["vsh"], writes=[f"vh{h % 2}"])

            kth_load(0)
            vh_load(0)
            for h in range(HEADS):
                if h + 1 < HEADS:
                    kth_load(h + 1)
                    vh_load(h + 1)
                p0 = (h % 2) * 64
                qh = qT3[:, h // 2, :][64 * (h % 2):64 * (h % 2) + 64, :]
                for kc in range(NHC):
                    i = kc % 2
                    em.mm(ps[i], pk(i),
                          [(ktt[p0:p0 + 64, kc * 128:(kc + 1) * 128], qh,
                            [f"ktt{h % 2}", "qT"])])
                    em.op("dve", lambda v, i=i, kc=kc: v.tensor_tensor(
                        out=sbe[i][:], in0=ps[i][:], in1=BT3[:, kc, :],
                        op=ALU.add), reads=pk(i) + ("BT",), writes=[f"sbe{i}"])
                    em.op("act", lambda s, i=i: s.activation(
                        Eb[i][:], sbe[i][:], AF.Exp),
                        reads=[f"sbe{i}"], writes=[f"Eb{i}"])
                    if kc >= 1:
                        j = (kc - 1) % 2
                        em.mm(ps[2][0:65, :], pk(2),
                              [(vh[h % 2][:, kc - 1, :], Eb[j][:],
                                [f"vh{h % 2}", f"Eb{j}"])],
                              start=(kc == 1), stop=False)
                em.mm(ps[2][0:65, :], pk(2),
                      [(vh[h % 2][:, NHC - 1, :], Eb[(NHC - 1) % 2][:],
                        [f"vh{h % 2}", f"Eb{(NHC - 1) % 2}"])],
                      start=False, stop=True)
                em.op("dve", lambda v: v.reciprocal(
                    out=rden[64:65, :], in_=ps[2][64:65, :]),
                    reads=pk(2), writes=["rden"])
                em.mm(ps[3][0:64, :], pk(3),
                      [(onesf[64:65, 0:64], rden[64:65, :],
                        ["rden", "onesf"])])
                em.op("act", lambda s: s.copy(rb[0:64, :], ps[3][0:64, :]),
                      reads=pk(3), writes=["rb"])
                if h % 2 == 0:
                    em.op("dve", lambda v, h=h: v.tensor_tensor(
                        out=ctxn3[:, h // 2, :][0:64, :],
                        in0=ps[2][0:64, :], in1=rb[0:64, :], op=ALU.mult),
                        reads=pk(2) + ("rb",), writes=["ctxn"])
                else:
                    em.op("dve", lambda v: v.tensor_tensor(
                        out=ctmp[0:64, :], in0=ps[2][0:64, :],
                        in1=rb[0:64, :], op=ALU.mult),
                        reads=pk(2) + ("rb",), writes=["ctmp"])
                    em.dma("sp", ctxn3[:, h // 2, :][64:128, :], ctmp[0:64, :],
                           reads=["ctmp"], writes=["ctxn"])

            em.fence()
            scO = Bump(pool, scH.base, SCE)
            x1in = [scO.take(1024, f32=True), scO.take(1024, f32=True)]
            xo2 = [scO.take(1024, f32=True), scO.take(1024, f32=True)]
            out2 = outd if stage == "x2" else x2d
            okey = "outd" if stage == "x2" else "x2d"

            def o_consume(mc, buf, bk):
                i = mc % 2
                em.dma("sp", x1in[i][:], x1d.ap()[mc * P:(mc + 1) * P, :],
                       reads=["x1d"], writes=[f"x1in{i}"])
                em.mm(ps[i], pk(i),
                      [(buf[:, kc, :], ctxn3[:, kc, :], [bk, "ctxn"])
                       for kc in range(NHC)])
                em.op("dve", lambda v, i=i: v.tensor_tensor(
                    out=xo2[i][:], in0=ps[i][:], in1=x1in[i][:], op=ALU.add),
                    reads=pk(i) + (f"x1in{i}",), writes=[f"xo2{i}"])
                em.dma("act", out2.ap()[mc * P:(mc + 1) * P, :], xo2[i][:],
                       reads=[f"xo2{i}"], writes=[okey])

            proj_loop(32, "woT", o_consume)

        if stage == "full":
            emit_ffn_call(x2d, "x2d", outd, "outd")

        for sk in (("d", "act"), ("d", "sp"), ("d", "pool")):
            if em.cnt[sk]:
                g.wait_ge(em.sem[sk], em.cnt[sk])
        em._deps("pool", ["nonce"], [])
        with g.If_eq(r_prank, 0):
            g.dma_start(out=flg.ap()[2:3, :], in_=nonce_sb[:]).then_inc(
                spin_sem, 16)
        with g.Else():
            g.dma_start(out=flg.ap()[3:4, :], in_=nonce_sb[:]).then_inc(
                spin_sem, 16)
        g.wait_ge(spin_sem, 16)
        with (g.register("r_sm2") as r_sm, g.register("r_ne2") as r_ne,
              g.register("r_2n2") as r_2n, g.register("r_nn2") as r_nn,
              g.register("r_tg2") as r_tg):
            g.reg_load(r_nn, nonce_sb[0:1, 0:1])
            g.reg_add(r_2n, r_nn, r_nn)
            g.reg_mov(r_ne, 1)
            g.reg_mov(r_tg, 16)
            with g.While(r_ne):
                g.dma_start(out=pf2_sb[:, 0:32],
                            in_=flg.ap()[2:4, :].rearrange("a b -> (a b)")
                            .rearrange("(o ab) -> o ab", o=1)
                            ).then_inc(spin_sem, 16)
                g.reg_add(r_tg, r_tg, 16)
                g.wait_ge(spin_sem, r_tg)
                g.reg_load(r_sm, pf2_sb[0:1, 0:1])
                g.reg_load(r_ne, pf2_sb[0:1, 16:17])
                g.reg_add(r_sm, r_sm, r_ne)
                g.reg_alu(r_ne, r_sm, r_2n, ALU.not_equal)

    return nc




def _prep_inputs(inputs, nonce):
    f = np.float32
    rel = np.asarray(inputs["L0_attn_rel_pos"], dtype=f)
    base = {
        "wqT": np.ascontiguousarray(np.asarray(inputs["L0_attn_wq"]).T, dtype=f),
        "wkT": np.ascontiguousarray(np.asarray(inputs["L0_attn_wk"]).T, dtype=f),
        "wvT": np.ascontiguousarray(np.asarray(inputs["L0_attn_wv"]).T, dtype=f),
        "woT": np.ascontiguousarray(np.asarray(inputs["L0_attn_wo"]).T, dtype=f),
        "wup": np.ascontiguousarray(inputs["L0_ffn_Wup"], dtype=f),
        "wsw": np.ascontiguousarray(inputs["L0_ffn_Wsw"], dtype=f),
        "wdn": np.ascontiguousarray(inputs["L0_ffn_Wdn"], dtype=f),
        "wsu": np.ascontiguousarray(inputs["L0_ffn_Wsu"], dtype=f),
        "wsa": np.ascontiguousarray(inputs["L0_ffn_Wsa"], dtype=f),
        "wsd": np.ascontiguousarray(inputs["L0_ffn_Wsd"], dtype=f),
        "wg": np.ascontiguousarray(
            np.asarray(inputs["L0_ffn_Wg"], dtype=f).reshape(2, 16, 128, 8)
            .transpose(2, 0, 1, 3).reshape(128, 256)),
        "bupr": np.ascontiguousarray(
            np.asarray(inputs["L0_ffn_bup"], dtype=f).reshape(6, 8, 128)
            .transpose(2, 0, 1).reshape(128, 48)),
        "bswr": np.ascontiguousarray(
            np.asarray(inputs["L0_ffn_bsw"], dtype=f).reshape(6, 8, 128)
            .transpose(2, 0, 1).reshape(128, 48)),
        "bsur": np.ascontiguousarray(
            np.asarray(inputs["L0_ffn_bsu"], dtype=f).reshape(16, 128).T),
        "bsar": np.ascontiguousarray(
            np.asarray(inputs["L0_ffn_bsa"], dtype=f).reshape(8, 128).T),
        "bsdr": np.ascontiguousarray(
            np.asarray(inputs["L0_ffn_bsd"], dtype=f).reshape(16, 128).T),
        "bgr": np.ascontiguousarray(np.asarray(inputs["L0_ffn_bg"], dtype=f).T),
        "bdnr": np.ascontiguousarray(inputs["L0_ffn_bdn"], dtype=f),
        "relrev": np.ascontiguousarray(
            np.concatenate([rel[::-1], np.zeros((1, 64), f)], axis=0)),
        "consts": np.ascontiguousarray(np.concatenate(
            [np.eye(128, dtype=f), np.eye(128, dtype=f)[:, ::-1],
             np.ones((128, 1), f)], axis=1)),
        "ind6h": np.ascontiguousarray(np.concatenate(
            [np.tile(np.eye(8, dtype=f)[n][:, None], (1, 128))
             for n in range(6)], axis=1)),
        "onesf": np.ones((128, 128), f),
        "noncein": np.full((1, 16), nonce, np.int32),
    }
    x = np.asarray(inputs["L0_attn_hidden_states"], dtype=f)
    maps = []
    for c in range(8):
        b, half = c // 2, c % 2
        m = dict(base)
        m["xT"] = np.ascontiguousarray(x[b, half * TC:(half + 1) * TC, :].T)
        maps.append(m)
    return maps


def kernel(stage="full", **inputs):
    nonce = (int.from_bytes(os.urandom(4), "little") | 1) & 0x7FFFFFFF
    maps = _prep_inputs(inputs, nonce)
    nc = build(stage=stage)
    res = run_bass_kernel_spmd(nc, maps, core_ids=list(range(8)))
    out = np.empty((4, 2048, 2048), np.float32)
    for c in range(8):
        b, half = c // 2, c % 2
        out[b, half * TC:(half + 1) * TC, :] = np.asarray(
            res.results[c]["out"]).T
    return out


# version 02
# speedup vs baseline: 1.3576x; 1.3576x over previous
import os
import dataclasses
import numpy as np
import concourse.bass as bass
import concourse.mybir as mybir
from concourse.bass_utils import run_bass_kernel_spmd
from contextlib import ExitStack

F32 = mybir.dt.float32
BF16 = mybir.dt.bfloat16
I32 = mybir.dt.int32
AF = mybir.ActivationFunctionType
ALU = mybir.AluOpType
AXX = mybir.AxisListType.X

P = 128
H = 2048
NHC = 16
TC = 1024
E = 1024
NEC = 8
NEXP = 6
HEADS = 32

PKEY = ("p", "pe")


class Em:

    def __init__(self, nc, ctx):
        self.nc = nc
        self.eng = {"pe": nc.tensor, "act": nc.scalar, "dve": nc.vector,
                    "sp": nc.sync, "pool": nc.gpsimd}
        self.sem = {}
        for e in ("pe", "act", "dve", "pool"):
            self.sem[("p", e)] = ctx.enter_context(nc.semaphore("prog_" + e))
        for q in ("sp", "act", "pool"):
            self.sem[("d", q)] = ctx.enter_context(nc.semaphore("dma_" + q))
        self.cnt = {k: 0 for k in self.sem}
        self.hw = {}
        self.writer = {}
        self.readers = {}

    def _wait(self, eng, tok):
        if tok is None:
            return
        sk, v = tok
        if sk == PKEY and eng == "pe":
            return
        if v <= self.hw.get((eng, sk), 0):
            return
        self.eng[eng].wait_ge(self.sem[sk], v)
        self.hw[(eng, sk)] = v

    def _deps(self, eng, reads, writes):
        for k in reads:
            self._wait(eng, self.writer.get(k))
        for k in writes:
            self._wait(eng, self.writer.get(k))
            for sk, v in self.readers.get(k, {}).items():
                self._wait(eng, (sk, v))

    def _record(self, tok, reads, writes):
        for k in writes:
            self.writer[k] = tok
            self.readers[k] = {}
        if tok is not None:
            sk, v = tok
            for k in reads:
                r = self.readers.setdefault(k, {})
                r[sk] = max(r.get(sk, 0), v)

    def op(self, eng, fn, reads=(), writes=()):
        self._deps(eng, reads, writes)
        inst = fn(self.eng[eng])
        sk = ("p", eng)
        self.cnt[sk] += 1
        inst.then_inc(self.sem[sk], 1)
        tok = (sk, self.cnt[sk])
        self._record(tok, reads, writes)
        return tok

    def dma(self, q, out, in_, reads=(), writes=(), slow=False):
        self._deps(q, reads, writes)
        inst = self.eng[q].dma_start(out=out, in_=in_,
                                     allow_slow_non_contiguous=slow)
        sk = ("d", q)
        self.cnt[sk] += 16
        inst.then_inc(self.sem[sk], 16)
        tok = (sk, self.cnt[sk])
        self._record(tok, reads, writes)
        return tok

    def rankdma(self, g, r_prank, mk0, mk1, reads=(), writes=()):
        self._deps("pool", reads, writes)
        sk = ("d", "pool")
        with g.If_eq(r_prank, 0):
            o0, i0 = mk0()
            g.dma_start(out=o0, in_=i0).then_inc(self.sem[sk], 16)
        with g.Else():
            o1, i1 = mk1()
            g.dma_start(out=o1, in_=i1).then_inc(self.sem[sk], 16)
        self.cnt[sk] += 16
        tok = (sk, self.cnt[sk])
        self._record(tok, reads, writes)
        return tok

    def mm(self, pap, pkeys, kitems, start=True, stop=True):
        W = pap.shape[-1]
        halves = [(0, W)] if W <= 512 else [(a, a + 512) for a in range(0, W, 512)]
        all_reads = set()
        n = len(kitems)
        for j, (la, ra, reads) in enumerate(kitems):
            first, last = j == 0, j == n - 1
            self._deps("pe", reads, pkeys if (first and start) else ())
            all_reads |= set(reads)
            for hi, (a, b) in enumerate(halves):
                rv = ra if len(halves) == 1 else ra[:, a:b]
                pv = pap if len(halves) == 1 else pap[:, a:b]
                inst = self.eng["pe"].matmul(
                    pv, la, rv, start=(first and start),
                    stop=(last and stop))
                if last and hi == len(halves) - 1:
                    self.cnt[PKEY] += 1
                    inst.then_inc(self.sem[PKEY], 1)
                    tok = (PKEY, self.cnt[PKEY])
        if start:
            self._record(tok, sorted(all_reads), pkeys)
        else:
            for k in pkeys:
                self.writer[k] = tok
            self._record(tok, sorted(all_reads), ())
        return tok

    def fence(self):
        for eng in ("pe", "act", "dve", "sp", "pool"):
            for sk, c in self.cnt.items():
                if c == 0:
                    continue
                if sk == ("p", eng):
                    continue
                self._wait(eng, (sk, c))


class Bump:
    def __init__(self, pool, base, limit):
        self.pool, self.base, self.limit, self.off = pool, base, limit, base

    def take(self, n, f32=False):
        elems = 2 * n if f32 else n
        pad = (elems + 63) & ~63
        a = self.off
        assert a + pad <= self.limit, f"scratch overflow {a}+{pad}>{self.limit}"
        self.off = a + pad
        v = self.pool[:, a:a + elems]
        return v.bitcast(F32) if f32 else v


def build(stage="full"):
    nc = bass.Bass("TRN2", target_bir_lowering=False)

    din = {}
    for name, shape, dt in [
        ("xT", [H, TC], F32),
        ("wqT", [H, H], F32), ("wkT", [H, H], F32),
        ("wvT", [H, H], F32), ("woT", [H, H], F32),
        ("wg", [P, 256], F32),
        ("wup", [NEXP, H, E], F32), ("wsw", [NEXP, E, E], F32),
        ("wdn", [NEXP, E, H], F32),
        ("wsu", [H, 2 * E], F32), ("wsa", [E, E], F32), ("wsd", [2 * E, H], F32),
        ("bupr", [P, 48], F32), ("bswr", [P, 48], F32), ("bsur", [P, 16], F32),
        ("bsar", [P, 8], F32), ("bsdr", [P, 16], F32), ("bgr", [8, 2], F32),
        ("bdnr", [NEXP, H], F32),
        ("relrev", [1024, 64], F32),
        ("consts", [P, 257], F32),
        ("ind6h", [8, NEXP * P], F32),
        ("onesf", [P, P], F32),
        ("noncein", [1, 16], I32),
    ]:
        din[name] = nc.dram_tensor(name, shape, dt, kind="ExternalInput")

    outd = nc.dram_tensor("out", [H, TC], F32, kind="ExternalOutput")

    x1d = nc.dram_tensor("x1d", [H, TC], F32, kind="Internal")
    x2d = nc.dram_tensor("x2d", [H, TC], F32, kind="Internal")
    hd = nc.dram_tensor("hd", [NEXP, E, TC], BF16, kind="Internal")
    Pd = nc.dram_tensor("Pd", [4096], F32, kind="Internal")
    mbrd = nc.dram_tensor("mbrd", [1024], F32, kind="Internal")
    BTd = nc.dram_tensor("BTd", [2 * 2 * TC, TC], BF16, kind="Internal")
    ktsh = nc.dram_tensor("ktsh", [H, 2 * TC], BF16, kind="Internal",
                          addr_space="Shared")
    vsh = nc.dram_tensor("vsh", [2 * TC, HEADS * 65], BF16, kind="Internal",
                         addr_space="Shared")
    flg = nc.dram_tensor("flg", [4, 16], I32, kind="Internal",
                         addr_space="Shared")

    with ExitStack() as ctx:
        em = Em(nc, ctx)
        spin_sem = ctx.enter_context(nc.semaphore("spin"))

        POOL_E = 92 * 1024
        pool = ctx.enter_context(nc.sbuf_tensor("pool", [P, POOL_E], BF16))
        A1, A2, A3, A4, SC = 0, 16384, 32768, 49152, 65536
        SCE = POOL_E

        bcb = ctx.enter_context(nc.sbuf_tensor("bcb", [P, 2048], F32))
        consb = ctx.enter_context(nc.sbuf_tensor("consb", [P, 257], BF16))
        onesf = ctx.enter_context(nc.sbuf_tensor("onesfs", [P, P], F32))
        biasf = ctx.enter_context(nc.sbuf_tensor("biasf", [P, 136], F32))
        bg_sb = ctx.enter_context(nc.sbuf_tensor("bg_sb", [8, 2], F32))
        bdn_sb = ctx.enter_context(nc.sbuf_tensor("bdn_sb", [NEXP, H], BF16))
        wg_sb = ctx.enter_context(nc.sbuf_tensor("wg_sb", [P, 256], BF16))
        ind6 = ctx.enter_context(nc.sbuf_tensor("ind6", [8, NEXP * P], BF16))
        gsT = bcb[:, 0:512].bitcast(BF16)[0:8, :]
        wT_sb = bcb[:, 512:1024].bitcast(BF16)[0:8, :]
        nonce_sb = ctx.enter_context(nc.sbuf_tensor("nonce_sb", [1, 16], I32))
        epsc = ctx.enter_context(nc.sbuf_tensor("epsc", [1, 8], F32))
        pf_sb = ctx.enter_context(nc.sbuf_tensor("pf_sb", [1, 32], I32))
        pf2_sb = ctx.enter_context(nc.sbuf_tensor("pf2_sb", [1, 32], I32))

        ps = [ctx.enter_context(nc.psum_tensor(f"ps{i}", [P, 1024], F32))
              for i in range(4)]

        block = ctx.enter_context(nc.Block())

        def pk(i):
            return (f"ps{i}a", f"ps{i}b")

        ID = consb[:, 0:128]
        JM = consb[:, 128:256]
        ONE1 = consb[:, 256:257]

        def bf3(base, nch, w):
            return pool[:, base:base + nch * w].rearrange(
                "p (c w) -> p c w", w=w)

        xln = bf3(A1, NHC, 1024)
        KLN = "xlnA1"

        em.dma("pool", consb[:], din["consts"].ap(), writes=["consb"])
        em.dma("sp", onesf[:], din["onesf"].ap(), writes=["onesf"])
        em.dma("sp", biasf[:, 0:48], din["bupr"].ap(), writes=["biasf"])
        em.dma("sp", biasf[:, 48:96], din["bswr"].ap(), writes=["biasf"])
        em.dma("sp", biasf[:, 96:112], din["bsur"].ap(), writes=["biasf"])
        em.dma("sp", biasf[:, 112:120], din["bsar"].ap(), writes=["biasf"])
        em.dma("sp", biasf[:, 120:136], din["bsdr"].ap(), writes=["biasf"])
        em.dma("sp", bg_sb[:], din["bgr"].ap(), writes=["bg"])
        em.dma("pool", bdn_sb[:], din["bdnr"].ap(), writes=["bdn"])
        em.dma("pool", wg_sb[:], din["wg"].ap(), writes=["wg"])
        em.dma("pool", ind6[:], din["ind6h"].ap(), writes=["ind6"])
        em.dma("sp", nonce_sb[:], din["noncein"].ap(), writes=["nonce"])
        em.op("dve", lambda v: v.memset(epsc[:], 1e-5), writes=["epsc"])

        g = em.eng["pool"]
        pid = g.partition_id()
        r_prank = ctx.enter_context(g.register("r_prank"))
        g.reg_alu(r_prank, pid, 1, ALU.bitwise_and)


        def emit_btbuild():
            scp = Bump(pool, SC + 22528, SCE)
            wwin = [scp.take(1024), scp.take(1024)]
            btst = [scp.take(1024), scp.take(1024)]
            relsb = scp.take(512, f32=True)
            mbrf = scp.take(8, f32=True)
            em.dma("sp", relsb.rearrange("p (c d) -> p c d", d=64),
                   din["relrev"].ap().rearrange("(c p) d -> p c d", p=P),
                   writes=["relsb"])
            for c in range(8):
                em.op("dve", lambda v, c=c: v.reduce_sum(
                    out=mbrf[:, c:c + 1], in_=relsb[:, c * 64:(c + 1) * 64],
                    axis=AXX), reads=["relsb"], writes=["mbr"])
            em.op("dve", lambda v: v.tensor_scalar(
                out=mbrf[:], in0=mbrf[:], scalar1=1.0 / 64, scalar2=None,
                op0=ALU.mult), reads=["mbr"], writes=["mbr"])
            mbr_dst = dataclasses.replace(mbrd.ap(), ap=[[1, P], [P, 8]], offset=0)
            em.dma("sp", mbr_dst, mbrf[:], reads=["mbr"], writes=["mbrd"], slow=True)
            em.dma("sp", dataclasses.replace(Pd.ap(), ap=[[1, 1023]], offset=1536),
                   dataclasses.replace(mbrd.ap(), ap=[[1, 1023]], offset=0),
                   reads=["mbrd"], writes=["Pd"])
            em.dma("sp", dataclasses.replace(Pd.ap(), ap=[[1, 1536], [1, 1]], offset=0),
                   dataclasses.replace(mbrd.ap(), ap=[[0, 1536], [1, 1]], offset=0),
                   reads=["mbrd"], writes=["Pd"])
            em.dma("sp",
                   dataclasses.replace(Pd.ap(), ap=[[1, 4096 - 2559], [1, 1]], offset=2559),
                   dataclasses.replace(mbrd.ap(), ap=[[0, 4096 - 2559], [1, 1]], offset=1022),
                   reads=["mbrd"], writes=["Pd"])

            for sel in range(2):
                for kc in range(16):
                    par = kc % 2
                    src = dataclasses.replace(
                        Pd.ap(), ap=[[1, P], [1, 1024]],
                        offset=1920 + sel * TC - 128 * kc)
                    em.dma("pool", wwin[par][:], src, reads=["Pd"],
                           writes=[f"wwin{par}"])
                    em.mm(ps[3], pk(3), [(JM, wwin[par][:], [f"wwin{par}", "consb"])])
                    em.op("act", lambda s, par=par: s.copy(btst[par][:], ps[3][:]),
                          reads=pk(3), writes=[f"btst{par}"])
                    em.dma("pool",
                           BTd.ap()[(sel * 16 + kc) * P:(sel * 16 + kc + 1) * P, :],
                           btst[par][:], reads=[f"btst{par}"], writes=["BTd"])


        def emit_ln(src_dram, src_key):
            em.fence()
            xh = pool[:, A2:A2 + 2 * NHC * 1024].bitcast(F32).rearrange(
                "p (c w) -> p c w", w=1024)
            em.dma("sp", xh[:, :, :],
                   src_dram.ap().rearrange("(c p) t -> p c t", p=P),
                   reads=[src_key], writes=["xh"])
            s2 = Bump(pool, SC, SCE)
            xb = [s2.take(1024), s2.take(1024)]
            xq = [s2.take(1024), s2.take(1024)]
            lnr_mean = s2.take(1024, f32=True)
            lnr_var = s2.take(1024, f32=True)
            lnr_tmp = s2.take(1024, f32=True)
            lnr_rstd = s2.take(1024, f32=True)
            lnr_mrs = s2.take(1024, f32=True)
            for hc in range(NHC):
                par = hc % 2
                em.op("act", lambda s, hc=hc, par=par: s.copy(
                    xb[par][:], xh[:, hc, :]), reads=["xh"], writes=[f"xb{par}"])
                em.op("act", lambda s, hc=hc, par=par: s.activation(
                    xq[par][:], xh[:, hc, :], AF.Square),
                    reads=["xh"], writes=[f"xq{par}"])
                em.mm(ps[2][0:1, :], pk(2),
                      [(ONE1, xb[par][:], [f"xb{par}", "consb"])],
                      start=(hc == 0), stop=(hc == NHC - 1))
                em.mm(ps[3][0:1, :], pk(3),
                      [(ONE1, xq[par][:], [f"xq{par}", "consb"])],
                      start=(hc == 0), stop=(hc == NHC - 1))
            em.op("act", lambda s: s.mul(lnr_mean[0:1, :], ps[2][0:1, :],
                                          1.0 / H),
                  reads=pk(2), writes=["lnmean"])
            em.op("act", lambda s: s.mul(lnr_var[0:1, :], ps[3][0:1, :],
                                         1.0 / H),
                  reads=pk(3), writes=["lnvar"])
            em.op("act", lambda s: s.square(lnr_tmp[0:1, :], lnr_mean[0:1, :]),
                  reads=["lnmean"], writes=["lntmp"])
            em.op("dve", lambda v: v.tensor_tensor(
                out=lnr_var[0:1, :], in0=lnr_var[0:1, :],
                in1=lnr_tmp[0:1, :], op=ALU.subtract),
                reads=["lnvar", "lntmp"], writes=["lnvar"])
            em.op("act", lambda s: s.activation(
                lnr_tmp[0:1, :], lnr_var[0:1, :], AF.Ln,
                bias=epsc[0:1, 0:1]),
                reads=["lnvar", "epsc"], writes=["lntmp"])
            em.op("act", lambda s: s.activation(
                lnr_rstd[0:1, :], lnr_tmp[0:1, :], AF.Exp, scale=-0.5),
                reads=["lntmp"], writes=["rstd"])
            em.op("dve", lambda v: v.tensor_tensor(
                out=lnr_mrs[0:1, :], in0=lnr_mean[0:1, :],
                in1=lnr_rstd[0:1, :], op=ALU.mult),
                reads=["lnmean", "rstd"], writes=["mrs"])
            em.mm(ps[2], pk(2), [(onesf[0:1, :], lnr_rstd[0:1, :], ["rstd", "onesf"])])
            em.op("act", lambda s: s.copy(bcb[:, 0:1024], ps[2][:]),
                  reads=pk(2), writes=["bcb0"])
            em.mm(ps[3], pk(3), [(onesf[0:1, :], lnr_mrs[0:1, :], ["mrs", "onesf"])])
            em.op("act", lambda s: s.copy(bcb[:, 1024:2048], ps[3][:]),
                  reads=pk(3), writes=["bcb1"])
            t32 = [s2.take(1024, f32=True), s2.take(1024, f32=True)]
            for hc in range(NHC):
                par = hc % 2
                em.op("dve", lambda v, hc=hc, par=par: v.tensor_tensor(
                    out=t32[par][:], in0=xh[:, hc, :], in1=bcb[:, 0:1024],
                    op=ALU.mult), reads=["xh", "bcb0"], writes=[f"t32{par}"])
                em.op("dve", lambda v, hc=hc, par=par: v.tensor_tensor(
                    out=xln[:, hc, :], in0=t32[par][:], in1=bcb[:, 1024:2048],
                    op=ALU.subtract), reads=[f"t32{par}", "bcb1"], writes=[KLN])

        class WS:
            def __init__(self, name, scb, nch, w, nbuf=4):
                self.name, self.nbuf = name, nbuf
                self.bufs = [scb.take(nch * w).rearrange("p (c w) -> p c w", w=w)
                             for _ in range(nbuf)]

            def key(self, i):
                return f"{self.name}{i % self.nbuf}"

            def load(self, i, src_ap):
                em.dma("pool", self.bufs[i % self.nbuf][:, :, :], src_ap,
                       writes=[self.key(i)])

            def get(self, i):
                return self.bufs[i % self.nbuf], self.key(i)

        def colslice(t_ap, c0, c1, p=P):
            return t_ap[:, c0:c1].rearrange("(c p) o -> p c o", p=p)

        def emit_ffn(j, xt3, xt_key, out_mode, base_dram, base_key,
                     out_dram, out_key, also_out=None):
            em.fence()
            sc = Bump(pool, SC, SCE)
            gtm = sc.take(64, f32=True)
            gsc = sc.take(40, f32=True)
            wtmb = sc.take(64)
            em.mm(ps[2][0:8, :], pk(2),
                  [(wg_sb[:, j * 128 + kc * 8: j * 128 + kc * 8 + 8],
                    xt3[:, kc, :], ["wg", xt_key]) for kc in range(NHC)])
            em.op("act", lambda s: s.activation(
                gsT[:], ps[2][0:8, :], AF.Identity, bias=bg_sb[:, j:j + 1]),
                reads=pk(2) + ("bg",), writes=["gsT"])
            for tb in range(8):
                bank = 0 if tb < 4 else 1
                em.mm(ps[3][:, tb * 128: tb * 128 + 8], (pk(3)[bank],),
                      [(gsT[:, tb * 128:(tb + 1) * 128], ID[0:8, 0:8],
                        ["gsT", "consb"])],
                      start=(tb % 4 == 0), stop=(tb % 4 == 3))
            for tb in range(8):
                em.op("act", lambda s, tb=tb: s.copy(
                    gtm[:, tb * 8:(tb + 1) * 8],
                    ps[3][:, tb * 128:tb * 128 + 8]),
                    reads=pk(3), writes=["gtm"])
            srt, msk, e8, nmx, den = (gsc[:, 0:8], gsc[:, 8:16], gsc[:, 16:24],
                                      gsc[:, 24:25], gsc[:, 25:26])
            for tb in range(8):
                gt = gtm[:, tb * 8:(tb + 1) * 8]
                em.op("dve", lambda v, gt=gt: v.max(out=srt, in_=gt),
                      reads=["gtm"], writes=["srt"])
                em.op("dve", lambda v, gt=gt: v.tensor_scalar(
                    out=msk, in0=gt, scalar1=srt[:, 5:6], scalar2=None,
                    op0=ALU.is_ge), reads=["gtm", "srt"], writes=["msk"])
                em.op("dve", lambda v: v.tensor_scalar(
                    out=nmx, in0=srt[:, 0:1], scalar1=-1.0, scalar2=None,
                    op0=ALU.mult), reads=["srt"], writes=["nmx"])
                em.op("act", lambda s, gt=gt: s.activation(
                    e8, gt, AF.Exp, bias=nmx[:, 0:1]),
                    reads=["gtm", "nmx"], writes=["e8"])
                em.op("dve", lambda v: v.tensor_tensor(
                    out=e8, in0=e8, in1=msk, op=ALU.mult),
                    reads=["e8", "msk"], writes=["e8"])
                em.op("dve", lambda v: v.reduce_sum(out=den, in_=e8, axis=AXX),
                      reads=["e8"], writes=["den"])
                em.op("dve", lambda v: v.reciprocal(out=den, in_=den),
                      reads=["den"], writes=["den"])
                em.op("dve", lambda v, tb=tb: v.tensor_scalar(
                    out=wtmb[:, tb * 8:(tb + 1) * 8], in0=e8,
                    scalar1=den[:, 0:1], scalar2=None, op0=ALU.mult),
                    reads=["e8", "den"], writes=["wtmb"])
            for tb in range(8):
                bank = 0 if tb < 4 else 1
                em.mm(ps[2][0:8, tb * 128:(tb + 1) * 128], (pk(2)[bank],),
                      [(wtmb[:, tb * 8:(tb + 1) * 8], ID, ["wtmb", "consb"])],
                      start=(tb % 4 == 0), stop=(tb % 4 == 3))
            em.op("act", lambda s: s.copy(wT_sb[:], ps[2][0:8, :]),
                  reads=pk(2), writes=["wT"])
            wbc = sc.take(NEXP * 1024).rearrange("p (n w) -> p n w", w=1024)
            for n in range(NEXP):
                i = n % 2
                em.mm(ps[i], pk(i),
                      [(ind6[:, n * 128:(n + 1) * 128], wT_sb[:],
                        ["ind6", "wT"])])
                em.op("act", lambda s, n=n, i=i: s.copy(wbc[:, n, :], ps[i][:]),
                      reads=pk(i), writes=["wbc"])

            tq = [sc.take(1024), sc.take(1024)]
            scbase = sc.off
            scS = Bump(pool, scbase, SCE)
            sh3 = bf3(A4, NHC, 1024)
            su = WS("su", scS, NHC, 128)
            for uc in range(3):
                su.load(uc, colslice(din["wsu"].ap(), uc * 128, (uc + 1) * 128))
            for uc in range(NHC):
                if uc + 3 < NHC:
                    su.load(uc + 3, colslice(din["wsu"].ap(), (uc + 3) * 128,
                                             (uc + 4) * 128))
                buf, bk = su.get(uc)
                i = uc % 2
                em.mm(ps[i], pk(i),
                      [(buf[:, kc, :], xt3[:, kc, :], [bk, xt_key])
                       for kc in range(NHC)])
                em.op("act", lambda s, uc=uc, i=i: s.activation(
                    sh3[:, uc, :], ps[i][:], AF.Identity,
                    bias=biasf[:, 96 + uc:97 + uc]),
                    reads=pk(i) + ("biasf",), writes=["sh"])
            outc3 = bf3(A3, NHC, 1024)
            sa = WS("sa", scS, NEC, 128)
            for q_ in range(2):
                sa.load(q_, colslice(din["wsa"].ap(), q_ * 128, (q_ + 1) * 128))
            for s_ in range(2):
                for fc in range(NEC):
                    gidx = s_ * NEC + fc
                    if gidx + 2 < 16:
                        f2 = (gidx + 2) % NEC
                        sa.load(gidx + 2, colslice(din["wsa"].ap(), f2 * 128,
                                                   (f2 + 1) * 128))
                    buf, bk = sa.get(gidx)
                    i = gidx % 2
                    em.mm(ps[i], pk(i),
                          [(buf[:, kc, :], sh3[:, s_ * NEC + kc, :], [bk, "sh"])
                           for kc in range(NEC)])
                    em.op("act", lambda s, fc=fc, i=i: s.activation(
                        tq[i][:], ps[i][:], AF.Silu,
                        bias=biasf[:, 112 + fc:113 + fc]),
                        reads=pk(i) + ("biasf",), writes=[f"tq{i}"])
                    em.op("dve", lambda v, s_=s_, fc=fc, i=i: v.tensor_tensor(
                        out=outc3[:, s_ * NEC + fc, :], in0=tq[i][:],
                        in1=sh3[:, s_ * NEC + fc, :], op=ALU.mult),
                        reads=[f"tq{i}", "sh"], writes=["outc"])

            em.fence()
            h13 = bf3(A4, NEC, 1024)
            h1w3 = bf3(A4 + NEC * 1024, NEC, 1024)
            scE = Bump(pool, scbase, SCE)
            up = WS("up", scE, NHC, 128, nbuf=3)
            sw = WS("sw", scE, NEC, 128)
            hst = scE.take(NEC * 1024).rearrange("p (c w) -> p c w", w=1024)

            def up_src(gi):
                n, ec = divmod(gi, NEC)
                return colslice(din["wup"].ap()[n], ec * 128, (ec + 1) * 128)

            def sw_src(gi):
                n, fc = divmod(gi, NEC)
                return colslice(din["wsw"].ap()[n], fc * 128, (fc + 1) * 128)

            for i0 in range(2):
                up.load(i0, up_src(i0))
                sw.load(i0, sw_src(i0))
            sw.load(2, sw_src(2))
            for n in range(NEXP):
                for ec in range(NEC):
                    gi = n * NEC + ec
                    if gi + 2 < NEXP * NEC:
                        up.load(gi + 2, up_src(gi + 2))
                    buf, bk = up.get(gi)
                    i = ec % 2
                    em.mm(ps[i], pk(i),
                          [(buf[:, kc, :], xt3[:, kc, :], [bk, xt_key])
                           for kc in range(NHC)])
                    em.op("act", lambda s, n=n, ec=ec, i=i: s.activation(
                        h13[:, ec, :], ps[i][:], AF.Identity,
                        bias=biasf[:, n * 8 + ec:n * 8 + ec + 1]),
                        reads=pk(i) + ("biasf",), writes=["h1"])
                    em.op("dve", lambda v, n=n, ec=ec: v.tensor_tensor(
                        out=h1w3[:, ec, :], in0=h13[:, ec, :], in1=wbc[:, n, :],
                        op=ALU.mult), reads=["h1", "wbc"], writes=["h1w"])
                for fc in range(NEC):
                    gi = n * NEC + fc
                    if gi + 3 < NEXP * NEC:
                        sw.load(gi + 3, sw_src(gi + 3))

                    buf, bk = sw.get(gi)
                    i = fc % 2
                    em.mm(ps[i], pk(i),
                          [(buf[:, kc, :], h13[:, kc, :], [bk, "h1"])
                           for kc in range(NEC)])
                    em.op("act", lambda s, n=n, fc=fc, i=i: s.activation(
                        tq[i][:], ps[i][:], AF.Silu,
                        bias=biasf[:, 48 + n * 8 + fc:49 + n * 8 + fc]),
                        reads=pk(i) + ("biasf",), writes=[f"tq{i}"])
                    em.op("dve", lambda v, fc=fc, i=i: v.tensor_tensor(
                        out=hst[:, fc, :], in0=tq[i][:], in1=h1w3[:, fc, :],
                        op=ALU.mult), reads=[f"tq{i}", "h1w"], writes=["hst"])
                em.dma("sp", hd.ap()[n].rearrange("(c p) t -> p c t", p=P),
                       hst[:, :, :], reads=["hst"], writes=[f"hd{n}"])

            em.fence()
            sc2 = Bump(pool, SC, SCE)
            dnb = [sc2.take(NEC * 512).rearrange("p (c w) -> p c w", w=512)
                   for _ in range(2)]
            sdb = sc2.take(NHC * 512).rearrange("p (c w) -> p c w", w=512)
            hrv = [bf3(A4, NEC, 1024), bf3(A4 + NEC * 1024, NEC, 1024)]
            if out_mode == "resid":
                scA2 = Bump(pool, A2, A3)
                xin = scA2.take(4 * 1024, f32=True).rearrange(
                    "p (c w) -> p c w", w=1024)
                xo = [scA2.take(1024, f32=True), scA2.take(1024, f32=True)]
                u32 = scA2.take(1024, f32=True)
            xt1_3 = bf3(A2, NHC, 1024)
            for gidx in range(4):
                hc0 = gidx * 4
                em.dma("pool", dnb[0][:, :, :],
                       colslice(din["wdn"].ap()[0], gidx * 512,
                                (gidx + 1) * 512), writes=["dnb0"])
                em.dma("pool", sdb[:, :, :],
                       colslice(din["wsd"].ap(), gidx * 512, (gidx + 1) * 512),
                       writes=["sdb"])
                em.dma("sp", hrv[0][:, :, :],
                       hd.ap()[0].rearrange("(c p) t -> p c t", p=P),
                       reads=["hd0"], writes=["hr0"])
                if out_mode == "resid":
                    em.dma("sp", xin[:, :, :],
                           base_dram.ap()[hc0 * P:(hc0 + 4) * P, :].rearrange(
                               "(c p) t -> p c t", p=P),
                           reads=[base_key], writes=["xin"])
                for i in range(4):
                    hc = hc0 + i
                    em.mm(ps[i], pk(i),
                          [(bdn_sb[:, hc * 128:(hc + 1) * 128], wT_sb[0:6, :],
                            ["bdn", "wT"])], start=True, stop=False)
                for n in range(NEXP):
                    if n + 1 < NEXP:
                        em.dma("pool", dnb[(n + 1) % 2][:, :, :],
                               colslice(din["wdn"].ap()[n + 1], gidx * 512,
                                        (gidx + 1) * 512),
                               writes=[f"dnb{(n + 1) % 2}"])
                        em.dma("sp", hrv[(n + 1) % 2][:, :, :],
                               hd.ap()[n + 1].rearrange("(c p) t -> p c t", p=P),
                               reads=[f"hd{n + 1}"],
                               writes=[f"hr{(n + 1) % 2}"])
                    for i in range(4):
                        em.mm(ps[i], pk(i),
                              [(dnb[n % 2][:, ec, i * 128:(i + 1) * 128],
                                hrv[n % 2][:, ec, :],
                                [f"dnb{n % 2}", f"hr{n % 2}"])
                               for ec in range(NEC)],
                              start=False, stop=False)
                for i in range(4):
                    em.mm(ps[i], pk(i),
                          [(sdb[:, kc, i * 128:(i + 1) * 128],
                            outc3[:, kc, :], ["sdb", "outc"])
                           for kc in range(NHC)],
                          start=False, stop=True)
                for i in range(4):
                    hc = hc0 + i
                    bsd_c = biasf[:, 120 + hc:121 + hc]
                    if out_mode == "xt1":
                        em.op("dve", lambda v, i=i, hc=hc, bsd_c=bsd_c:
                              v.scalar_tensor_tensor(
                                  out=xt1_3[:, hc, :], in0=ps[i][:],
                                  scalar=bsd_c, in1=xln[:, hc, :],
                                  op0=ALU.add, op1=ALU.add),
                              reads=pk(i) + ("biasf", KLN), writes=["xt1"])
                    else:
                        em.op("dve", lambda v, i=i, hc=hc, bsd_c=bsd_c:
                              v.scalar_tensor_tensor(
                                  out=u32[:], in0=ps[i][:], scalar=bsd_c,
                                  in1=xln[:, hc, :], op0=ALU.add, op1=ALU.add),
                              reads=pk(i) + ("biasf", KLN), writes=["u32"])
                        par = i % 2
                        em.op("dve", lambda v, i=i, par=par:
                              v.scalar_tensor_tensor(
                                  out=xo[par][:], in0=u32[:], scalar=0.5,
                                  in1=xin[:, i, :], op0=ALU.mult, op1=ALU.add),
                              reads=["u32", "xin"], writes=[f"xo{par}"])
                        em.dma("act", out_dram.ap()[hc * P:(hc + 1) * P, :],
                               xo[par][:], reads=[f"xo{par}"], writes=[out_key])
                        if also_out is not None:
                            em.dma("act",
                                   also_out.ap()[hc * P:(hc + 1) * P, :],
                                   xo[par][:], reads=[f"xo{par}"],
                                   writes=["alsoout"])

        def emit_ffn_call_post_ln(out_dram, out_key, also_out=None,
                                  xsrc_dram=None, xsrc_key="xT"):
            if xsrc_dram is None:
                xsrc_dram = din["xT"]
            xt1_3 = bf3(A2, NHC, 1024)
            emit_ffn(0, xln, KLN, "xt1", None, None, None, None)
            emit_ffn(1, xt1_3, "xt1", "resid", xsrc_dram, xsrc_key,
                     out_dram, out_key, also_out=also_out)

        def emit_ffn_call(xsrc_dram, xsrc_key, out_dram, out_key,
                          also_out=None):
            emit_ln(xsrc_dram, xsrc_key)
            emit_ffn_call_post_ln(out_dram, out_key, also_out=also_out,
                                  xsrc_dram=xsrc_dram, xsrc_key=xsrc_key)

        if stage == "xln1":
            emit_ln(din["xT"], "xT")
            emit_btbuild()
            s3 = Bump(pool, A3, SC)
            st = [s3.take(1024, f32=True), s3.take(1024, f32=True)]
            for hc in range(NHC):
                par = hc % 2
                em.op("act", lambda s, hc=hc, par=par: s.copy(
                    st[par][:], xln[:, hc, :]), reads=[KLN],
                    writes=[f"st{par}"])
                em.dma("act", outd.ap()[hc * P:(hc + 1) * P, :], st[par][:],
                       reads=[f"st{par}"], writes=["outd"])
        else:
            emit_ln(din["xT"], "xT")
            emit_btbuild()
            emit_ffn_call_post_ln(x1d, "x1d",
                                  also_out=(outd if stage == "x1" else None))

        if stage in ("x2", "full"):
            emit_ln(x1d, "x1d")
            em.fence()
            sc = Bump(pool, SC, SCE)
            qT3 = bf3(A2, NHC, 1024)
            wsl = WS("wsl", sc, NHC, 128)
            kst = [sc.take(1024), sc.take(1024)]
            vst = [sc.take(16 * 65), sc.take(16 * 65)]

            def proj_src(wname, mc):
                return colslice(din[wname].ap(), mc * 128, (mc + 1) * 128)

            def proj_loop(idx0, wname, consume):
                for mc in range(3):
                    wsl.load(idx0 + mc, proj_src(wname, mc))
                for mc in range(NHC):
                    if mc + 3 < NHC:
                        wsl.load(idx0 + mc + 3, proj_src(wname, mc + 3))
                    buf, bk = wsl.get(idx0 + mc)
                    consume(mc, buf, bk)

            def q_consume(mc, buf, bk):
                i = mc % 2
                em.mm(ps[i], pk(i),
                      [(buf[:, kc, :], xln[:, kc, :], [bk, KLN])
                       for kc in range(NHC)])
                em.op("act", lambda s, mc=mc, i=i: s.activation(
                    qT3[:, mc, :], ps[i][:], AF.Identity, scale=0.125),
                    reads=pk(i), writes=["qT"])

            def k_consume(mc, buf, bk):
                i = mc % 2
                em.mm(ps[i], pk(i),
                      [(buf[:, kc, :], xln[:, kc, :], [bk, KLN])
                       for kc in range(NHC)])
                em.op("act", lambda s, i=i: s.copy(kst[i][:], ps[i][:]),
                      reads=pk(i), writes=[f"kst{i}"])
                em.rankdma(
                    g, r_prank,
                    lambda mc=mc, i=i: (
                        ktsh.ap()[mc * P:(mc + 1) * P, 0:TC], kst[i][:]),
                    lambda mc=mc, i=i: (
                        ktsh.ap()[mc * P:(mc + 1) * P, TC:2 * TC], kst[i][:]),
                    reads=[f"kst{i}"], writes=["ktsh"])

            proj_loop(0, "wqT", q_consume)
            proj_loop(16, "wkT", k_consume)

            vhalf = bf3(A4, NHC, 1024)
            for half in range(2):
                em.fence() if half == 0 else None
                em.dma("pool", vhalf[:, :, :],
                       din["wvT"].ap()[:, half * 1024:(half + 1) * 1024]
                       .rearrange("(c p) o -> p c o", p=P),
                       writes=["vhalf"])
                for tc in range(8):
                    i = tc % 2
                    em.mm(ps[i], pk(i),
                          [(xln[:, kc, tc * 128:(tc + 1) * 128],
                            vhalf[:, kc, :], [KLN, "vhalf"])
                           for kc in range(NHC)])
                    em.op("dve", lambda v, i=i: v.memset(vst[i][:], 1.0),
                          writes=[f"vst{i}"])
                    em.op("act", lambda s, i=i: s.copy(
                        vst[i].rearrange("p (b c) -> p b c", c=65)[:, :, 0:64],
                        ps[i][:].rearrange("p (b c) -> p b c", c=64)),
                        reads=pk(i) + (f"vst{i}",), writes=[f"vst{i}"])
                    def _vdst(q0v, tc=tc, half=half):
                        return dataclasses.replace(
                            vsh.ap(), ap=[[HEADS * 65, P], [65, 16], [1, 65]],
                            offset=(q0v + tc * 128) * HEADS * 65 + half * 16 * 65)
                    em.rankdma(
                        g, r_prank,
                        lambda i=i: (_vdst(0),
                                     vst[i].rearrange("p (b c) -> p b c", c=65)),
                        lambda i=i: (_vdst(TC),
                                     vst[i].rearrange("p (b c) -> p b c", c=65)),
                        reads=[f"vst{i}"], writes=["vsh"])

            em.rankdma(g, r_prank,
                       lambda: (flg.ap()[0:1, :], nonce_sb[:]),
                       lambda: (flg.ap()[1:2, :], nonce_sb[:]),
                       reads=["ktsh", "vsh", "nonce"], writes=["flg"])
            g.wait_ge(em.sem[("d", "pool")], em.cnt[("d", "pool")])
            em.hw[("pool", ("d", "pool"))] = em.cnt[("d", "pool")]
            with (g.register("r_sm") as r_sm, g.register("r_ne") as r_ne,
                  g.register("r_2n") as r_2n, g.register("r_nn") as r_nn,
                  g.register("r_tg") as r_tg):
                g.reg_load(r_nn, nonce_sb[0:1, 0:1])
                g.reg_add(r_2n, r_nn, r_nn)
                g.reg_mov(r_ne, 1)
                g.reg_mov(r_tg, 0)
                with g.While(r_ne):
                    g.dma_start(out=pf_sb[:, 0:32],
                                in_=flg.ap()[0:2, :].rearrange("a b -> (a b)")
                                .rearrange("(o ab) -> o ab", o=1)
                                ).then_inc(spin_sem, 16)
                    g.reg_add(r_tg, r_tg, 16)
                    g.wait_ge(spin_sem, r_tg)
                    g.reg_load(r_sm, pf_sb[0:1, 0:1])
                    g.reg_load(r_ne, pf_sb[0:1, 16:17])
                    g.reg_add(r_sm, r_sm, r_ne)
                    g.reg_alu(r_ne, r_sm, r_2n, ALU.not_equal)
            barrier_tok = em.op("pool", lambda gg: gg.memset(pf2_sb[:], 0))
            em.writer["ktsh"] = barrier_tok
            em.writer["vsh"] = barrier_tok

            em.fence()
            BT3 = bf3(A3, NHC, 1024)
            em.rankdma(g, r_prank,
                       lambda: (BT3[:, :, :],
                                BTd.ap()[0:16 * P, :].rearrange(
                                    "(c p) t -> p c t", p=P)),
                       lambda: (BT3[:, :, :],
                                BTd.ap()[16 * P:32 * P, :].rearrange(
                                    "(c p) t -> p c t", p=P)),
                       reads=["BTd"], writes=["BT"])
            ctxn3 = bf3(A4, NHC, 1024)
            scH = Bump(pool, sc.off, SCE)
            ktt = scH.take(2048)
            vh = [scH.take(16 * 65).rearrange("p (c w) -> p c w", w=65)
                  for _ in range(2)]
            sbe = [scH.take(1024), scH.take(1024)]
            Eb = [scH.take(1024), scH.take(1024)]
            rb = scH.take(1024, f32=True)
            rden = scH.take(2048, f32=True)
            ctmp = scH.take(1024)

            def kth_load(h):
                p0 = (h % 2) * 64
                em.dma("sp", ktt[p0:p0 + 64, :],
                       ktsh.ap()[h * 64:(h + 1) * 64, :],
                       reads=["ktsh"], writes=[f"ktt{h % 2}"])

            def vh_load(h):
                em.dma("sp", vh[h % 2][:, :, :],
                       vsh.ap().rearrange("(c p) (b e) -> p c b e", p=P, e=65)
                       [:, :, h, :], reads=["vsh"], writes=[f"vh{h % 2}"])

            kth_load(0)
            vh_load(0)
            for h in range(HEADS):
                if h + 1 < HEADS:
                    kth_load(h + 1)
                    vh_load(h + 1)
                p0 = (h % 2) * 64
                qh = qT3[:, h // 2, :][64 * (h % 2):64 * (h % 2) + 64, :]
                for kc in range(NHC):
                    i = kc % 2
                    em.mm(ps[i], pk(i),
                          [(ktt[p0:p0 + 64, kc * 128:(kc + 1) * 128], qh,
                            [f"ktt{h % 2}", "qT"])])
                    em.op("dve", lambda v, i=i, kc=kc: v.tensor_tensor(
                        out=sbe[i][:], in0=ps[i][:], in1=BT3[:, kc, :],
                        op=ALU.add), reads=pk(i) + ("BT",), writes=[f"sbe{i}"])
                    em.op("act", lambda s, i=i: s.activation(
                        Eb[i][:], sbe[i][:], AF.Exp),
                        reads=[f"sbe{i}"], writes=[f"Eb{i}"])
                    if kc >= 1:
                        j = (kc - 1) % 2
                        em.mm(ps[2][0:65, :], pk(2),
                              [(vh[h % 2][:, kc - 1, :], Eb[j][:],
                                [f"vh{h % 2}", f"Eb{j}"])],
                              start=(kc == 1), stop=False)
                em.mm(ps[2][0:65, :], pk(2),
                      [(vh[h % 2][:, NHC - 1, :], Eb[(NHC - 1) % 2][:],
                        [f"vh{h % 2}", f"Eb{(NHC - 1) % 2}"])],
                      start=False, stop=True)
                em.op("act", lambda s: s.activation(
                    rden[64:65, 0:1024], ps[2][64:65, :], AF.Ln),
                    reads=pk(2), writes=["rdent"])
                em.op("act", lambda s: s.activation(
                    rden[64:65, 1024:2048], rden[64:65, 0:1024], AF.Exp,
                    scale=-1.0),
                    reads=["rdent"], writes=["rden"])
                em.mm(ps[3][0:64, :], pk(3),
                      [(onesf[64:65, 0:64], rden[64:65, 1024:2048],
                        ["rden", "onesf"])])
                em.op("dve", lambda v: v.tensor_copy(
                    out=rb[0:64, :], in_=ps[3][0:64, :]),
                    reads=pk(3), writes=["rb"])
                if h % 2 == 0:
                    em.op("dve", lambda v, h=h: v.tensor_tensor(
                        out=ctxn3[:, h // 2, :][0:64, :],
                        in0=ps[2][0:64, :], in1=rb[0:64, :], op=ALU.mult),
                        reads=pk(2) + ("rb",), writes=["ctxn"])
                else:
                    em.op("dve", lambda v: v.tensor_tensor(
                        out=ctmp[0:64, :], in0=ps[2][0:64, :],
                        in1=rb[0:64, :], op=ALU.mult),
                        reads=pk(2) + ("rb",), writes=["ctmp"])
                    em.dma("sp", ctxn3[:, h // 2, :][64:128, :], ctmp[0:64, :],
                           reads=["ctmp"], writes=["ctxn"])

            em.fence()
            scO = Bump(pool, scH.base, SCE)
            x1in = [scO.take(1024, f32=True), scO.take(1024, f32=True)]
            xo2 = [scO.take(1024, f32=True), scO.take(1024, f32=True)]
            out2 = outd if stage == "x2" else x2d
            okey = "outd" if stage == "x2" else "x2d"

            def o_consume(mc, buf, bk):
                i = mc % 2
                em.dma("sp", x1in[i][:], x1d.ap()[mc * P:(mc + 1) * P, :],
                       reads=["x1d"], writes=[f"x1in{i}"])
                em.mm(ps[i], pk(i),
                      [(buf[:, kc, :], ctxn3[:, kc, :], [bk, "ctxn"])
                       for kc in range(NHC)])
                em.op("dve", lambda v, i=i: v.tensor_tensor(
                    out=xo2[i][:], in0=ps[i][:], in1=x1in[i][:], op=ALU.add),
                    reads=pk(i) + (f"x1in{i}",), writes=[f"xo2{i}"])
                em.dma("act", out2.ap()[mc * P:(mc + 1) * P, :], xo2[i][:],
                       reads=[f"xo2{i}"], writes=[okey])

            proj_loop(32, "woT", o_consume)

        if stage == "full":
            emit_ffn_call(x2d, "x2d", outd, "outd")

        for sk in (("d", "act"), ("d", "sp"), ("d", "pool")):
            if em.cnt[sk]:
                g.wait_ge(em.sem[sk], em.cnt[sk])
        em._deps("pool", ["nonce"], [])
        with g.If_eq(r_prank, 0):
            g.dma_start(out=flg.ap()[2:3, :], in_=nonce_sb[:]).then_inc(
                spin_sem, 16)
        with g.Else():
            g.dma_start(out=flg.ap()[3:4, :], in_=nonce_sb[:]).then_inc(
                spin_sem, 16)
        g.wait_ge(spin_sem, 16)
        with (g.register("r_sm2") as r_sm, g.register("r_ne2") as r_ne,
              g.register("r_2n2") as r_2n, g.register("r_nn2") as r_nn,
              g.register("r_tg2") as r_tg):
            g.reg_load(r_nn, nonce_sb[0:1, 0:1])
            g.reg_add(r_2n, r_nn, r_nn)
            g.reg_mov(r_ne, 1)
            g.reg_mov(r_tg, 16)
            with g.While(r_ne):
                g.dma_start(out=pf2_sb[:, 0:32],
                            in_=flg.ap()[2:4, :].rearrange("a b -> (a b)")
                            .rearrange("(o ab) -> o ab", o=1)
                            ).then_inc(spin_sem, 16)
                g.reg_add(r_tg, r_tg, 16)
                g.wait_ge(spin_sem, r_tg)
                g.reg_load(r_sm, pf2_sb[0:1, 0:1])
                g.reg_load(r_ne, pf2_sb[0:1, 16:17])
                g.reg_add(r_sm, r_sm, r_ne)
                g.reg_alu(r_ne, r_sm, r_2n, ALU.not_equal)

    return nc




def _prep_inputs(inputs, nonce):
    f = np.float32
    rel = np.asarray(inputs["L0_attn_rel_pos"], dtype=f)
    base = {
        "wqT": np.ascontiguousarray(np.asarray(inputs["L0_attn_wq"]).T, dtype=f),
        "wkT": np.ascontiguousarray(np.asarray(inputs["L0_attn_wk"]).T, dtype=f),
        "wvT": np.ascontiguousarray(np.asarray(inputs["L0_attn_wv"]).T, dtype=f),
        "woT": np.ascontiguousarray(np.asarray(inputs["L0_attn_wo"]).T, dtype=f),
        "wup": np.ascontiguousarray(inputs["L0_ffn_Wup"], dtype=f),
        "wsw": np.ascontiguousarray(inputs["L0_ffn_Wsw"], dtype=f),
        "wdn": np.ascontiguousarray(inputs["L0_ffn_Wdn"], dtype=f),
        "wsu": np.ascontiguousarray(inputs["L0_ffn_Wsu"], dtype=f),
        "wsa": np.ascontiguousarray(inputs["L0_ffn_Wsa"], dtype=f),
        "wsd": np.ascontiguousarray(inputs["L0_ffn_Wsd"], dtype=f),
        "wg": np.ascontiguousarray(
            np.asarray(inputs["L0_ffn_Wg"], dtype=f).reshape(2, 16, 128, 8)
            .transpose(2, 0, 1, 3).reshape(128, 256)),
        "bupr": np.ascontiguousarray(
            np.asarray(inputs["L0_ffn_bup"], dtype=f).reshape(6, 8, 128)
            .transpose(2, 0, 1).reshape(128, 48)),
        "bswr": np.ascontiguousarray(
            np.asarray(inputs["L0_ffn_bsw"], dtype=f).reshape(6, 8, 128)
            .transpose(2, 0, 1).reshape(128, 48)),
        "bsur": np.ascontiguousarray(
            np.asarray(inputs["L0_ffn_bsu"], dtype=f).reshape(16, 128).T),
        "bsar": np.ascontiguousarray(
            np.asarray(inputs["L0_ffn_bsa"], dtype=f).reshape(8, 128).T),
        "bsdr": np.ascontiguousarray(
            np.asarray(inputs["L0_ffn_bsd"], dtype=f).reshape(16, 128).T),
        "bgr": np.ascontiguousarray(np.asarray(inputs["L0_ffn_bg"], dtype=f).T),
        "bdnr": np.ascontiguousarray(inputs["L0_ffn_bdn"], dtype=f),
        "relrev": np.ascontiguousarray(
            np.concatenate([rel[::-1], np.zeros((1, 64), f)], axis=0)),
        "consts": np.ascontiguousarray(np.concatenate(
            [np.eye(128, dtype=f), np.eye(128, dtype=f)[:, ::-1],
             np.ones((128, 1), f)], axis=1)),
        "ind6h": np.ascontiguousarray(np.concatenate(
            [np.tile(np.eye(8, dtype=f)[n][:, None], (1, 128))
             for n in range(6)], axis=1)),
        "onesf": np.ones((128, 128), f),
        "noncein": np.full((1, 16), nonce, np.int32),
    }
    x = np.asarray(inputs["L0_attn_hidden_states"], dtype=f)
    maps = []
    for c in range(8):
        b, half = c // 2, c % 2
        m = dict(base)
        m["xT"] = np.ascontiguousarray(x[b, half * TC:(half + 1) * TC, :].T)
        maps.append(m)
    return maps


def kernel(stage="full", **inputs):
    nonce = (int.from_bytes(os.urandom(4), "little") | 1) & 0x7FFFFFFF
    maps = _prep_inputs(inputs, nonce)
    nc = build(stage=stage)
    res = run_bass_kernel_spmd(nc, maps, core_ids=list(range(8)))
    out = np.empty((4, 2048, 2048), np.float32)
    for c in range(8):
        b, half = c // 2, c % 2
        out[b, half * TC:(half + 1) * TC, :] = np.asarray(
            res.results[c]["out"]).T
    return out


# version 03
# speedup vs baseline: 1.5815x; 1.1650x over previous
import os
import dataclasses
import numpy as np
import concourse.bass as bass
import concourse.mybir as mybir
from concourse.bass_utils import run_bass_kernel_spmd
from contextlib import ExitStack

F32 = mybir.dt.float32
BF16 = mybir.dt.bfloat16
I32 = mybir.dt.int32
AF = mybir.ActivationFunctionType
ALU = mybir.AluOpType
AXX = mybir.AxisListType.X

P = 128
H = 2048
NHC = 16
TC = 1024
E = 1024
NEC = 8
NEXP = 6
HEADS = 32

PKEY = ("p", "pe")


class Em:

    def __init__(self, nc, ctx):
        self.nc = nc
        self.eng = {"pe": nc.tensor, "act": nc.scalar, "dve": nc.vector,
                    "sp": nc.sync, "pool": nc.gpsimd}
        self.sem = {}
        for e in ("pe", "act", "dve", "pool"):
            self.sem[("p", e)] = ctx.enter_context(nc.semaphore("prog_" + e))
        for q in ("sp", "act", "pool"):
            self.sem[("d", q)] = ctx.enter_context(nc.semaphore("dma_" + q))
        self.cnt = {k: 0 for k in self.sem}
        self.hw = {}
        self.writer = {}
        self.readers = {}

    def _wait(self, eng, tok):
        if tok is None:
            return
        sk, v = tok
        if sk == PKEY and eng == "pe":
            return
        if v <= self.hw.get((eng, sk), 0):
            return
        self.eng[eng].wait_ge(self.sem[sk], v)
        self.hw[(eng, sk)] = v

    def _deps(self, eng, reads, writes):
        for k in reads:
            self._wait(eng, self.writer.get(k))
        for k in writes:
            self._wait(eng, self.writer.get(k))
            for sk, v in self.readers.get(k, {}).items():
                self._wait(eng, (sk, v))

    def _record(self, tok, reads, writes):
        for k in writes:
            self.writer[k] = tok
            self.readers[k] = {}
        if tok is not None:
            sk, v = tok
            for k in reads:
                r = self.readers.setdefault(k, {})
                r[sk] = max(r.get(sk, 0), v)

    def op(self, eng, fn, reads=(), writes=()):
        self._deps(eng, reads, writes)
        inst = fn(self.eng[eng])
        sk = ("p", eng)
        self.cnt[sk] += 1
        inst.then_inc(self.sem[sk], 1)
        tok = (sk, self.cnt[sk])
        self._record(tok, reads, writes)
        return tok

    def dma(self, q, out, in_, reads=(), writes=(), slow=False):
        self._deps(q, reads, writes)
        inst = self.eng[q].dma_start(out=out, in_=in_,
                                     allow_slow_non_contiguous=slow)
        sk = ("d", q)
        self.cnt[sk] += 16
        inst.then_inc(self.sem[sk], 16)
        tok = (sk, self.cnt[sk])
        self._record(tok, reads, writes)
        return tok

    def rankdma(self, g, r_prank, mk0, mk1, reads=(), writes=()):
        self._deps("pool", reads, writes)
        sk = ("d", "pool")
        with g.If_eq(r_prank, 0):
            o0, i0 = mk0()
            g.dma_start(out=o0, in_=i0).then_inc(self.sem[sk], 16)
        with g.Else():
            o1, i1 = mk1()
            g.dma_start(out=o1, in_=i1).then_inc(self.sem[sk], 16)
        self.cnt[sk] += 16
        tok = (sk, self.cnt[sk])
        self._record(tok, reads, writes)
        return tok

    def mm(self, pap, pkeys, kitems, start=True, stop=True):
        W = pap.shape[-1]
        halves = [(0, W)] if W <= 512 else [(a, a + 512) for a in range(0, W, 512)]
        all_reads = set()
        n = len(kitems)
        for j, (la, ra, reads) in enumerate(kitems):
            first, last = j == 0, j == n - 1
            self._deps("pe", reads, pkeys if (first and start) else ())
            all_reads |= set(reads)
            for hi, (a, b) in enumerate(halves):
                rv = ra if len(halves) == 1 else ra[:, a:b]
                pv = pap if len(halves) == 1 else pap[:, a:b]
                inst = self.eng["pe"].matmul(
                    pv, la, rv, start=(first and start),
                    stop=(last and stop))
                if last and hi == len(halves) - 1:
                    self.cnt[PKEY] += 1
                    inst.then_inc(self.sem[PKEY], 1)
                    tok = (PKEY, self.cnt[PKEY])
        if start:
            self._record(tok, sorted(all_reads), pkeys)
        else:
            for k in pkeys:
                self.writer[k] = tok
            self._record(tok, sorted(all_reads), ())
        return tok

    def fence(self):
        for eng in ("pe", "act", "dve", "sp", "pool"):
            for sk, c in self.cnt.items():
                if c == 0:
                    continue
                if sk == ("p", eng):
                    continue
                self._wait(eng, (sk, c))


class Bump:
    def __init__(self, pool, base, limit):
        self.pool, self.base, self.limit, self.off = pool, base, limit, base

    def take(self, n, f32=False):
        elems = 2 * n if f32 else n
        pad = (elems + 63) & ~63
        a = self.off
        assert a + pad <= self.limit, f"scratch overflow {a}+{pad}>{self.limit}"
        self.off = a + pad
        v = self.pool[:, a:a + elems]
        return v.bitcast(F32) if f32 else v


def build(stage="full"):
    nc = bass.Bass("TRN2", target_bir_lowering=False)

    din = {}
    for name, shape, dt in [
        ("xT", [H, TC], F32),
        ("wqT", [H, H], F32), ("wkT", [H, H], F32),
        ("wvT", [H, H], F32), ("woT", [H, H], F32),
        ("wg", [P, 256], F32),
        ("wup", [NEXP, H, E], F32), ("wsw", [NEXP, E, E], F32),
        ("wdn", [NEXP, E, H], F32),
        ("wsu", [H, 2 * E], F32), ("wsa", [E, E], F32), ("wsd", [2 * E, H], F32),
        ("bupr", [P, 48], F32), ("bswr", [P, 48], F32), ("bsur", [P, 16], F32),
        ("bsar", [P, 8], F32), ("bsdr", [P, 16], F32), ("bgr", [8, 2], F32),
        ("bdnr", [NEXP, H], F32),
        ("relrev", [1024, 64], F32),
        ("consts", [P, 257], F32),
        ("ind6h", [8, NEXP * P], F32),
        ("onesf", [P, P], F32),
        ("noncein", [1, 16], I32),
    ]:
        din[name] = nc.dram_tensor(name, shape, dt, kind="ExternalInput")

    outd = nc.dram_tensor("out", [H, TC], F32, kind="ExternalOutput")

    x1d = nc.dram_tensor("x1d", [H, TC], F32, kind="Internal")
    x2d = nc.dram_tensor("x2d", [H, TC], F32, kind="Internal")
    hd = nc.dram_tensor("hd", [NEXP, E, TC], BF16, kind="Internal")
    Pd = nc.dram_tensor("Pd", [4096], F32, kind="Internal")
    mbrd = nc.dram_tensor("mbrd", [1024], F32, kind="Internal")
    BTd = nc.dram_tensor("BTd", [2 * 2 * TC, TC], BF16, kind="Internal")
    ktsh = nc.dram_tensor("ktsh", [H, 2 * TC], BF16, kind="Internal",
                          addr_space="Shared")
    vsh = nc.dram_tensor("vsh", [2 * TC, HEADS * 65], BF16, kind="Internal",
                         addr_space="Shared")
    flg = nc.dram_tensor("flg", [4, 16], I32, kind="Internal",
                         addr_space="Shared")

    with ExitStack() as ctx:
        em = Em(nc, ctx)
        spin_sem = ctx.enter_context(nc.semaphore("spin"))

        POOL_E = 92 * 1024
        pool = ctx.enter_context(nc.sbuf_tensor("pool", [P, POOL_E], BF16))
        A1, A2, A3, A4, SC = 0, 16384, 32768, 49152, 65536
        SCE = POOL_E

        bcb = ctx.enter_context(nc.sbuf_tensor("bcb", [P, 2048], F32))
        consb = ctx.enter_context(nc.sbuf_tensor("consb", [P, 257], BF16))
        onesf = ctx.enter_context(nc.sbuf_tensor("onesfs", [P, P], F32))
        biasf = ctx.enter_context(nc.sbuf_tensor("biasf", [P, 136], F32))
        bg_sb = ctx.enter_context(nc.sbuf_tensor("bg_sb", [8, 2], F32))
        bdn_sb = ctx.enter_context(nc.sbuf_tensor("bdn_sb", [NEXP, H], BF16))
        wg_sb = ctx.enter_context(nc.sbuf_tensor("wg_sb", [P, 256], BF16))
        ind6 = ctx.enter_context(nc.sbuf_tensor("ind6", [8, NEXP * P], BF16))
        gsT = bcb[:, 0:512].bitcast(BF16)[0:8, :]
        wT_sb = bcb[:, 512:1024].bitcast(BF16)[0:8, :]
        nonce_sb = ctx.enter_context(nc.sbuf_tensor("nonce_sb", [1, 16], I32))
        epsc = ctx.enter_context(nc.sbuf_tensor("epsc", [1, 8], F32))
        pf_sb = ctx.enter_context(nc.sbuf_tensor("pf_sb", [1, 32], I32))
        pf2_sb = ctx.enter_context(nc.sbuf_tensor("pf2_sb", [1, 32], I32))

        ps = [ctx.enter_context(nc.psum_tensor(f"ps{i}", [P, 1024], F32))
              for i in range(4)]

        block = ctx.enter_context(nc.Block())

        def pk(i):
            return (f"ps{i}a", f"ps{i}b")

        ID = consb[:, 0:128]
        JM = consb[:, 128:256]
        ONE1 = consb[:, 256:257]

        def bf3(base, nch, w):
            return pool[:, base:base + nch * w].rearrange(
                "p (c w) -> p c w", w=w)

        xln = bf3(A1, NHC, 1024)
        KLN = "xlnA1"

        em.dma("pool", consb[:], din["consts"].ap(), writes=["consb"])
        em.dma("sp", onesf[:], din["onesf"].ap(), writes=["onesf"])
        em.dma("sp", biasf[:, 0:48], din["bupr"].ap(), writes=["biasf"])
        em.dma("sp", biasf[:, 48:96], din["bswr"].ap(), writes=["biasf"])
        em.dma("sp", biasf[:, 96:112], din["bsur"].ap(), writes=["biasf"])
        em.dma("sp", biasf[:, 112:120], din["bsar"].ap(), writes=["biasf"])
        em.dma("sp", biasf[:, 120:136], din["bsdr"].ap(), writes=["biasf"])
        em.dma("sp", bg_sb[:], din["bgr"].ap(), writes=["bg"])
        em.dma("pool", bdn_sb[:], din["bdnr"].ap(), writes=["bdn"])
        em.dma("pool", wg_sb[:], din["wg"].ap(), writes=["wg"])
        em.dma("pool", ind6[:], din["ind6h"].ap(), writes=["ind6"])
        em.dma("sp", nonce_sb[:], din["noncein"].ap(), writes=["nonce"])
        em.op("dve", lambda v: v.memset(epsc[:], 1e-5), writes=["epsc"])

        g = em.eng["pool"]
        pid = g.partition_id()
        r_prank = ctx.enter_context(g.register("r_prank"))
        g.reg_alu(r_prank, pid, 1, ALU.bitwise_and)


        def emit_btbuild():
            scp = Bump(pool, SC + 22528, SCE)
            wwin = [scp.take(1024), scp.take(1024)]
            btst = [scp.take(1024), scp.take(1024)]
            relsb = scp.take(512, f32=True)
            mbrf = scp.take(8, f32=True)
            em.dma("sp", relsb.rearrange("p (c d) -> p c d", d=64),
                   din["relrev"].ap().rearrange("(c p) d -> p c d", p=P),
                   writes=["relsb"])
            for c in range(8):
                em.op("dve", lambda v, c=c: v.reduce_sum(
                    out=mbrf[:, c:c + 1], in_=relsb[:, c * 64:(c + 1) * 64],
                    axis=AXX), reads=["relsb"], writes=["mbr"])
            em.op("dve", lambda v: v.tensor_scalar(
                out=mbrf[:], in0=mbrf[:], scalar1=1.0 / 64, scalar2=None,
                op0=ALU.mult), reads=["mbr"], writes=["mbr"])
            mbr_dst = dataclasses.replace(mbrd.ap(), ap=[[1, P], [P, 8]], offset=0)
            em.dma("sp", mbr_dst, mbrf[:], reads=["mbr"], writes=["mbrd"], slow=True)
            em.dma("sp", dataclasses.replace(Pd.ap(), ap=[[1, 1023]], offset=1536),
                   dataclasses.replace(mbrd.ap(), ap=[[1, 1023]], offset=0),
                   reads=["mbrd"], writes=["Pd"])
            em.dma("sp", dataclasses.replace(Pd.ap(), ap=[[1, 1536], [1, 1]], offset=0),
                   dataclasses.replace(mbrd.ap(), ap=[[0, 1536], [1, 1]], offset=0),
                   reads=["mbrd"], writes=["Pd"])
            em.dma("sp",
                   dataclasses.replace(Pd.ap(), ap=[[1, 4096 - 2559], [1, 1]], offset=2559),
                   dataclasses.replace(mbrd.ap(), ap=[[0, 4096 - 2559], [1, 1]], offset=1022),
                   reads=["mbrd"], writes=["Pd"])

            for sel in range(2):
                for kc in range(16):
                    par = kc % 2
                    src = dataclasses.replace(
                        Pd.ap(), ap=[[1, P], [1, 1024]],
                        offset=1920 + sel * TC - 128 * kc)
                    em.dma("pool", wwin[par][:], src, reads=["Pd"],
                           writes=[f"wwin{par}"])
                    em.mm(ps[3], pk(3), [(JM, wwin[par][:], [f"wwin{par}", "consb"])])
                    em.op("act", lambda s, par=par: s.copy(btst[par][:], ps[3][:]),
                          reads=pk(3), writes=[f"btst{par}"])
                    em.dma("pool",
                           BTd.ap()[(sel * 16 + kc) * P:(sel * 16 + kc + 1) * P, :],
                           btst[par][:], reads=[f"btst{par}"], writes=["BTd"])


        def emit_ln(src_dram, src_key):
            em.fence()
            xh = pool[:, A2:A2 + 2 * NHC * 1024].bitcast(F32).rearrange(
                "p (c w) -> p c w", w=1024)
            em.dma("sp", xh[:, :, :],
                   src_dram.ap().rearrange("(c p) t -> p c t", p=P),
                   reads=[src_key], writes=["xh"])
            s2 = Bump(pool, SC, SCE)
            xb = [s2.take(1024), s2.take(1024)]
            xq = [s2.take(1024), s2.take(1024)]
            lnr_mean = s2.take(1024, f32=True)
            lnr_var = s2.take(1024, f32=True)
            lnr_tmp = s2.take(1024, f32=True)
            lnr_rstd = s2.take(1024, f32=True)
            lnr_mrs = s2.take(1024, f32=True)
            for hc in range(NHC):
                par = hc % 2
                em.op("act", lambda s, hc=hc, par=par: s.copy(
                    xb[par][:], xh[:, hc, :]), reads=["xh"], writes=[f"xb{par}"])
                em.op("act", lambda s, hc=hc, par=par: s.activation(
                    xq[par][:], xh[:, hc, :], AF.Square),
                    reads=["xh"], writes=[f"xq{par}"])
                em.mm(ps[2][0:1, :], pk(2),
                      [(ONE1, xb[par][:], [f"xb{par}", "consb"])],
                      start=(hc == 0), stop=(hc == NHC - 1))
                em.mm(ps[3][0:1, :], pk(3),
                      [(ONE1, xq[par][:], [f"xq{par}", "consb"])],
                      start=(hc == 0), stop=(hc == NHC - 1))
            em.op("act", lambda s: s.mul(lnr_mean[0:1, :], ps[2][0:1, :],
                                          1.0 / H),
                  reads=pk(2), writes=["lnmean"])
            em.op("act", lambda s: s.mul(lnr_var[0:1, :], ps[3][0:1, :],
                                         1.0 / H),
                  reads=pk(3), writes=["lnvar"])
            em.op("act", lambda s: s.square(lnr_tmp[0:1, :], lnr_mean[0:1, :]),
                  reads=["lnmean"], writes=["lntmp"])
            em.op("dve", lambda v: v.tensor_tensor(
                out=lnr_var[0:1, :], in0=lnr_var[0:1, :],
                in1=lnr_tmp[0:1, :], op=ALU.subtract),
                reads=["lnvar", "lntmp"], writes=["lnvar"])
            em.op("act", lambda s: s.activation(
                lnr_tmp[0:1, :], lnr_var[0:1, :], AF.Ln,
                bias=epsc[0:1, 0:1]),
                reads=["lnvar", "epsc"], writes=["lntmp"])
            em.op("act", lambda s: s.activation(
                lnr_rstd[0:1, :], lnr_tmp[0:1, :], AF.Exp, scale=-0.5),
                reads=["lntmp"], writes=["rstd"])
            em.op("dve", lambda v: v.tensor_tensor(
                out=lnr_mrs[0:1, :], in0=lnr_mean[0:1, :],
                in1=lnr_rstd[0:1, :], op=ALU.mult),
                reads=["lnmean", "rstd"], writes=["mrs"])
            em.mm(ps[2], pk(2), [(onesf[0:1, :], lnr_rstd[0:1, :], ["rstd", "onesf"])])
            em.op("act", lambda s: s.copy(bcb[:, 0:1024], ps[2][:]),
                  reads=pk(2), writes=["bcb0"])
            em.mm(ps[3], pk(3), [(onesf[0:1, :], lnr_mrs[0:1, :], ["mrs", "onesf"])])
            em.op("act", lambda s: s.copy(bcb[:, 1024:2048], ps[3][:]),
                  reads=pk(3), writes=["bcb1"])
            t32 = [s2.take(1024, f32=True), s2.take(1024, f32=True)]
            for hc in range(NHC):
                par = hc % 2
                em.op("dve", lambda v, hc=hc, par=par: v.tensor_tensor(
                    out=t32[par][:], in0=xh[:, hc, :], in1=bcb[:, 0:1024],
                    op=ALU.mult), reads=["xh", "bcb0"], writes=[f"t32{par}"])
                em.op("dve", lambda v, hc=hc, par=par: v.tensor_tensor(
                    out=xln[:, hc, :], in0=t32[par][:], in1=bcb[:, 1024:2048],
                    op=ALU.subtract), reads=[f"t32{par}", "bcb1"], writes=[KLN])

        class WS:
            def __init__(self, name, scb, nch, w, nbuf=4):
                self.name, self.nbuf = name, nbuf
                self.bufs = [scb.take(nch * w).rearrange("p (c w) -> p c w", w=w)
                             for _ in range(nbuf)]

            def key(self, i):
                return f"{self.name}{i % self.nbuf}"

            def load(self, i, src_ap):
                em.dma("pool", self.bufs[i % self.nbuf][:, :, :], src_ap,
                       writes=[self.key(i)])

            def get(self, i):
                return self.bufs[i % self.nbuf], self.key(i)

        def colslice(t_ap, c0, c1, p=P):
            return t_ap[:, c0:c1].rearrange("(c p) o -> p c o", p=p)

        def emit_ffn(j, xt3, xt_key, out_mode, base_dram, base_key,
                     out_dram, out_key, also_out=None):
            em.fence()
            sc = Bump(pool, SC, SCE)
            gtm = sc.take(64, f32=True)
            gsc = sc.take(40, f32=True)
            wtmb = sc.take(64)
            em.mm(ps[2][0:8, :], pk(2),
                  [(wg_sb[:, j * 128 + kc * 8: j * 128 + kc * 8 + 8],
                    xt3[:, kc, :], ["wg", xt_key]) for kc in range(NHC)])
            em.op("act", lambda s: s.activation(
                gsT[:], ps[2][0:8, :], AF.Identity, bias=bg_sb[:, j:j + 1]),
                reads=pk(2) + ("bg",), writes=["gsT"])
            for tb in range(8):
                bank = 0 if tb < 4 else 1
                em.mm(ps[3][:, tb * 128: tb * 128 + 8], (pk(3)[bank],),
                      [(gsT[:, tb * 128:(tb + 1) * 128], ID[0:8, 0:8],
                        ["gsT", "consb"])],
                      start=(tb % 4 == 0), stop=(tb % 4 == 3))
            for tb in range(8):
                em.op("act", lambda s, tb=tb: s.copy(
                    gtm[:, tb * 8:(tb + 1) * 8],
                    ps[3][:, tb * 128:tb * 128 + 8]),
                    reads=pk(3), writes=["gtm"])
            srt, msk, e8, nmx, den = (gsc[:, 0:8], gsc[:, 8:16], gsc[:, 16:24],
                                      gsc[:, 24:25], gsc[:, 25:26])
            for tb in range(8):
                gt = gtm[:, tb * 8:(tb + 1) * 8]
                em.op("dve", lambda v, gt=gt: v.max(out=srt, in_=gt),
                      reads=["gtm"], writes=["srt"])
                em.op("dve", lambda v, gt=gt: v.tensor_scalar(
                    out=msk, in0=gt, scalar1=srt[:, 5:6], scalar2=None,
                    op0=ALU.is_ge), reads=["gtm", "srt"], writes=["msk"])
                em.op("dve", lambda v: v.tensor_scalar(
                    out=nmx, in0=srt[:, 0:1], scalar1=-1.0, scalar2=None,
                    op0=ALU.mult), reads=["srt"], writes=["nmx"])
                em.op("act", lambda s, gt=gt: s.activation(
                    e8, gt, AF.Exp, bias=nmx[:, 0:1]),
                    reads=["gtm", "nmx"], writes=["e8"])
                em.op("dve", lambda v: v.tensor_tensor(
                    out=e8, in0=e8, in1=msk, op=ALU.mult),
                    reads=["e8", "msk"], writes=["e8"])
                em.op("dve", lambda v: v.reduce_sum(out=den, in_=e8, axis=AXX),
                      reads=["e8"], writes=["den"])
                em.op("dve", lambda v: v.reciprocal(out=den, in_=den),
                      reads=["den"], writes=["den"])
                em.op("dve", lambda v, tb=tb: v.tensor_scalar(
                    out=wtmb[:, tb * 8:(tb + 1) * 8], in0=e8,
                    scalar1=den[:, 0:1], scalar2=None, op0=ALU.mult),
                    reads=["e8", "den"], writes=["wtmb"])
            for tb in range(8):
                bank = 0 if tb < 4 else 1
                em.mm(ps[2][0:8, tb * 128:(tb + 1) * 128], (pk(2)[bank],),
                      [(wtmb[:, tb * 8:(tb + 1) * 8], ID, ["wtmb", "consb"])],
                      start=(tb % 4 == 0), stop=(tb % 4 == 3))
            em.op("act", lambda s: s.copy(wT_sb[:], ps[2][0:8, :]),
                  reads=pk(2), writes=["wT"])
            wbc = sc.take(NEXP * 1024).rearrange("p (n w) -> p n w", w=1024)
            for n in range(NEXP):
                i = n % 2
                em.mm(ps[i], pk(i),
                      [(ind6[:, n * 128:(n + 1) * 128], wT_sb[:],
                        ["ind6", "wT"])])
                em.op("act", lambda s, n=n, i=i: s.copy(wbc[:, n, :], ps[i][:]),
                      reads=pk(i), writes=["wbc"])

            tq = [sc.take(1024), sc.take(1024)]
            scbase = sc.off
            scS = Bump(pool, scbase, SCE)
            sh3 = bf3(A4, NHC, 1024)
            su = WS("su", scS, NHC, 128)
            for uc in range(3):
                su.load(uc, colslice(din["wsu"].ap(), uc * 128, (uc + 1) * 128))
            for uc in range(NHC):
                if uc + 3 < NHC:
                    su.load(uc + 3, colslice(din["wsu"].ap(), (uc + 3) * 128,
                                             (uc + 4) * 128))
                buf, bk = su.get(uc)
                i = uc % 2
                em.mm(ps[i], pk(i),
                      [(buf[:, kc, :], xt3[:, kc, :], [bk, xt_key])
                       for kc in range(NHC)])
                em.op("act", lambda s, uc=uc, i=i: s.activation(
                    sh3[:, uc, :], ps[i][:], AF.Identity,
                    bias=biasf[:, 96 + uc:97 + uc]),
                    reads=pk(i) + ("biasf",), writes=["sh"])
            outc3 = bf3(A3, NHC, 1024)
            sa = WS("sa", scS, NEC, 128)
            for q_ in range(2):
                sa.load(q_, colslice(din["wsa"].ap(), q_ * 128, (q_ + 1) * 128))
            for s_ in range(2):
                for fc in range(NEC):
                    gidx = s_ * NEC + fc
                    if gidx + 2 < 16:
                        f2 = (gidx + 2) % NEC
                        sa.load(gidx + 2, colslice(din["wsa"].ap(), f2 * 128,
                                                   (f2 + 1) * 128))
                    buf, bk = sa.get(gidx)
                    i = gidx % 2
                    em.mm(ps[i], pk(i),
                          [(buf[:, kc, :], sh3[:, s_ * NEC + kc, :], [bk, "sh"])
                           for kc in range(NEC)])
                    em.op("act", lambda s, fc=fc, i=i: s.activation(
                        tq[i][:], ps[i][:], AF.Silu,
                        bias=biasf[:, 112 + fc:113 + fc]),
                        reads=pk(i) + ("biasf",), writes=[f"tq{i}"])
                    em.op("dve", lambda v, s_=s_, fc=fc, i=i: v.tensor_tensor(
                        out=outc3[:, s_ * NEC + fc, :], in0=tq[i][:],
                        in1=sh3[:, s_ * NEC + fc, :], op=ALU.mult),
                        reads=[f"tq{i}", "sh"], writes=["outc"])

            em.fence()
            h13 = bf3(A4, NEC, 1024)
            h1w3 = bf3(A4 + NEC * 1024, NEC, 1024)
            scE = Bump(pool, scbase, SCE)
            up = WS("up", scE, NHC, 128, nbuf=3)
            sw = WS("sw", scE, NEC, 128)
            hst = scE.take(NEC * 1024).rearrange("p (c w) -> p c w", w=1024)

            def up_src(gi):
                n, ec = divmod(gi, NEC)
                return colslice(din["wup"].ap()[n], ec * 128, (ec + 1) * 128)

            def sw_src(gi):
                n, fc = divmod(gi, NEC)
                return colslice(din["wsw"].ap()[n], fc * 128, (fc + 1) * 128)

            for i0 in range(2):
                up.load(i0, up_src(i0))
                sw.load(i0, sw_src(i0))
            sw.load(2, sw_src(2))
            for n in range(NEXP):
                for ec in range(NEC):
                    gi = n * NEC + ec
                    if gi + 2 < NEXP * NEC:
                        up.load(gi + 2, up_src(gi + 2))
                    buf, bk = up.get(gi)
                    i = ec % 2
                    em.mm(ps[i], pk(i),
                          [(buf[:, kc, :], xt3[:, kc, :], [bk, xt_key])
                           for kc in range(NHC)])
                    em.op("act", lambda s, n=n, ec=ec, i=i: s.activation(
                        h13[:, ec, :], ps[i][:], AF.Identity,
                        bias=biasf[:, n * 8 + ec:n * 8 + ec + 1]),
                        reads=pk(i) + ("biasf",), writes=["h1"])
                    em.op("dve", lambda v, n=n, ec=ec: v.tensor_tensor(
                        out=h1w3[:, ec, :], in0=h13[:, ec, :], in1=wbc[:, n, :],
                        op=ALU.mult), reads=["h1", "wbc"], writes=["h1w"])
                for fc in range(NEC):
                    gi = n * NEC + fc
                    if gi + 3 < NEXP * NEC:
                        sw.load(gi + 3, sw_src(gi + 3))

                    buf, bk = sw.get(gi)
                    i = fc % 2
                    em.mm(ps[i], pk(i),
                          [(buf[:, kc, :], h13[:, kc, :], [bk, "h1"])
                           for kc in range(NEC)])
                    em.op("act", lambda s, n=n, fc=fc, i=i: s.activation(
                        tq[i][:], ps[i][:], AF.Silu,
                        bias=biasf[:, 48 + n * 8 + fc:49 + n * 8 + fc]),
                        reads=pk(i) + ("biasf",), writes=[f"tq{i}"])
                    em.op("dve", lambda v, fc=fc, i=i: v.tensor_tensor(
                        out=hst[:, fc, :], in0=tq[i][:], in1=h1w3[:, fc, :],
                        op=ALU.mult), reads=[f"tq{i}", "h1w"], writes=["hst"])
                em.dma("sp", hd.ap()[n].rearrange("(c p) t -> p c t", p=P),
                       hst[:, :, :], reads=["hst"], writes=[f"hd{n}"])

            em.fence()
            sc2 = Bump(pool, SC, SCE)
            dnb = [sc2.take(NEC * 512).rearrange("p (c w) -> p c w", w=512)
                   for _ in range(2)]
            sdb = sc2.take(NHC * 512).rearrange("p (c w) -> p c w", w=512)
            hrv = [bf3(A4, NEC, 1024), bf3(A4 + NEC * 1024, NEC, 1024)]
            if out_mode == "resid":
                scA2 = Bump(pool, A2, A3)
                xin = scA2.take(4 * 1024, f32=True).rearrange(
                    "p (c w) -> p c w", w=1024)
                xo = [scA2.take(1024, f32=True), scA2.take(1024, f32=True)]
                u32 = scA2.take(1024, f32=True)
            xt1_3 = bf3(A2, NHC, 1024)
            for gidx in range(4):
                hc0 = gidx * 4
                em.dma("pool", dnb[0][:, :, :],
                       colslice(din["wdn"].ap()[0], gidx * 512,
                                (gidx + 1) * 512), writes=["dnb0"])
                em.dma("pool", sdb[:, :, :],
                       colslice(din["wsd"].ap(), gidx * 512, (gidx + 1) * 512),
                       writes=["sdb"])
                em.dma("sp", hrv[0][:, :, :],
                       hd.ap()[0].rearrange("(c p) t -> p c t", p=P),
                       reads=["hd0"], writes=["hr0"])
                if out_mode == "resid":
                    em.dma("sp", xin[:, :, :],
                           base_dram.ap()[hc0 * P:(hc0 + 4) * P, :].rearrange(
                               "(c p) t -> p c t", p=P),
                           reads=[base_key], writes=["xin"])
                for i in range(4):
                    hc = hc0 + i
                    em.mm(ps[i], pk(i),
                          [(bdn_sb[:, hc * 128:(hc + 1) * 128], wT_sb[0:6, :],
                            ["bdn", "wT"])], start=True, stop=False)
                for n in range(NEXP):
                    if n + 1 < NEXP:
                        em.dma("pool", dnb[(n + 1) % 2][:, :, :],
                               colslice(din["wdn"].ap()[n + 1], gidx * 512,
                                        (gidx + 1) * 512),
                               writes=[f"dnb{(n + 1) % 2}"])
                        em.dma("sp", hrv[(n + 1) % 2][:, :, :],
                               hd.ap()[n + 1].rearrange("(c p) t -> p c t", p=P),
                               reads=[f"hd{n + 1}"],
                               writes=[f"hr{(n + 1) % 2}"])
                    for i in range(4):
                        em.mm(ps[i], pk(i),
                              [(dnb[n % 2][:, ec, i * 128:(i + 1) * 128],
                                hrv[n % 2][:, ec, :],
                                [f"dnb{n % 2}", f"hr{n % 2}"])
                               for ec in range(NEC)],
                              start=False, stop=False)
                for i in range(4):
                    em.mm(ps[i], pk(i),
                          [(sdb[:, kc, i * 128:(i + 1) * 128],
                            outc3[:, kc, :], ["sdb", "outc"])
                           for kc in range(NHC)],
                          start=False, stop=True)
                for i in range(4):
                    hc = hc0 + i
                    bsd_c = biasf[:, 120 + hc:121 + hc]
                    if out_mode == "xt1":
                        em.op("dve", lambda v, i=i, hc=hc, bsd_c=bsd_c:
                              v.scalar_tensor_tensor(
                                  out=xt1_3[:, hc, :], in0=ps[i][:],
                                  scalar=bsd_c, in1=xln[:, hc, :],
                                  op0=ALU.add, op1=ALU.add),
                              reads=pk(i) + ("biasf", KLN), writes=["xt1"])
                    else:
                        em.op("dve", lambda v, i=i, hc=hc, bsd_c=bsd_c:
                              v.scalar_tensor_tensor(
                                  out=u32[:], in0=ps[i][:], scalar=bsd_c,
                                  in1=xln[:, hc, :], op0=ALU.add, op1=ALU.add),
                              reads=pk(i) + ("biasf", KLN), writes=["u32"])
                        par = i % 2
                        em.op("dve", lambda v, i=i, par=par:
                              v.scalar_tensor_tensor(
                                  out=xo[par][:], in0=u32[:], scalar=0.5,
                                  in1=xin[:, i, :], op0=ALU.mult, op1=ALU.add),
                              reads=["u32", "xin"], writes=[f"xo{par}"])
                        em.dma("act", out_dram.ap()[hc * P:(hc + 1) * P, :],
                               xo[par][:], reads=[f"xo{par}"], writes=[out_key])
                        if also_out is not None:
                            em.dma("act",
                                   also_out.ap()[hc * P:(hc + 1) * P, :],
                                   xo[par][:], reads=[f"xo{par}"],
                                   writes=["alsoout"])

        def emit_ffn_call_post_ln(out_dram, out_key, also_out=None,
                                  xsrc_dram=None, xsrc_key="xT"):
            if xsrc_dram is None:
                xsrc_dram = din["xT"]
            xt1_3 = bf3(A2, NHC, 1024)
            emit_ffn(0, xln, KLN, "xt1", None, None, None, None)
            emit_ffn(1, xt1_3, "xt1", "resid", xsrc_dram, xsrc_key,
                     out_dram, out_key, also_out=also_out)

        def emit_ffn_call(xsrc_dram, xsrc_key, out_dram, out_key,
                          also_out=None):
            emit_ln(xsrc_dram, xsrc_key)
            emit_ffn_call_post_ln(out_dram, out_key, also_out=also_out,
                                  xsrc_dram=xsrc_dram, xsrc_key=xsrc_key)

        if stage == "xln1":
            emit_ln(din["xT"], "xT")
            emit_btbuild()
            s3 = Bump(pool, A3, SC)
            st = [s3.take(1024, f32=True), s3.take(1024, f32=True)]
            for hc in range(NHC):
                par = hc % 2
                em.op("act", lambda s, hc=hc, par=par: s.copy(
                    st[par][:], xln[:, hc, :]), reads=[KLN],
                    writes=[f"st{par}"])
                em.dma("act", outd.ap()[hc * P:(hc + 1) * P, :], st[par][:],
                       reads=[f"st{par}"], writes=["outd"])
        else:
            emit_ln(din["xT"], "xT")
            emit_btbuild()
            emit_ffn_call_post_ln(x1d, "x1d",
                                  also_out=(outd if stage == "x1" else None))

        if stage in ("x2", "full"):
            emit_ln(x1d, "x1d")
            em.fence()
            sc = Bump(pool, SC, SCE)
            qT3 = bf3(A2, NHC, 1024)
            wsl = WS("wsl", sc, NHC, 128)
            kst = [sc.take(1024), sc.take(1024)]
            vst = [sc.take(16 * 65), sc.take(16 * 65)]

            def proj_src(wname, mc):
                return colslice(din[wname].ap(), mc * 128, (mc + 1) * 128)

            def proj_loop(idx0, wname, consume):
                for mc in range(3):
                    wsl.load(idx0 + mc, proj_src(wname, mc))
                for mc in range(NHC):
                    if mc + 3 < NHC:
                        wsl.load(idx0 + mc + 3, proj_src(wname, mc + 3))
                    buf, bk = wsl.get(idx0 + mc)
                    consume(mc, buf, bk)

            def q_consume(mc, buf, bk):
                i = mc % 2
                em.mm(ps[i], pk(i),
                      [(buf[:, kc, :], xln[:, kc, :], [bk, KLN])
                       for kc in range(NHC)])
                em.op("act", lambda s, mc=mc, i=i: s.activation(
                    qT3[:, mc, :], ps[i][:], AF.Identity, scale=0.125),
                    reads=pk(i), writes=["qT"])

            def k_consume(mc, buf, bk):
                i = mc % 2
                em.mm(ps[i], pk(i),
                      [(buf[:, kc, :], xln[:, kc, :], [bk, KLN])
                       for kc in range(NHC)])
                em.op("act", lambda s, i=i: s.copy(kst[i][:], ps[i][:]),
                      reads=pk(i), writes=[f"kst{i}"])
                em.rankdma(
                    g, r_prank,
                    lambda mc=mc, i=i: (
                        ktsh.ap()[mc * P:(mc + 1) * P, 0:TC], kst[i][:]),
                    lambda mc=mc, i=i: (
                        ktsh.ap()[mc * P:(mc + 1) * P, TC:2 * TC], kst[i][:]),
                    reads=[f"kst{i}"], writes=["ktsh"])

            proj_loop(0, "wqT", q_consume)
            proj_loop(16, "wkT", k_consume)

            vhalf = bf3(A4, NHC, 1024)
            for half in range(2):
                em.fence() if half == 0 else None
                em.dma("pool", vhalf[:, :, :],
                       din["wvT"].ap()[:, half * 1024:(half + 1) * 1024]
                       .rearrange("(c p) o -> p c o", p=P),
                       writes=["vhalf"])
                for tc in range(8):
                    i = tc % 2
                    em.mm(ps[i], pk(i),
                          [(xln[:, kc, tc * 128:(tc + 1) * 128],
                            vhalf[:, kc, :], [KLN, "vhalf"])
                           for kc in range(NHC)])
                    em.op("dve", lambda v, i=i: v.memset(vst[i][:], 1.0),
                          writes=[f"vst{i}"])
                    em.op("act", lambda s, i=i: s.copy(
                        vst[i].rearrange("p (b c) -> p b c", c=65)[:, :, 0:64],
                        ps[i][:].rearrange("p (b c) -> p b c", c=64)),
                        reads=pk(i) + (f"vst{i}",), writes=[f"vst{i}"])
                    def _vdst(q0v, tc=tc, half=half):
                        return dataclasses.replace(
                            vsh.ap(), ap=[[HEADS * 65, P], [65, 16], [1, 65]],
                            offset=(q0v + tc * 128) * HEADS * 65 + half * 16 * 65)
                    em.rankdma(
                        g, r_prank,
                        lambda i=i: (_vdst(0),
                                     vst[i].rearrange("p (b c) -> p b c", c=65)),
                        lambda i=i: (_vdst(TC),
                                     vst[i].rearrange("p (b c) -> p b c", c=65)),
                        reads=[f"vst{i}"], writes=["vsh"])

            em.rankdma(g, r_prank,
                       lambda: (flg.ap()[0:1, :], nonce_sb[:]),
                       lambda: (flg.ap()[1:2, :], nonce_sb[:]),
                       reads=["ktsh", "vsh", "nonce"], writes=["flg"])
            g.wait_ge(em.sem[("d", "pool")], em.cnt[("d", "pool")])
            em.hw[("pool", ("d", "pool"))] = em.cnt[("d", "pool")]
            with (g.register("r_sm") as r_sm, g.register("r_ne") as r_ne,
                  g.register("r_2n") as r_2n, g.register("r_nn") as r_nn,
                  g.register("r_tg") as r_tg):
                g.reg_load(r_nn, nonce_sb[0:1, 0:1])
                g.reg_add(r_2n, r_nn, r_nn)
                g.reg_mov(r_ne, 1)
                g.reg_mov(r_tg, 0)
                with g.While(r_ne):
                    g.dma_start(out=pf_sb[:, 0:32],
                                in_=flg.ap()[0:2, :].rearrange("a b -> (a b)")
                                .rearrange("(o ab) -> o ab", o=1)
                                ).then_inc(spin_sem, 16)
                    g.reg_add(r_tg, r_tg, 16)
                    g.wait_ge(spin_sem, r_tg)
                    g.reg_load(r_sm, pf_sb[0:1, 0:1])
                    g.reg_load(r_ne, pf_sb[0:1, 16:17])
                    g.reg_add(r_sm, r_sm, r_ne)
                    g.reg_alu(r_ne, r_sm, r_2n, ALU.not_equal)
            barrier_tok = em.op("pool", lambda gg: gg.memset(pf2_sb[:], 0))
            em.writer["ktsh"] = barrier_tok
            em.writer["vsh"] = barrier_tok

            em.fence()
            BT3 = bf3(A3, NHC, 1024)
            em.rankdma(g, r_prank,
                       lambda: (BT3[:, :, :],
                                BTd.ap()[0:16 * P, :].rearrange(
                                    "(c p) t -> p c t", p=P)),
                       lambda: (BT3[:, :, :],
                                BTd.ap()[16 * P:32 * P, :].rearrange(
                                    "(c p) t -> p c t", p=P)),
                       reads=["BTd"], writes=["BT"])
            ctxn3 = bf3(A4, NHC, 1024)
            scH = Bump(pool, SC, SCE)
            ktt = [scH.take(2048) for _ in range(2)]
            vh = [[scH.take(16 * 65).rearrange("p (c w) -> p c w", w=65)
                   for _ in range(2)] for _ in range(2)]
            Eb = [[scH.take(1024), scH.take(1024)] for _ in range(2)]
            rb = scH.take(2048, f32=True)
            rden = scH.take(2048, f32=True)
            ctmp = scH.take(1024)
            em.fence()

            def kpair_load(j):
                em.dma("sp", ktt[j % 2][:, :],
                       ktsh.ap()[j * 128:(j + 1) * 128, :],
                       reads=["ktsh"], writes=[f"ktt{j % 2}"])

            def vpair_load(j):
                for ab in range(2):
                    em.dma("sp", vh[j % 2][ab][:, :, :],
                           vsh.ap().rearrange("(c p) (b e) -> p c b e",
                                              p=P, e=65)[:, :, 2 * j + ab, :],
                           reads=["vsh"], writes=[f"vh{j % 2}{ab}"])

            kpair_load(0)
            vpair_load(0)
            for j in range(HEADS // 2):
                par = j % 2
                if j + 1 < HEADS // 2:
                    kpair_load(j + 1)
                    vpair_load(j + 1)
                qA = qT3[:, j, :][0:64, :]
                qB = qT3[:, j, :][64:128, :]
                kk = f"ktt{par}"
                for kc in range(NHC):
                    em.mm(ps[0], pk(0),
                          [(ktt[par][0:64, kc * 128:(kc + 1) * 128], qA,
                            [kk, "qT"])], start=True, stop=False)
                    em.mm(ps[1], pk(1),
                          [(ktt[par][64:128, kc * 128:(kc + 1) * 128], qB,
                            [kk, "qT"])], start=True, stop=False)
                    em.mm(ps[0], pk(0),
                          [(ID, BT3[:, kc, :], ["consb", "BT"])],
                          start=False, stop=True)
                    em.mm(ps[1], pk(1),
                          [(ID, BT3[:, kc, :], ["consb", "BT"])],
                          start=False, stop=True)
                    em.op("act", lambda s, kc=kc: s.activation(
                        Eb[0][kc % 2][:], ps[0][:], AF.Exp),
                        reads=pk(0), writes=[f"EbA{kc % 2}"])
                    em.op("act", lambda s, kc=kc: s.activation(
                        Eb[1][kc % 2][:], ps[1][:], AF.Exp),
                        reads=pk(1), writes=[f"EbB{kc % 2}"])
                    if kc >= 1:
                        jj = (kc - 1) % 2
                        em.mm(ps[2][0:65, :], pk(2),
                              [(vh[par][0][:, kc - 1, :], Eb[0][jj][:],
                                [f"vh{par}0", f"EbA{jj}"])],
                              start=(kc == 1), stop=False)
                        em.mm(ps[3][0:65, :], pk(3),
                              [(vh[par][1][:, kc - 1, :], Eb[1][jj][:],
                                [f"vh{par}1", f"EbB{jj}"])],
                              start=(kc == 1), stop=False)
                jj = (NHC - 1) % 2
                em.mm(ps[2][0:65, :], pk(2),
                      [(vh[par][0][:, NHC - 1, :], Eb[0][jj][:],
                        [f"vh{par}0", f"EbA{jj}"])], start=False, stop=True)
                em.mm(ps[3][0:65, :], pk(3),
                      [(vh[par][1][:, NHC - 1, :], Eb[1][jj][:],
                        [f"vh{par}1", f"EbB{jj}"])], start=False, stop=True)
                for ab in range(2):
                    psc = ps[2 + ab]
                    psb = ps[ab]
                    ck = pk(2 + ab)
                    em.op("act", lambda s, psc=psc, ab=ab: s.activation(
                        rden[64:65, ab * 1024:ab * 1024 + 1024],
                        psc[64:65, :], AF.Ln),
                        reads=ck, writes=[f"rdent{ab}"])
                    em.op("act", lambda s, ab=ab: s.activation(
                        rden[64:65, ab * 1024:ab * 1024 + 1024],
                        rden[64:65, ab * 1024:ab * 1024 + 1024], AF.Exp,
                        scale=-1.0),
                        reads=[f"rdent{ab}"], writes=[f"rdent{ab}"])
                    em.mm(psb[0:64, :], pk(ab),
                          [(onesf[64:65, 0:64],
                            rden[64:65, ab * 1024:ab * 1024 + 1024],
                            [f"rdent{ab}", "onesf"])])
                    em.op("dve", lambda v, psb=psb, ab=ab: v.tensor_copy(
                        out=rb[0:64, ab * 1024:ab * 1024 + 1024],
                        in_=psb[0:64, :]),
                        reads=pk(ab), writes=[f"rb{ab}"])
                    if ab == 0:
                        em.op("dve", lambda v, j=j, psc=psc, ab=ab:
                              v.tensor_tensor(
                                  out=ctxn3[:, j, :][0:64, :],
                                  in0=psc[0:64, :],
                                  in1=rb[0:64, ab * 1024:ab * 1024 + 1024],
                                  op=ALU.mult),
                              reads=ck + (f"rb{ab}",), writes=["ctxn"])
                    else:
                        em.op("dve", lambda v, psc=psc, ab=ab:
                              v.tensor_tensor(
                                  out=ctmp[0:64, :], in0=psc[0:64, :],
                                  in1=rb[0:64, ab * 1024:ab * 1024 + 1024],
                                  op=ALU.mult),
                              reads=ck + (f"rb{ab}",), writes=["ctmp"])
                        em.dma("sp", ctxn3[:, j, :][64:128, :],
                               ctmp[0:64, :], reads=["ctmp"],
                               writes=["ctxn"])

            em.fence()
            scO = Bump(pool, SC + 12416, SCE)
            x1in = [scO.take(1024, f32=True), scO.take(1024, f32=True)]
            xo2 = [scO.take(1024, f32=True), scO.take(1024, f32=True)]
            out2 = outd if stage == "x2" else x2d
            okey = "outd" if stage == "x2" else "x2d"

            def o_consume(mc, buf, bk):
                i = mc % 2
                em.dma("sp", x1in[i][:], x1d.ap()[mc * P:(mc + 1) * P, :],
                       reads=["x1d"], writes=[f"x1in{i}"])
                em.mm(ps[i], pk(i),
                      [(buf[:, kc, :], ctxn3[:, kc, :], [bk, "ctxn"])
                       for kc in range(NHC)])
                em.op("dve", lambda v, i=i: v.tensor_tensor(
                    out=xo2[i][:], in0=ps[i][:], in1=x1in[i][:], op=ALU.add),
                    reads=pk(i) + (f"x1in{i}",), writes=[f"xo2{i}"])
                em.dma("act", out2.ap()[mc * P:(mc + 1) * P, :], xo2[i][:],
                       reads=[f"xo2{i}"], writes=[okey])

            proj_loop(32, "woT", o_consume)

        if stage == "full":
            emit_ffn_call(x2d, "x2d", outd, "outd")

        for sk in (("d", "act"), ("d", "sp"), ("d", "pool")):
            if em.cnt[sk]:
                g.wait_ge(em.sem[sk], em.cnt[sk])
        em._deps("pool", ["nonce"], [])
        with g.If_eq(r_prank, 0):
            g.dma_start(out=flg.ap()[2:3, :], in_=nonce_sb[:]).then_inc(
                spin_sem, 16)
        with g.Else():
            g.dma_start(out=flg.ap()[3:4, :], in_=nonce_sb[:]).then_inc(
                spin_sem, 16)
        g.wait_ge(spin_sem, 16)
        with (g.register("r_sm2") as r_sm, g.register("r_ne2") as r_ne,
              g.register("r_2n2") as r_2n, g.register("r_nn2") as r_nn,
              g.register("r_tg2") as r_tg):
            g.reg_load(r_nn, nonce_sb[0:1, 0:1])
            g.reg_add(r_2n, r_nn, r_nn)
            g.reg_mov(r_ne, 1)
            g.reg_mov(r_tg, 16)
            with g.While(r_ne):
                g.dma_start(out=pf2_sb[:, 0:32],
                            in_=flg.ap()[2:4, :].rearrange("a b -> (a b)")
                            .rearrange("(o ab) -> o ab", o=1)
                            ).then_inc(spin_sem, 16)
                g.reg_add(r_tg, r_tg, 16)
                g.wait_ge(spin_sem, r_tg)
                g.reg_load(r_sm, pf2_sb[0:1, 0:1])
                g.reg_load(r_ne, pf2_sb[0:1, 16:17])
                g.reg_add(r_sm, r_sm, r_ne)
                g.reg_alu(r_ne, r_sm, r_2n, ALU.not_equal)

    return nc




def _prep_inputs(inputs, nonce):
    f = np.float32
    rel = np.asarray(inputs["L0_attn_rel_pos"], dtype=f)
    base = {
        "wqT": np.ascontiguousarray(np.asarray(inputs["L0_attn_wq"]).T, dtype=f),
        "wkT": np.ascontiguousarray(np.asarray(inputs["L0_attn_wk"]).T, dtype=f),
        "wvT": np.ascontiguousarray(np.asarray(inputs["L0_attn_wv"]).T, dtype=f),
        "woT": np.ascontiguousarray(np.asarray(inputs["L0_attn_wo"]).T, dtype=f),
        "wup": np.ascontiguousarray(inputs["L0_ffn_Wup"], dtype=f),
        "wsw": np.ascontiguousarray(inputs["L0_ffn_Wsw"], dtype=f),
        "wdn": np.ascontiguousarray(inputs["L0_ffn_Wdn"], dtype=f),
        "wsu": np.ascontiguousarray(inputs["L0_ffn_Wsu"], dtype=f),
        "wsa": np.ascontiguousarray(inputs["L0_ffn_Wsa"], dtype=f),
        "wsd": np.ascontiguousarray(inputs["L0_ffn_Wsd"], dtype=f),
        "wg": np.ascontiguousarray(
            np.asarray(inputs["L0_ffn_Wg"], dtype=f).reshape(2, 16, 128, 8)
            .transpose(2, 0, 1, 3).reshape(128, 256)),
        "bupr": np.ascontiguousarray(
            np.asarray(inputs["L0_ffn_bup"], dtype=f).reshape(6, 8, 128)
            .transpose(2, 0, 1).reshape(128, 48)),
        "bswr": np.ascontiguousarray(
            np.asarray(inputs["L0_ffn_bsw"], dtype=f).reshape(6, 8, 128)
            .transpose(2, 0, 1).reshape(128, 48)),
        "bsur": np.ascontiguousarray(
            np.asarray(inputs["L0_ffn_bsu"], dtype=f).reshape(16, 128).T),
        "bsar": np.ascontiguousarray(
            np.asarray(inputs["L0_ffn_bsa"], dtype=f).reshape(8, 128).T),
        "bsdr": np.ascontiguousarray(
            np.asarray(inputs["L0_ffn_bsd"], dtype=f).reshape(16, 128).T),
        "bgr": np.ascontiguousarray(np.asarray(inputs["L0_ffn_bg"], dtype=f).T),
        "bdnr": np.ascontiguousarray(inputs["L0_ffn_bdn"], dtype=f),
        "relrev": np.ascontiguousarray(
            np.concatenate([rel[::-1], np.zeros((1, 64), f)], axis=0)),
        "consts": np.ascontiguousarray(np.concatenate(
            [np.eye(128, dtype=f), np.eye(128, dtype=f)[:, ::-1],
             np.ones((128, 1), f)], axis=1)),
        "ind6h": np.ascontiguousarray(np.concatenate(
            [np.tile(np.eye(8, dtype=f)[n][:, None], (1, 128))
             for n in range(6)], axis=1)),
        "onesf": np.ones((128, 128), f),
        "noncein": np.full((1, 16), nonce, np.int32),
    }
    x = np.asarray(inputs["L0_attn_hidden_states"], dtype=f)
    maps = []
    for c in range(8):
        b, half = c // 2, c % 2
        m = dict(base)
        m["xT"] = np.ascontiguousarray(x[b, half * TC:(half + 1) * TC, :].T)
        maps.append(m)
    return maps


def kernel(stage="full", **inputs):
    nonce = (int.from_bytes(os.urandom(4), "little") | 1) & 0x7FFFFFFF
    maps = _prep_inputs(inputs, nonce)
    nc = build(stage=stage)
    res = run_bass_kernel_spmd(nc, maps, core_ids=list(range(8)))
    out = np.empty((4, 2048, 2048), np.float32)
    for c in range(8):
        b, half = c // 2, c % 2
        out[b, half * TC:(half + 1) * TC, :] = np.asarray(
            res.results[c]["out"]).T
    return out


# version 04
# speedup vs baseline: 1.7370x; 1.0983x over previous
import os
import dataclasses
import numpy as np
import concourse.bass as bass
import concourse.mybir as mybir
from concourse.bass_utils import run_bass_kernel_spmd
from contextlib import ExitStack

F32 = mybir.dt.float32
BF16 = mybir.dt.bfloat16
I32 = mybir.dt.int32
AF = mybir.ActivationFunctionType
ALU = mybir.AluOpType
AXX = mybir.AxisListType.X

P = 128
H = 2048
NHC = 16
TC = 1024
E = 1024
NEC = 8
NEXP = 6
HEADS = 32

PKEY = ("p", "pe")


class Em:

    def __init__(self, nc, ctx):
        self.nc = nc
        self.eng = {"pe": nc.tensor, "act": nc.scalar, "dve": nc.vector,
                    "sp": nc.sync, "pool": nc.gpsimd}
        self.sem = {}
        for e in ("pe", "act", "dve", "pool"):
            self.sem[("p", e)] = ctx.enter_context(nc.semaphore("prog_" + e))
        for q in ("sp", "act", "pool"):
            self.sem[("d", q)] = ctx.enter_context(nc.semaphore("dma_" + q))
        self.cnt = {k: 0 for k in self.sem}
        self.hw = {}
        self.writer = {}
        self.readers = {}

    def _wait(self, eng, tok):
        if tok is None:
            return
        sk, v = tok
        if sk == PKEY and eng == "pe":
            return
        if v <= self.hw.get((eng, sk), 0):
            return
        self.eng[eng].wait_ge(self.sem[sk], v)
        self.hw[(eng, sk)] = v

    def _deps(self, eng, reads, writes):
        for k in reads:
            self._wait(eng, self.writer.get(k))
        for k in writes:
            self._wait(eng, self.writer.get(k))
            for sk, v in self.readers.get(k, {}).items():
                self._wait(eng, (sk, v))

    def _record(self, tok, reads, writes):
        for k in writes:
            self.writer[k] = tok
            self.readers[k] = {}
        if tok is not None:
            sk, v = tok
            for k in reads:
                r = self.readers.setdefault(k, {})
                r[sk] = max(r.get(sk, 0), v)

    def op(self, eng, fn, reads=(), writes=()):
        self._deps(eng, reads, writes)
        inst = fn(self.eng[eng])
        sk = ("p", eng)
        self.cnt[sk] += 1
        inst.then_inc(self.sem[sk], 1)
        tok = (sk, self.cnt[sk])
        self._record(tok, reads, writes)
        return tok

    def dma(self, q, out, in_, reads=(), writes=(), slow=False):
        self._deps(q, reads, writes)
        inst = self.eng[q].dma_start(out=out, in_=in_,
                                     allow_slow_non_contiguous=slow)
        sk = ("d", q)
        self.cnt[sk] += 16
        inst.then_inc(self.sem[sk], 16)
        tok = (sk, self.cnt[sk])
        self._record(tok, reads, writes)
        return tok

    def rankdma(self, g, r_prank, mk0, mk1, reads=(), writes=()):
        self._deps("pool", reads, writes)
        sk = ("d", "pool")
        with g.If_eq(r_prank, 0):
            o0, i0 = mk0()
            g.dma_start(out=o0, in_=i0).then_inc(self.sem[sk], 16)
        with g.Else():
            o1, i1 = mk1()
            g.dma_start(out=o1, in_=i1).then_inc(self.sem[sk], 16)
        self.cnt[sk] += 16
        tok = (sk, self.cnt[sk])
        self._record(tok, reads, writes)
        return tok

    def mm(self, pap, pkeys, kitems, start=True, stop=True):
        W = pap.shape[-1]
        halves = [(0, W)] if W <= 512 else [(a, a + 512) for a in range(0, W, 512)]
        all_reads = set()
        n = len(kitems)
        for j, (la, ra, reads) in enumerate(kitems):
            first, last = j == 0, j == n - 1
            self._deps("pe", reads, pkeys if (first and start) else ())
            all_reads |= set(reads)
            for hi, (a, b) in enumerate(halves):
                rv = ra if len(halves) == 1 else ra[:, a:b]
                pv = pap if len(halves) == 1 else pap[:, a:b]
                inst = self.eng["pe"].matmul(
                    pv, la, rv, start=(first and start),
                    stop=(last and stop))
                if last and hi == len(halves) - 1:
                    self.cnt[PKEY] += 1
                    inst.then_inc(self.sem[PKEY], 1)
                    tok = (PKEY, self.cnt[PKEY])
        if start:
            self._record(tok, sorted(all_reads), pkeys)
        else:
            for k in pkeys:
                self.writer[k] = tok
            self._record(tok, sorted(all_reads), ())
        return tok

    def fence(self):
        for eng in ("pe", "act", "dve", "sp", "pool"):
            for sk, c in self.cnt.items():
                if c == 0:
                    continue
                if sk == ("p", eng):
                    continue
                self._wait(eng, (sk, c))


class Bump:
    def __init__(self, pool, base, limit):
        self.pool, self.base, self.limit, self.off = pool, base, limit, base

    def take(self, n, f32=False):
        elems = 2 * n if f32 else n
        pad = (elems + 63) & ~63
        a = self.off
        assert a + pad <= self.limit, f"scratch overflow {a}+{pad}>{self.limit}"
        self.off = a + pad
        v = self.pool[:, a:a + elems]
        return v.bitcast(F32) if f32 else v


def build(stage="full"):
    nc = bass.Bass("TRN2", target_bir_lowering=False)

    din = {}
    for name, shape, dt in [
        ("xT", [H, TC], F32),
        ("wqT", [H, H], F32), ("wkT", [H, H], F32),
        ("wvT", [H, H], F32), ("woT", [H, H], F32),
        ("wg", [P, 256], F32),
        ("wup", [NEXP, H, E], F32), ("wsw", [NEXP, E, E], F32),
        ("wdn", [NEXP, E, H], F32),
        ("wsu", [H, 2 * E], F32), ("wsa", [E, E], F32), ("wsd", [2 * E, H], F32),
        ("bupr", [P, 48], F32), ("bswr", [P, 48], F32), ("bsur", [P, 16], F32),
        ("bsar", [P, 8], F32), ("bsdr", [P, 16], F32), ("bgr", [8, 2], F32),
        ("bdnr", [NEXP, H], F32),
        ("relrev", [1024, 64], F32),
        ("consts", [P, 257], F32),
        ("ind6h", [8, NEXP * P], F32),
        ("onesf", [P, P], F32),
        ("noncein", [1, 16], I32),
    ]:
        din[name] = nc.dram_tensor(name, shape, dt, kind="ExternalInput")

    outd = nc.dram_tensor("out", [H, TC], F32, kind="ExternalOutput")

    x1d = nc.dram_tensor("x1d", [H, TC], F32, kind="Internal")
    x2d = nc.dram_tensor("x2d", [H, TC], F32, kind="Internal")
    hd = nc.dram_tensor("hd", [NEXP, E, TC], BF16, kind="Internal")
    Pd = nc.dram_tensor("Pd", [4096], F32, kind="Internal")
    mbrd = nc.dram_tensor("mbrd", [1024], F32, kind="Internal")
    BTd = nc.dram_tensor("BTd", [2 * 2 * TC, TC], BF16, kind="Internal")
    ktsh = nc.dram_tensor("ktsh", [H, 2 * TC], BF16, kind="Internal",
                          addr_space="Shared")
    vsh = nc.dram_tensor("vsh", [2 * TC, HEADS * 65], BF16, kind="Internal",
                         addr_space="Shared")
    flg = nc.dram_tensor("flg", [4, 16], I32, kind="Internal",
                         addr_space="Shared")

    with ExitStack() as ctx:
        em = Em(nc, ctx)
        spin_sem = ctx.enter_context(nc.semaphore("spin"))

        POOL_E = 92 * 1024
        pool = ctx.enter_context(nc.sbuf_tensor("pool", [P, POOL_E], BF16))
        A1, A2, A3, A4, SC = 0, 16384, 32768, 49152, 65536
        SCE = POOL_E

        bcb = ctx.enter_context(nc.sbuf_tensor("bcb", [P, 2048], F32))
        consb = ctx.enter_context(nc.sbuf_tensor("consb", [P, 257], BF16))
        onesf = ctx.enter_context(nc.sbuf_tensor("onesfs", [P, P], F32))
        biasf = ctx.enter_context(nc.sbuf_tensor("biasf", [P, 136], F32))
        bg_sb = ctx.enter_context(nc.sbuf_tensor("bg_sb", [8, 2], F32))
        bdn_sb = ctx.enter_context(nc.sbuf_tensor("bdn_sb", [NEXP, H], BF16))
        wg_sb = ctx.enter_context(nc.sbuf_tensor("wg_sb", [P, 256], BF16))
        ind6 = ctx.enter_context(nc.sbuf_tensor("ind6", [8, NEXP * P], BF16))
        gsT = bcb[:, 0:512].bitcast(BF16)[0:8, :]
        wT_sb = bcb[:, 512:1024].bitcast(BF16)[0:8, :]
        nonce_sb = ctx.enter_context(nc.sbuf_tensor("nonce_sb", [1, 16], I32))
        epsc = ctx.enter_context(nc.sbuf_tensor("epsc", [1, 8], F32))
        pf_sb = ctx.enter_context(nc.sbuf_tensor("pf_sb", [1, 32], I32))
        pf2_sb = ctx.enter_context(nc.sbuf_tensor("pf2_sb", [1, 32], I32))

        ps = [ctx.enter_context(nc.psum_tensor(f"ps{i}", [P, 1024], F32))
              for i in range(4)]

        block = ctx.enter_context(nc.Block())

        def pk(i):
            return (f"ps{i}a", f"ps{i}b")

        ID = consb[:, 0:128]
        JM = consb[:, 128:256]
        ONE1 = consb[:, 256:257]

        def bf3(base, nch, w):
            return pool[:, base:base + nch * w].rearrange(
                "p (c w) -> p c w", w=w)

        xln = bf3(A1, NHC, 1024)
        KLN = "xlnA1"

        em.dma("pool", consb[:], din["consts"].ap(), writes=["consb"])
        em.dma("sp", onesf[:], din["onesf"].ap(), writes=["onesf"])
        em.dma("sp", biasf[:, 0:48], din["bupr"].ap(), writes=["biasf"])
        em.dma("sp", biasf[:, 48:96], din["bswr"].ap(), writes=["biasf"])
        em.dma("sp", biasf[:, 96:112], din["bsur"].ap(), writes=["biasf"])
        em.dma("sp", biasf[:, 112:120], din["bsar"].ap(), writes=["biasf"])
        em.dma("sp", biasf[:, 120:136], din["bsdr"].ap(), writes=["biasf"])
        em.dma("sp", bg_sb[:], din["bgr"].ap(), writes=["bg"])
        em.dma("pool", bdn_sb[:], din["bdnr"].ap(), writes=["bdn"])
        em.dma("pool", wg_sb[:], din["wg"].ap(), writes=["wg"])
        em.dma("pool", ind6[:], din["ind6h"].ap(), writes=["ind6"])
        em.dma("sp", nonce_sb[:], din["noncein"].ap(), writes=["nonce"])
        em.op("dve", lambda v: v.memset(epsc[:], 1e-5), writes=["epsc"])

        g = em.eng["pool"]
        pid = g.partition_id()
        r_prank = ctx.enter_context(g.register("r_prank"))
        g.reg_alu(r_prank, pid, 1, ALU.bitwise_and)


        def emit_btbuild():
            scp = Bump(pool, SC + 22528, SCE)
            wwin = [scp.take(1024), scp.take(1024)]
            btst = [scp.take(1024), scp.take(1024)]
            relsb = scp.take(512, f32=True)
            mbrf = scp.take(8, f32=True)
            P_sb = scp.take(32, f32=True)
            vcol0 = scp.take(1, f32=True)
            vcol1 = scp.take(1, f32=True)
            vt = scp.take(1, f32=True)
            idf = scp.take(128, f32=True)
            Pout = scp.take(128, f32=True)
            em.dma("sp", idf[:], din["consts"].ap()[:, 0:128], writes=["idf"])
            em.dma("sp", relsb.rearrange("p (c d) -> p c d", d=64),
                   din["relrev"].ap().rearrange("(c p) d -> p c d", p=P),
                   writes=["relsb"])
            for c in range(8):
                em.op("dve", lambda v, c=c: v.reduce_sum(
                    out=mbrf[:, c:c + 1], in_=relsb[:, c * 64:(c + 1) * 64],
                    axis=AXX), reads=["relsb"], writes=["mbr"])
            em.op("dve", lambda v: v.tensor_scalar(
                out=mbrf[:], in0=mbrf[:], scalar1=1.0 / 64, scalar2=None,
                op0=ALU.mult), reads=["mbr"], writes=["mbr"])
            em.mm(ps[3][:, 0:1], (pk(3)[0],),
                  [(onesf[0:1, :], mbrf[0:1, 0:1], ["mbr", "onesf"])])
            em.op("act", lambda s: s.copy(vcol0[:], ps[3][:, 0:1]),
                  reads=pk(3), writes=["vcol0"])
            em.dma("sp", vt[0:1, 0:1], mbrf[126:127, 7:8],
                   reads=["mbr"], writes=["vt"])
            em.mm(ps[3][:, 0:1], (pk(3)[0],),
                  [(onesf[0:1, :], vt[0:1, 0:1], ["vt", "onesf"])])
            em.op("act", lambda s: s.copy(vcol1[:], ps[3][:, 0:1]),
                  reads=pk(3), writes=["vcol1"])
            em.op("dve", lambda v: v.memset(P_sb[:], 0.0), writes=["P_sb"])
            em.op("dve", lambda v: v.tensor_scalar(
                out=P_sb[:, 0:12], in0=P_sb[:, 0:12], scalar1=vcol0[:, 0:1],
                scalar2=None, op0=ALU.add),
                reads=["P_sb", "vcol0"], writes=["P_sb"])
            em.op("dve", lambda v: v.tensor_scalar(
                out=P_sb[:, 20:32], in0=P_sb[:, 20:32], scalar1=vcol1[:, 0:1],
                scalar2=None, op0=ALU.add),
                reads=["P_sb", "vcol1"], writes=["P_sb"])
            em.op("dve", lambda v: v.tensor_copy(
                out=P_sb[:, 12:20], in_=mbrf[:]),
                reads=["mbr", "P_sb"], writes=["P_sb"])
            em.dma("sp", P_sb[127:128, 19:20], vcol1[0:1, 0:1],
                   reads=["vcol1", "P_sb"], writes=["P_sb"])
            em.mm(ps[3][0:32, 0:128], (pk(3)[0],),
                  [(P_sb[:], idf[:], ["P_sb", "idf"])])
            em.op("act", lambda s: s.copy(Pout[0:32, :], ps[3][0:32, 0:128]),
                  reads=pk(3), writes=["Pout"])
            em.dma("sp", Pd.ap().rearrange("(c p) -> c p", p=128),
                   Pout[0:32, :], reads=["Pout"], writes=["Pd"])
            for sel in range(2):
                for kc in range(16):
                    par = kc % 2
                    src = dataclasses.replace(
                        Pd.ap(), ap=[[1, P], [1, 1024]],
                        offset=1920 + sel * TC - 128 * kc)
                    em.dma("pool", wwin[par][:], src, reads=["Pd"],
                           writes=[f"wwin{par}"])
                    em.mm(ps[3], pk(3),
                          [(JM, wwin[par][:], [f"wwin{par}", "consb"])])
                    em.op("act", lambda s, par=par: s.copy(
                        btst[par][:], ps[3][:]),
                        reads=pk(3), writes=[f"btst{par}"])
                    em.dma("pool",
                           BTd.ap()[(sel * 16 + kc) * P:
                                    (sel * 16 + kc + 1) * P, :],
                           btst[par][:], reads=[f"btst{par}"],
                           writes=["BTd"])

        def emit_ln(src_dram, src_key):
            em.fence()
            xh = pool[:, A2:A2 + 2 * NHC * 1024].bitcast(F32).rearrange(
                "p (c w) -> p c w", w=1024)
            em.dma("sp", xh[:, :, :],
                   src_dram.ap().rearrange("(c p) t -> p c t", p=P),
                   reads=[src_key], writes=["xh"])
            s2 = Bump(pool, SC, SCE)
            xb = [s2.take(1024), s2.take(1024)]
            xq = [s2.take(1024), s2.take(1024)]
            lnr_mean = s2.take(1024, f32=True)
            lnr_var = s2.take(1024, f32=True)
            lnr_tmp = s2.take(1024, f32=True)
            lnr_rstd = s2.take(1024, f32=True)
            lnr_mrs = s2.take(1024, f32=True)
            for hc in range(NHC):
                par = hc % 2
                em.op("act", lambda s, hc=hc, par=par: s.copy(
                    xb[par][:], xh[:, hc, :]), reads=["xh"], writes=[f"xb{par}"])
                em.op("act", lambda s, hc=hc, par=par: s.activation(
                    xq[par][:], xh[:, hc, :], AF.Square),
                    reads=["xh"], writes=[f"xq{par}"])
                em.mm(ps[2][0:1, :], pk(2),
                      [(ONE1, xb[par][:], [f"xb{par}", "consb"])],
                      start=(hc == 0), stop=(hc == NHC - 1))
                em.mm(ps[3][0:1, :], pk(3),
                      [(ONE1, xq[par][:], [f"xq{par}", "consb"])],
                      start=(hc == 0), stop=(hc == NHC - 1))
            em.op("act", lambda s: s.mul(lnr_mean[0:1, :], ps[2][0:1, :],
                                          1.0 / H),
                  reads=pk(2), writes=["lnmean"])
            em.op("act", lambda s: s.mul(lnr_var[0:1, :], ps[3][0:1, :],
                                         1.0 / H),
                  reads=pk(3), writes=["lnvar"])
            em.op("act", lambda s: s.square(lnr_tmp[0:1, :], lnr_mean[0:1, :]),
                  reads=["lnmean"], writes=["lntmp"])
            em.op("dve", lambda v: v.tensor_tensor(
                out=lnr_var[0:1, :], in0=lnr_var[0:1, :],
                in1=lnr_tmp[0:1, :], op=ALU.subtract),
                reads=["lnvar", "lntmp"], writes=["lnvar"])
            em.op("act", lambda s: s.activation(
                lnr_tmp[0:1, :], lnr_var[0:1, :], AF.Ln,
                bias=epsc[0:1, 0:1]),
                reads=["lnvar", "epsc"], writes=["lntmp"])
            em.op("act", lambda s: s.activation(
                lnr_rstd[0:1, :], lnr_tmp[0:1, :], AF.Exp, scale=-0.5),
                reads=["lntmp"], writes=["rstd"])
            em.op("dve", lambda v: v.tensor_tensor(
                out=lnr_mrs[0:1, :], in0=lnr_mean[0:1, :],
                in1=lnr_rstd[0:1, :], op=ALU.mult),
                reads=["lnmean", "rstd"], writes=["mrs"])
            em.mm(ps[2], pk(2), [(onesf[0:1, :], lnr_rstd[0:1, :], ["rstd", "onesf"])])
            em.op("act", lambda s: s.copy(bcb[:, 0:1024], ps[2][:]),
                  reads=pk(2), writes=["bcb0"])
            em.mm(ps[3], pk(3), [(onesf[0:1, :], lnr_mrs[0:1, :], ["mrs", "onesf"])])
            em.op("act", lambda s: s.copy(bcb[:, 1024:2048], ps[3][:]),
                  reads=pk(3), writes=["bcb1"])
            t32 = [s2.take(1024, f32=True), s2.take(1024, f32=True)]
            for hc in range(NHC):
                par = hc % 2
                em.op("dve", lambda v, hc=hc, par=par: v.tensor_tensor(
                    out=t32[par][:], in0=xh[:, hc, :], in1=bcb[:, 0:1024],
                    op=ALU.mult), reads=["xh", "bcb0"], writes=[f"t32{par}"])
                em.op("dve", lambda v, hc=hc, par=par: v.tensor_tensor(
                    out=xln[:, hc, :], in0=t32[par][:], in1=bcb[:, 1024:2048],
                    op=ALU.subtract), reads=[f"t32{par}", "bcb1"], writes=[KLN])

        class WS:
            def __init__(self, name, scb, nch, w, nbuf=4):
                self.name, self.nbuf = name, nbuf
                self.bufs = [scb.take(nch * w).rearrange("p (c w) -> p c w", w=w)
                             for _ in range(nbuf)]

            def key(self, i):
                return f"{self.name}{i % self.nbuf}"

            def load(self, i, src_ap):
                em.dma("pool", self.bufs[i % self.nbuf][:, :, :], src_ap,
                       writes=[self.key(i)])

            def get(self, i):
                return self.bufs[i % self.nbuf], self.key(i)

        def colslice(t_ap, c0, c1, p=P):
            return t_ap[:, c0:c1].rearrange("(c p) o -> p c o", p=p)

        def emit_ffn(j, xt3, xt_key, out_mode, base_dram, base_key,
                     out_dram, out_key, also_out=None):
            em.fence()
            sc = Bump(pool, SC, SCE)
            gtm = sc.take(64, f32=True)
            gsc = sc.take(40, f32=True)
            wtmb = sc.take(64)
            em.mm(ps[2][0:8, :], pk(2),
                  [(wg_sb[:, j * 128 + kc * 8: j * 128 + kc * 8 + 8],
                    xt3[:, kc, :], ["wg", xt_key]) for kc in range(NHC)])
            em.op("act", lambda s: s.activation(
                gsT[:], ps[2][0:8, :], AF.Identity, bias=bg_sb[:, j:j + 1]),
                reads=pk(2) + ("bg",), writes=["gsT"])
            for tb in range(8):
                bank = 0 if tb < 4 else 1
                em.mm(ps[3][:, tb * 128: tb * 128 + 8], (pk(3)[bank],),
                      [(gsT[:, tb * 128:(tb + 1) * 128], ID[0:8, 0:8],
                        ["gsT", "consb"])],
                      start=(tb % 4 == 0), stop=(tb % 4 == 3))
            for tb in range(8):
                em.op("act", lambda s, tb=tb: s.copy(
                    gtm[:, tb * 8:(tb + 1) * 8],
                    ps[3][:, tb * 128:tb * 128 + 8]),
                    reads=pk(3), writes=["gtm"])
            srt, msk, e8, nmx, den = (gsc[:, 0:8], gsc[:, 8:16], gsc[:, 16:24],
                                      gsc[:, 24:25], gsc[:, 25:26])
            for tb in range(8):
                gt = gtm[:, tb * 8:(tb + 1) * 8]
                em.op("dve", lambda v, gt=gt: v.max(out=srt, in_=gt),
                      reads=["gtm"], writes=["srt"])
                em.op("dve", lambda v, gt=gt: v.tensor_scalar(
                    out=msk, in0=gt, scalar1=srt[:, 5:6], scalar2=None,
                    op0=ALU.is_ge), reads=["gtm", "srt"], writes=["msk"])
                em.op("dve", lambda v: v.tensor_scalar(
                    out=nmx, in0=srt[:, 0:1], scalar1=-1.0, scalar2=None,
                    op0=ALU.mult), reads=["srt"], writes=["nmx"])
                em.op("act", lambda s, gt=gt: s.activation(
                    e8, gt, AF.Exp, bias=nmx[:, 0:1]),
                    reads=["gtm", "nmx"], writes=["e8"])
                em.op("dve", lambda v: v.tensor_tensor(
                    out=e8, in0=e8, in1=msk, op=ALU.mult),
                    reads=["e8", "msk"], writes=["e8"])
                em.op("dve", lambda v: v.reduce_sum(out=den, in_=e8, axis=AXX),
                      reads=["e8"], writes=["den"])
                em.op("dve", lambda v: v.reciprocal(out=den, in_=den),
                      reads=["den"], writes=["den"])
                em.op("dve", lambda v, tb=tb: v.tensor_scalar(
                    out=wtmb[:, tb * 8:(tb + 1) * 8], in0=e8,
                    scalar1=den[:, 0:1], scalar2=None, op0=ALU.mult),
                    reads=["e8", "den"], writes=["wtmb"])
            for tb in range(8):
                bank = 0 if tb < 4 else 1
                em.mm(ps[2][0:8, tb * 128:(tb + 1) * 128], (pk(2)[bank],),
                      [(wtmb[:, tb * 8:(tb + 1) * 8], ID, ["wtmb", "consb"])],
                      start=(tb % 4 == 0), stop=(tb % 4 == 3))
            em.op("act", lambda s: s.copy(wT_sb[:], ps[2][0:8, :]),
                  reads=pk(2), writes=["wT"])
            wbc = sc.take(NEXP * 1024).rearrange("p (n w) -> p n w", w=1024)
            for n in range(NEXP):
                i = n % 2
                em.mm(ps[i], pk(i),
                      [(ind6[:, n * 128:(n + 1) * 128], wT_sb[:],
                        ["ind6", "wT"])])
                em.op("act", lambda s, n=n, i=i: s.copy(wbc[:, n, :], ps[i][:]),
                      reads=pk(i), writes=["wbc"])

            tq = [sc.take(1024), sc.take(1024)]
            scbase = sc.off
            scS = Bump(pool, scbase, SCE)
            sh3 = bf3(A4, NHC, 1024)
            su = WS("su", scS, NHC, 128)
            for uc in range(3):
                su.load(uc, colslice(din["wsu"].ap(), uc * 128, (uc + 1) * 128))
            for uc in range(NHC):
                if uc + 3 < NHC:
                    su.load(uc + 3, colslice(din["wsu"].ap(), (uc + 3) * 128,
                                             (uc + 4) * 128))
                buf, bk = su.get(uc)
                i = uc % 2
                em.mm(ps[i], pk(i),
                      [(buf[:, kc, :], xt3[:, kc, :], [bk, xt_key])
                       for kc in range(NHC)])
                em.op("act", lambda s, uc=uc, i=i: s.activation(
                    sh3[:, uc, :], ps[i][:], AF.Identity,
                    bias=biasf[:, 96 + uc:97 + uc]),
                    reads=pk(i) + ("biasf",), writes=["sh"])
            outc3 = bf3(A3, NHC, 1024)
            sa = WS("sa", scS, NEC, 128)
            for q_ in range(2):
                sa.load(q_, colslice(din["wsa"].ap(), q_ * 128, (q_ + 1) * 128))
            for s_ in range(2):
                for fc in range(NEC):
                    gidx = s_ * NEC + fc
                    if gidx + 2 < 16:
                        f2 = (gidx + 2) % NEC
                        sa.load(gidx + 2, colslice(din["wsa"].ap(), f2 * 128,
                                                   (f2 + 1) * 128))
                    buf, bk = sa.get(gidx)
                    i = gidx % 2
                    em.mm(ps[i], pk(i),
                          [(buf[:, kc, :], sh3[:, s_ * NEC + kc, :], [bk, "sh"])
                           for kc in range(NEC)])
                    em.op("act", lambda s, fc=fc, i=i: s.activation(
                        tq[i][:], ps[i][:], AF.Silu,
                        bias=biasf[:, 112 + fc:113 + fc]),
                        reads=pk(i) + ("biasf",), writes=[f"tq{i}"])
                    em.op("dve", lambda v, s_=s_, fc=fc, i=i: v.tensor_tensor(
                        out=outc3[:, s_ * NEC + fc, :], in0=tq[i][:],
                        in1=sh3[:, s_ * NEC + fc, :], op=ALU.mult),
                        reads=[f"tq{i}", "sh"], writes=["outc"])

            em.fence()
            h13 = bf3(A4, NEC, 1024)
            h1w3 = bf3(A4 + NEC * 1024, NEC, 1024)
            scE = Bump(pool, scbase, SCE)
            up = WS("up", scE, NHC, 128, nbuf=3)
            sw = WS("sw", scE, NEC, 128)
            hst = scE.take(NEC * 1024).rearrange("p (c w) -> p c w", w=1024)

            def up_src(gi):
                n, ec = divmod(gi, NEC)
                return colslice(din["wup"].ap()[n], ec * 128, (ec + 1) * 128)

            def sw_src(gi):
                n, fc = divmod(gi, NEC)
                return colslice(din["wsw"].ap()[n], fc * 128, (fc + 1) * 128)

            for i0 in range(2):
                up.load(i0, up_src(i0))
                sw.load(i0, sw_src(i0))
            sw.load(2, sw_src(2))
            for n in range(NEXP):
                for ec in range(NEC):
                    gi = n * NEC + ec
                    if gi + 2 < NEXP * NEC:
                        up.load(gi + 2, up_src(gi + 2))
                    buf, bk = up.get(gi)
                    i = ec % 2
                    em.mm(ps[i], pk(i),
                          [(buf[:, kc, :], xt3[:, kc, :], [bk, xt_key])
                           for kc in range(NHC)])
                    em.op("act", lambda s, n=n, ec=ec, i=i: s.activation(
                        h13[:, ec, :], ps[i][:], AF.Identity,
                        bias=biasf[:, n * 8 + ec:n * 8 + ec + 1]),
                        reads=pk(i) + ("biasf",), writes=["h1"])
                    em.op("dve", lambda v, n=n, ec=ec: v.tensor_tensor(
                        out=h1w3[:, ec, :], in0=h13[:, ec, :], in1=wbc[:, n, :],
                        op=ALU.mult), reads=["h1", "wbc"], writes=["h1w"])
                for fc in range(NEC):
                    gi = n * NEC + fc
                    if gi + 3 < NEXP * NEC:
                        sw.load(gi + 3, sw_src(gi + 3))

                    buf, bk = sw.get(gi)
                    i = fc % 2
                    em.mm(ps[i], pk(i),
                          [(buf[:, kc, :], h13[:, kc, :], [bk, "h1"])
                           for kc in range(NEC)])
                    em.op("act", lambda s, n=n, fc=fc, i=i: s.activation(
                        tq[i][:], ps[i][:], AF.Silu,
                        bias=biasf[:, 48 + n * 8 + fc:49 + n * 8 + fc]),
                        reads=pk(i) + ("biasf",), writes=[f"tq{i}"])
                    em.op("dve", lambda v, fc=fc, i=i: v.tensor_tensor(
                        out=hst[:, fc, :], in0=tq[i][:], in1=h1w3[:, fc, :],
                        op=ALU.mult), reads=[f"tq{i}", "h1w"], writes=["hst"])
                em.dma("sp", hd.ap()[n].rearrange("(c p) t -> p c t", p=P),
                       hst[:, :, :], reads=["hst"], writes=[f"hd{n}"])

            em.fence()
            sc2 = Bump(pool, SC, SCE)
            dnb = [sc2.take(NEC * 512).rearrange("p (c w) -> p c w", w=512)
                   for _ in range(2)]
            sdb = sc2.take(NHC * 512).rearrange("p (c w) -> p c w", w=512)
            hrv = [bf3(A4, NEC, 1024), bf3(A4 + NEC * 1024, NEC, 1024)]
            if out_mode == "resid":
                scA2 = Bump(pool, A2, A3)
                xin = scA2.take(4 * 1024, f32=True).rearrange(
                    "p (c w) -> p c w", w=1024)
                xo = [scA2.take(1024, f32=True), scA2.take(1024, f32=True)]
                u32 = scA2.take(1024, f32=True)
            xt1_3 = bf3(A2, NHC, 1024)
            for gidx in range(4):
                hc0 = gidx * 4
                em.dma("pool", dnb[0][:, :, :],
                       colslice(din["wdn"].ap()[0], gidx * 512,
                                (gidx + 1) * 512), writes=["dnb0"])
                em.dma("pool", sdb[:, :, :],
                       colslice(din["wsd"].ap(), gidx * 512, (gidx + 1) * 512),
                       writes=["sdb"])
                em.dma("sp", hrv[0][:, :, :],
                       hd.ap()[0].rearrange("(c p) t -> p c t", p=P),
                       reads=["hd0"], writes=["hr0"])
                if out_mode == "resid":
                    em.dma("sp", xin[:, :, :],
                           base_dram.ap()[hc0 * P:(hc0 + 4) * P, :].rearrange(
                               "(c p) t -> p c t", p=P),
                           reads=[base_key], writes=["xin"])
                for i in range(4):
                    hc = hc0 + i
                    em.mm(ps[i], pk(i),
                          [(bdn_sb[:, hc * 128:(hc + 1) * 128], wT_sb[0:6, :],
                            ["bdn", "wT"])], start=True, stop=False)
                for n in range(NEXP):
                    if n + 1 < NEXP:
                        em.dma("pool", dnb[(n + 1) % 2][:, :, :],
                               colslice(din["wdn"].ap()[n + 1], gidx * 512,
                                        (gidx + 1) * 512),
                               writes=[f"dnb{(n + 1) % 2}"])
                        em.dma("sp", hrv[(n + 1) % 2][:, :, :],
                               hd.ap()[n + 1].rearrange("(c p) t -> p c t", p=P),
                               reads=[f"hd{n + 1}"],
                               writes=[f"hr{(n + 1) % 2}"])
                    for i in range(4):
                        em.mm(ps[i], pk(i),
                              [(dnb[n % 2][:, ec, i * 128:(i + 1) * 128],
                                hrv[n % 2][:, ec, :],
                                [f"dnb{n % 2}", f"hr{n % 2}"])
                               for ec in range(NEC)],
                              start=False, stop=False)
                for i in range(4):
                    em.mm(ps[i], pk(i),
                          [(sdb[:, kc, i * 128:(i + 1) * 128],
                            outc3[:, kc, :], ["sdb", "outc"])
                           for kc in range(NHC)],
                          start=False, stop=True)
                for i in range(4):
                    hc = hc0 + i
                    bsd_c = biasf[:, 120 + hc:121 + hc]
                    if out_mode == "xt1":
                        em.op("dve", lambda v, i=i, hc=hc, bsd_c=bsd_c:
                              v.scalar_tensor_tensor(
                                  out=xt1_3[:, hc, :], in0=ps[i][:],
                                  scalar=bsd_c, in1=xln[:, hc, :],
                                  op0=ALU.add, op1=ALU.add),
                              reads=pk(i) + ("biasf", KLN), writes=["xt1"])
                    else:
                        em.op("dve", lambda v, i=i, hc=hc, bsd_c=bsd_c:
                              v.scalar_tensor_tensor(
                                  out=u32[:], in0=ps[i][:], scalar=bsd_c,
                                  in1=xln[:, hc, :], op0=ALU.add, op1=ALU.add),
                              reads=pk(i) + ("biasf", KLN), writes=["u32"])
                        par = i % 2
                        em.op("dve", lambda v, i=i, par=par:
                              v.scalar_tensor_tensor(
                                  out=xo[par][:], in0=u32[:], scalar=0.5,
                                  in1=xin[:, i, :], op0=ALU.mult, op1=ALU.add),
                              reads=["u32", "xin"], writes=[f"xo{par}"])
                        em.dma("act", out_dram.ap()[hc * P:(hc + 1) * P, :],
                               xo[par][:], reads=[f"xo{par}"], writes=[out_key])
                        if also_out is not None:
                            em.dma("act",
                                   also_out.ap()[hc * P:(hc + 1) * P, :],
                                   xo[par][:], reads=[f"xo{par}"],
                                   writes=["alsoout"])

        def emit_ffn_call_post_ln(out_dram, out_key, also_out=None,
                                  xsrc_dram=None, xsrc_key="xT"):
            if xsrc_dram is None:
                xsrc_dram = din["xT"]
            xt1_3 = bf3(A2, NHC, 1024)
            emit_ffn(0, xln, KLN, "xt1", None, None, None, None)
            emit_ffn(1, xt1_3, "xt1", "resid", xsrc_dram, xsrc_key,
                     out_dram, out_key, also_out=also_out)

        def emit_ffn_call(xsrc_dram, xsrc_key, out_dram, out_key,
                          also_out=None):
            emit_ln(xsrc_dram, xsrc_key)
            emit_ffn_call_post_ln(out_dram, out_key, also_out=also_out,
                                  xsrc_dram=xsrc_dram, xsrc_key=xsrc_key)

        if stage == "xln1":
            emit_ln(din["xT"], "xT")
            emit_btbuild()
            s3 = Bump(pool, A3, SC)
            st = [s3.take(1024, f32=True), s3.take(1024, f32=True)]
            for hc in range(NHC):
                par = hc % 2
                em.op("act", lambda s, hc=hc, par=par: s.copy(
                    st[par][:], xln[:, hc, :]), reads=[KLN],
                    writes=[f"st{par}"])
                em.dma("act", outd.ap()[hc * P:(hc + 1) * P, :], st[par][:],
                       reads=[f"st{par}"], writes=["outd"])
        else:
            emit_ln(din["xT"], "xT")
            emit_btbuild()
            emit_ffn_call_post_ln(x1d, "x1d",
                                  also_out=(outd if stage == "x1" else None))

        if stage in ("x2", "full"):
            emit_ln(x1d, "x1d")
            em.fence()
            sc = Bump(pool, SC, SCE)
            qT3 = bf3(A2, NHC, 1024)
            wsl = WS("wsl", sc, NHC, 128)
            kst = [sc.take(1024), sc.take(1024)]
            vst = [sc.take(16 * 65), sc.take(16 * 65)]

            def proj_src(wname, mc):
                return colslice(din[wname].ap(), mc * 128, (mc + 1) * 128)

            def proj_loop(idx0, wname, consume):
                for mc in range(3):
                    wsl.load(idx0 + mc, proj_src(wname, mc))
                for mc in range(NHC):
                    if mc + 3 < NHC:
                        wsl.load(idx0 + mc + 3, proj_src(wname, mc + 3))
                    buf, bk = wsl.get(idx0 + mc)
                    consume(mc, buf, bk)

            def q_consume(mc, buf, bk):
                i = mc % 2
                em.mm(ps[i], pk(i),
                      [(buf[:, kc, :], xln[:, kc, :], [bk, KLN])
                       for kc in range(NHC)])
                em.op("act", lambda s, mc=mc, i=i: s.activation(
                    qT3[:, mc, :], ps[i][:], AF.Identity, scale=0.125),
                    reads=pk(i), writes=["qT"])

            def k_consume(mc, buf, bk):
                i = mc % 2
                em.mm(ps[i], pk(i),
                      [(buf[:, kc, :], xln[:, kc, :], [bk, KLN])
                       for kc in range(NHC)])
                em.op("act", lambda s, i=i: s.copy(kst[i][:], ps[i][:]),
                      reads=pk(i), writes=[f"kst{i}"])
                em.rankdma(
                    g, r_prank,
                    lambda mc=mc, i=i: (
                        ktsh.ap()[mc * P:(mc + 1) * P, 0:TC], kst[i][:]),
                    lambda mc=mc, i=i: (
                        ktsh.ap()[mc * P:(mc + 1) * P, TC:2 * TC], kst[i][:]),
                    reads=[f"kst{i}"], writes=["ktsh"])

            proj_loop(0, "wqT", q_consume)
            proj_loop(16, "wkT", k_consume)

            vhalf = bf3(A4, NHC, 1024)
            for half in range(2):
                em.fence() if half == 0 else None
                em.dma("pool", vhalf[:, :, :],
                       din["wvT"].ap()[:, half * 1024:(half + 1) * 1024]
                       .rearrange("(c p) o -> p c o", p=P),
                       writes=["vhalf"])
                for tc in range(8):
                    i = tc % 2
                    em.mm(ps[i], pk(i),
                          [(xln[:, kc, tc * 128:(tc + 1) * 128],
                            vhalf[:, kc, :], [KLN, "vhalf"])
                           for kc in range(NHC)])
                    em.op("dve", lambda v, i=i: v.memset(vst[i][:], 1.0),
                          writes=[f"vst{i}"])
                    em.op("act", lambda s, i=i: s.copy(
                        vst[i].rearrange("p (b c) -> p b c", c=65)[:, :, 0:64],
                        ps[i][:].rearrange("p (b c) -> p b c", c=64)),
                        reads=pk(i) + (f"vst{i}",), writes=[f"vst{i}"])
                    def _vdst(q0v, tc=tc, half=half):
                        return dataclasses.replace(
                            vsh.ap(), ap=[[HEADS * 65, P], [65, 16], [1, 65]],
                            offset=(q0v + tc * 128) * HEADS * 65 + half * 16 * 65)
                    em.rankdma(
                        g, r_prank,
                        lambda i=i: (_vdst(0),
                                     vst[i].rearrange("p (b c) -> p b c", c=65)),
                        lambda i=i: (_vdst(TC),
                                     vst[i].rearrange("p (b c) -> p b c", c=65)),
                        reads=[f"vst{i}"], writes=["vsh"])

            em.rankdma(g, r_prank,
                       lambda: (flg.ap()[0:1, :], nonce_sb[:]),
                       lambda: (flg.ap()[1:2, :], nonce_sb[:]),
                       reads=["ktsh", "vsh", "nonce"], writes=["flg"])
            g.wait_ge(em.sem[("d", "pool")], em.cnt[("d", "pool")])
            em.hw[("pool", ("d", "pool"))] = em.cnt[("d", "pool")]
            with (g.register("r_sm") as r_sm, g.register("r_ne") as r_ne,
                  g.register("r_2n") as r_2n, g.register("r_nn") as r_nn,
                  g.register("r_tg") as r_tg):
                g.reg_load(r_nn, nonce_sb[0:1, 0:1])
                g.reg_add(r_2n, r_nn, r_nn)
                g.reg_mov(r_ne, 1)
                g.reg_mov(r_tg, 0)
                with g.While(r_ne):
                    g.dma_start(out=pf_sb[:, 0:32],
                                in_=flg.ap()[0:2, :].rearrange("a b -> (a b)")
                                .rearrange("(o ab) -> o ab", o=1)
                                ).then_inc(spin_sem, 16)
                    g.reg_add(r_tg, r_tg, 16)
                    g.wait_ge(spin_sem, r_tg)
                    g.reg_load(r_sm, pf_sb[0:1, 0:1])
                    g.reg_load(r_ne, pf_sb[0:1, 16:17])
                    g.reg_add(r_sm, r_sm, r_ne)
                    g.reg_alu(r_ne, r_sm, r_2n, ALU.not_equal)
            barrier_tok = em.op("pool", lambda gg: gg.memset(pf2_sb[:], 0))
            em.writer["ktsh"] = barrier_tok
            em.writer["vsh"] = barrier_tok

            em.fence()
            BT3 = bf3(A3, NHC, 1024)
            em.rankdma(g, r_prank,
                       lambda: (BT3[:, :, :],
                                BTd.ap()[0:16 * P, :].rearrange(
                                    "(c p) t -> p c t", p=P)),
                       lambda: (BT3[:, :, :],
                                BTd.ap()[16 * P:32 * P, :].rearrange(
                                    "(c p) t -> p c t", p=P)),
                       reads=["BTd"], writes=["BT"])
            ctxn3 = bf3(A4, NHC, 1024)
            scH = Bump(pool, SC, SCE)
            ktt = [scH.take(2048) for _ in range(2)]
            vh = [[scH.take(16 * 65).rearrange("p (c w) -> p c w", w=65)
                   for _ in range(2)] for _ in range(2)]
            Eb = [[scH.take(1024), scH.take(1024)] for _ in range(2)]
            rb = scH.take(2048, f32=True)
            rden = scH.take(2048, f32=True)
            ctmp = scH.take(1024)
            em.fence()

            def kpair_load(j):
                em.dma("sp", ktt[j % 2][:, :],
                       ktsh.ap()[j * 128:(j + 1) * 128, :],
                       reads=["ktsh"], writes=[f"ktt{j % 2}"])

            def vpair_load(j):
                for ab in range(2):
                    em.dma("sp", vh[j % 2][ab][:, :, :],
                           vsh.ap().rearrange("(c p) (b e) -> p c b e",
                                              p=P, e=65)[:, :, 2 * j + ab, :],
                           reads=["vsh"], writes=[f"vh{j % 2}{ab}"])

            kpair_load(0)
            vpair_load(0)
            for j in range(HEADS // 2):
                par = j % 2
                if j + 1 < HEADS // 2:
                    kpair_load(j + 1)
                    vpair_load(j + 1)
                qA = qT3[:, j, :][0:64, :]
                qB = qT3[:, j, :][64:128, :]
                kk = f"ktt{par}"
                for kc in range(NHC):
                    em.mm(ps[0], pk(0),
                          [(ktt[par][0:64, kc * 128:(kc + 1) * 128], qA,
                            [kk, "qT"])], start=True, stop=False)
                    em.mm(ps[1], pk(1),
                          [(ktt[par][64:128, kc * 128:(kc + 1) * 128], qB,
                            [kk, "qT"])], start=True, stop=False)
                    em.mm(ps[0], pk(0),
                          [(ID, BT3[:, kc, :], ["consb", "BT"])],
                          start=False, stop=True)
                    em.mm(ps[1], pk(1),
                          [(ID, BT3[:, kc, :], ["consb", "BT"])],
                          start=False, stop=True)
                    em.op("act", lambda s, kc=kc: s.activation(
                        Eb[0][kc % 2][:], ps[0][:], AF.Exp),
                        reads=pk(0), writes=[f"EbA{kc % 2}"])
                    em.op("act", lambda s, kc=kc: s.activation(
                        Eb[1][kc % 2][:], ps[1][:], AF.Exp),
                        reads=pk(1), writes=[f"EbB{kc % 2}"])
                    if kc >= 1:
                        jj = (kc - 1) % 2
                        em.mm(ps[2][0:65, :], pk(2),
                              [(vh[par][0][:, kc - 1, :], Eb[0][jj][:],
                                [f"vh{par}0", f"EbA{jj}"])],
                              start=(kc == 1), stop=False)
                        em.mm(ps[3][0:65, :], pk(3),
                              [(vh[par][1][:, kc - 1, :], Eb[1][jj][:],
                                [f"vh{par}1", f"EbB{jj}"])],
                              start=(kc == 1), stop=False)
                jj = (NHC - 1) % 2
                em.mm(ps[2][0:65, :], pk(2),
                      [(vh[par][0][:, NHC - 1, :], Eb[0][jj][:],
                        [f"vh{par}0", f"EbA{jj}"])], start=False, stop=True)
                em.mm(ps[3][0:65, :], pk(3),
                      [(vh[par][1][:, NHC - 1, :], Eb[1][jj][:],
                        [f"vh{par}1", f"EbB{jj}"])], start=False, stop=True)
                for ab in range(2):
                    psc = ps[2 + ab]
                    psb = ps[ab]
                    ck = pk(2 + ab)
                    em.op("act", lambda s, psc=psc, ab=ab: s.activation(
                        rden[64:65, ab * 1024:ab * 1024 + 1024],
                        psc[64:65, :], AF.Ln),
                        reads=ck, writes=[f"rdent{ab}"])
                    em.op("act", lambda s, ab=ab: s.activation(
                        rden[64:65, ab * 1024:ab * 1024 + 1024],
                        rden[64:65, ab * 1024:ab * 1024 + 1024], AF.Exp,
                        scale=-1.0),
                        reads=[f"rdent{ab}"], writes=[f"rdent{ab}"])
                    em.mm(psb[0:64, :], pk(ab),
                          [(onesf[64:65, 0:64],
                            rden[64:65, ab * 1024:ab * 1024 + 1024],
                            [f"rdent{ab}", "onesf"])])
                    em.op("dve", lambda v, psb=psb, ab=ab: v.tensor_copy(
                        out=rb[0:64, ab * 1024:ab * 1024 + 1024],
                        in_=psb[0:64, :]),
                        reads=pk(ab), writes=[f"rb{ab}"])
                    if ab == 0:
                        em.op("dve", lambda v, j=j, psc=psc, ab=ab:
                              v.tensor_tensor(
                                  out=ctxn3[:, j, :][0:64, :],
                                  in0=psc[0:64, :],
                                  in1=rb[0:64, ab * 1024:ab * 1024 + 1024],
                                  op=ALU.mult),
                              reads=ck + (f"rb{ab}",), writes=["ctxn"])
                    else:
                        em.op("dve", lambda v, psc=psc, ab=ab:
                              v.tensor_tensor(
                                  out=ctmp[0:64, :], in0=psc[0:64, :],
                                  in1=rb[0:64, ab * 1024:ab * 1024 + 1024],
                                  op=ALU.mult),
                              reads=ck + (f"rb{ab}",), writes=["ctmp"])
                        em.dma("sp", ctxn3[:, j, :][64:128, :],
                               ctmp[0:64, :], reads=["ctmp"],
                               writes=["ctxn"])

            em.fence()
            scO = Bump(pool, SC + 12416, SCE)
            x1in = [scO.take(1024, f32=True), scO.take(1024, f32=True)]
            xo2 = [scO.take(1024, f32=True), scO.take(1024, f32=True)]
            out2 = outd if stage == "x2" else x2d
            okey = "outd" if stage == "x2" else "x2d"

            def o_consume(mc, buf, bk):
                i = mc % 2
                em.dma("sp", x1in[i][:], x1d.ap()[mc * P:(mc + 1) * P, :],
                       reads=["x1d"], writes=[f"x1in{i}"])
                em.mm(ps[i], pk(i),
                      [(buf[:, kc, :], ctxn3[:, kc, :], [bk, "ctxn"])
                       for kc in range(NHC)])
                em.op("dve", lambda v, i=i: v.tensor_tensor(
                    out=xo2[i][:], in0=ps[i][:], in1=x1in[i][:], op=ALU.add),
                    reads=pk(i) + (f"x1in{i}",), writes=[f"xo2{i}"])
                em.dma("act", out2.ap()[mc * P:(mc + 1) * P, :], xo2[i][:],
                       reads=[f"xo2{i}"], writes=[okey])

            proj_loop(32, "woT", o_consume)

        if stage == "full":
            emit_ffn_call(x2d, "x2d", outd, "outd")

        for sk in (("d", "act"), ("d", "sp"), ("d", "pool")):
            if em.cnt[sk]:
                g.wait_ge(em.sem[sk], em.cnt[sk])
        em._deps("pool", ["nonce"], [])
        with g.If_eq(r_prank, 0):
            g.dma_start(out=flg.ap()[2:3, :], in_=nonce_sb[:]).then_inc(
                spin_sem, 16)
        with g.Else():
            g.dma_start(out=flg.ap()[3:4, :], in_=nonce_sb[:]).then_inc(
                spin_sem, 16)
        g.wait_ge(spin_sem, 16)
        with (g.register("r_sm2") as r_sm, g.register("r_ne2") as r_ne,
              g.register("r_2n2") as r_2n, g.register("r_nn2") as r_nn,
              g.register("r_tg2") as r_tg):
            g.reg_load(r_nn, nonce_sb[0:1, 0:1])
            g.reg_add(r_2n, r_nn, r_nn)
            g.reg_mov(r_ne, 1)
            g.reg_mov(r_tg, 16)
            with g.While(r_ne):
                g.dma_start(out=pf2_sb[:, 0:32],
                            in_=flg.ap()[2:4, :].rearrange("a b -> (a b)")
                            .rearrange("(o ab) -> o ab", o=1)
                            ).then_inc(spin_sem, 16)
                g.reg_add(r_tg, r_tg, 16)
                g.wait_ge(spin_sem, r_tg)
                g.reg_load(r_sm, pf2_sb[0:1, 0:1])
                g.reg_load(r_ne, pf2_sb[0:1, 16:17])
                g.reg_add(r_sm, r_sm, r_ne)
                g.reg_alu(r_ne, r_sm, r_2n, ALU.not_equal)

    return nc




def _prep_inputs(inputs, nonce):
    f = np.float32
    rel = np.asarray(inputs["L0_attn_rel_pos"], dtype=f)
    base = {
        "wqT": np.ascontiguousarray(np.asarray(inputs["L0_attn_wq"]).T, dtype=f),
        "wkT": np.ascontiguousarray(np.asarray(inputs["L0_attn_wk"]).T, dtype=f),
        "wvT": np.ascontiguousarray(np.asarray(inputs["L0_attn_wv"]).T, dtype=f),
        "woT": np.ascontiguousarray(np.asarray(inputs["L0_attn_wo"]).T, dtype=f),
        "wup": np.ascontiguousarray(inputs["L0_ffn_Wup"], dtype=f),
        "wsw": np.ascontiguousarray(inputs["L0_ffn_Wsw"], dtype=f),
        "wdn": np.ascontiguousarray(inputs["L0_ffn_Wdn"], dtype=f),
        "wsu": np.ascontiguousarray(inputs["L0_ffn_Wsu"], dtype=f),
        "wsa": np.ascontiguousarray(inputs["L0_ffn_Wsa"], dtype=f),
        "wsd": np.ascontiguousarray(inputs["L0_ffn_Wsd"], dtype=f),
        "wg": np.ascontiguousarray(
            np.asarray(inputs["L0_ffn_Wg"], dtype=f).reshape(2, 16, 128, 8)
            .transpose(2, 0, 1, 3).reshape(128, 256)),
        "bupr": np.ascontiguousarray(
            np.asarray(inputs["L0_ffn_bup"], dtype=f).reshape(6, 8, 128)
            .transpose(2, 0, 1).reshape(128, 48)),
        "bswr": np.ascontiguousarray(
            np.asarray(inputs["L0_ffn_bsw"], dtype=f).reshape(6, 8, 128)
            .transpose(2, 0, 1).reshape(128, 48)),
        "bsur": np.ascontiguousarray(
            np.asarray(inputs["L0_ffn_bsu"], dtype=f).reshape(16, 128).T),
        "bsar": np.ascontiguousarray(
            np.asarray(inputs["L0_ffn_bsa"], dtype=f).reshape(8, 128).T),
        "bsdr": np.ascontiguousarray(
            np.asarray(inputs["L0_ffn_bsd"], dtype=f).reshape(16, 128).T),
        "bgr": np.ascontiguousarray(np.asarray(inputs["L0_ffn_bg"], dtype=f).T),
        "bdnr": np.ascontiguousarray(inputs["L0_ffn_bdn"], dtype=f),
        "relrev": np.ascontiguousarray(
            np.concatenate([rel[::-1], np.zeros((1, 64), f)], axis=0)),
        "consts": np.ascontiguousarray(np.concatenate(
            [np.eye(128, dtype=f), np.eye(128, dtype=f)[:, ::-1],
             np.ones((128, 1), f)], axis=1)),
        "ind6h": np.ascontiguousarray(np.concatenate(
            [np.tile(np.eye(8, dtype=f)[n][:, None], (1, 128))
             for n in range(6)], axis=1)),
        "onesf": np.ones((128, 128), f),
        "noncein": np.full((1, 16), nonce, np.int32),
    }
    x = np.asarray(inputs["L0_attn_hidden_states"], dtype=f)
    maps = []
    for c in range(8):
        b, half = c // 2, c % 2
        m = dict(base)
        m["xT"] = np.ascontiguousarray(x[b, half * TC:(half + 1) * TC, :].T)
        maps.append(m)
    return maps


def kernel(stage="full", **inputs):
    nonce = (int.from_bytes(os.urandom(4), "little") | 1) & 0x7FFFFFFF
    maps = _prep_inputs(inputs, nonce)
    nc = build(stage=stage)
    res = run_bass_kernel_spmd(nc, maps, core_ids=list(range(8)))
    out = np.empty((4, 2048, 2048), np.float32)
    for c in range(8):
        b, half = c // 2, c % 2
        out[b, half * TC:(half + 1) * TC, :] = np.asarray(
            res.results[c]["out"]).T
    return out
